# version 09
# baseline (speedup 1.0000x reference)
import numpy as np
import concourse.bass as bass
import concourse.mybir as mybir
from concourse.bass_utils import run_bass_kernel_spmd

F32 = mybir.dt.float32
BF16 = mybir.dt.bfloat16
AF = mybir.ActivationFunctionType
ALU = mybir.AluOpType

B, T, D = 8, 2048, 1024
H, HD = 16, 64
C = 64
NCH = T // C
NPAIR = 8
EPS_LN = 1e-5
EPS_L2 = 1e-6


class Sched:

    ENGINES = ("tensor", "vector", "scalar", "gpsimd", "sync")

    def __init__(self, nc):
        self.nc = nc
        self.ops = {e: [] for e in self.ENGINES}
        self.sems = {}
        self.count = {}

    def sem(self, name):
        if name not in self.sems:
            self.sems[name] = self.nc.semaphore(name).__enter__()
            self.count[name] = 0
        return self.sems[name]

    def add(self, engine, fn, waits=(), inc=None, dinc=0):
        for w in waits:
            assert w is not None, (engine, "None wait")
            s, cnt = w
            assert cnt <= self.count[s], (
                "wait on future token", engine, s, cnt, self.count[s])
        tok = None
        if inc is not None:
            self.sem(inc)
            self.count[inc] += (dinc if dinc else 1)
            tok = (inc, self.count[inc])
        self.ops[engine].append((fn, tuple(waits), inc, dinc))
        return tok

    def now(self, sem):
        return (sem, self.count[sem])

    def emit(self, name):
        nc = self.nc
        with nc.Block(name) as blk:
            for engine in self.ENGINES:
                if not self.ops[engine]:
                    continue

                def make(engine):
                    oplist = self.ops[engine]

                    def body(eng):
                        waited = {}
                        for fn, waits, inc, dinc in oplist:
                            for (s, cnt) in waits:
                                if waited.get(s, -1) >= cnt:
                                    continue
                                eng.wait_ge(self.sems[s], cnt)
                                waited[s] = cnt
                            last = fn(eng)
                            if inc is not None and dinc == 0:
                                assert last is not None, (engine, inc)
                                last.then_inc(self.sems[inc], 1)
                    return body

                getattr(blk, engine)(make(engine))
        self.ops = {e: [] for e in self.ENGINES}


def _host_layout(inputs):
    f = np.float32
    x = np.asarray(inputs["L0_attn_x"], f)
    w_in = np.ascontiguousarray(np.asarray(inputs["L0_attn_W_in"], f))
    w_ba = np.ascontiguousarray(np.asarray(inputs["L0_attn_W_ba"], f))
    w_o = np.ascontiguousarray(np.asarray(inputs["L0_attn_W_o"], f))
    wrT = np.ascontiguousarray(np.asarray(inputs["L0_ffn_W_r"], f).T)
    wkT = np.ascontiguousarray(np.asarray(inputs["L0_ffn_W_k"], f).T)
    wvT = np.ascontiguousarray(np.asarray(inputs["L0_ffn_W_v"], f).T)
    mu_r = np.ascontiguousarray(
        np.asarray(inputs["L0_ffn_mu_r"], f).reshape(8, 128).T)
    mu_k = np.ascontiguousarray(
        np.asarray(inputs["L0_ffn_mu_k"], f).reshape(8, 128).T)
    conv = np.asarray(inputs["L0_attn_conv_w"], f)[:, 0, :]
    conv = np.ascontiguousarray(
        conv.reshape(24, 128, 4).transpose(1, 0, 2).reshape(128, 96))
    dtb = np.asarray(inputs["L0_attn_dt_bias"], f).reshape(1, 16)
    alog = np.asarray(inputs["L0_attn_A_log"], f).reshape(1, 16)
    nw = np.ascontiguousarray(
        np.tile(np.asarray(inputs["L0_attn_norm_w"], f), 2).reshape(128, 1))

    iden = np.eye(128, dtype=f)
    cum = np.zeros((128, 128), f)
    for blk in range(2):
        for s in range(64):
            for t in range(s, 64):
                cum[blk * 64 + s, blk * 64 + t] = 1.0
    rcum = np.zeros((128, 128), f)
    for blk in range(2):
        for s in range(64):
            for t in range(s):
                rcum[blk * 64 + s, blk * 64 + t] = 1.0
    selA = np.zeros((128, 128), f)
    selB = np.zeros((128, 128), f)
    for s in range(64):
        for h in (0, 64):
            selA[h + s, s] = 1.0
            selB[h + s, 64 + s] = 1.0
    selj = np.zeros((16, 8 * 128), f)
    for j in range(8):
        selj[2 * j, j * 128:j * 128 + 64] = 1.0
        selj[2 * j + 1, j * 128 + 64:(j + 1) * 128] = 1.0
    seg2 = np.zeros((128, 2), f)
    seg2[0:64, 0] = 1.0
    seg2[64:128, 1] = 1.0
    sel2 = np.zeros((2, 128), f)
    sel2[0, 0:64] = 1.0
    sel2[1, 64:128] = 1.0
    m_incl = np.zeros((128, 64), f)
    m_strict = np.zeros((128, 64), f)
    for p in range(128):
        s = p % 64
        for q in range(64):
            if s <= q:
                m_incl[p, q] = 1.0
            if s < q:
                m_strict[p, q] = 1.0
    mT_strict = np.zeros((128, 64), f)
    for p in range(128):
        q = p % 64
        for s in range(64):
            if s < q:
                mT_strict[p, s] = 1.0
    mb_incl = np.where(m_incl > 0, 0.0, -1e30).astype(f)
    mbT_strict = np.where(mT_strict > 0, 0.0, 1e30).astype(f)
    bdi = np.zeros((128, 64), f)
    for p in range(128):
        bdi[p, p % 64] = 1.0
    ones_r = np.ones((1, 128), f)
    ones_c = np.ones((128, 1), f)
    ones_all = np.ones((128, 128), f)

    shared = dict(
        w_in=w_in, w_ba=w_ba, w_o=w_o, wrT=wrT, wkT=wkT, wvT=wvT,
        mu_r=mu_r, mu_k=mu_k, conv=conv, dtb=dtb, alog=alog, nw=nw,
        iden=iden, cum=cum, rcum=rcum, selA=selA, selB=selB,
        selj=selj, seg2=seg2, sel2=sel2,
        m_incl=m_incl, m_strict=m_strict, mT_strict=mT_strict, bdi=bdi,
        mb_incl=mb_incl, mbT_strict=mbT_strict,
        ones_r=ones_r, ones_c=ones_c, ones_all=ones_all,
    )
    maps = []
    for bi in range(B):
        m = dict(shared)
        m["x"] = np.ascontiguousarray(x[bi].T)
        maps.append(m)
    return maps


def build(dbg=None, parts=(1, 2, 3)):
    dbg = dbg or {}
    nc = bass.Bass("TRN2", target_bir_lowering=False)

    ap = {}
    def din(name, shape):
        ap[name] = nc.dram_tensor(name, list(shape), F32,
                                  kind="ExternalInput").ap()
    din("x", (D, T))
    din("w_in", (D, 4 * D)); din("w_ba", (D, 2 * H)); din("w_o", (D, D))
    din("wrT", (D, D)); din("wkT", (D, D)); din("wvT", (D, D))
    din("mu_r", (128, 8)); din("mu_k", (128, 8))
    din("conv", (128, 96)); din("dtb", (1, 16)); din("alog", (1, 16))
    din("nw", (128, 1))
    din("iden", (128, 128)); din("cum", (128, 128)); din("rcum", (128, 128))
    din("selA", (128, 128)); din("selB", (128, 128))
    din("selj", (16, 1024)); din("seg2", (128, 2)); din("sel2", (2, 128))
    din("m_incl", (128, 64)); din("m_strict", (128, 64))
    din("mT_strict", (128, 64)); din("bdi", (128, 64))
    din("mb_incl", (128, 64)); din("mbT_strict", (128, 64))
    din("ones_r", (1, 128)); din("ones_c", (128, 1))
    din("ones_all", (128, 128))
    out = nc.dram_tensor("out", [D, T], F32, kind="ExternalOutput").ap()
    dbg_out = {k: nc.dram_tensor("dbg_" + k, list(v), F32,
                                 kind="ExternalOutput").ap()
               for k, v in dbg.items()}

    ctx = []
    def sb(name, shape, dtype):
        t = nc.sbuf_tensor(name, list(shape), dtype)
        ctx.append(t)
        return t.__enter__()
    def psum(name, shape, dtype=F32):
        t = nc.psum_tensor(name, list(shape), dtype)
        ctx.append(t)
        return t.__enter__()

    x_sb = sb("x_sb", [128, 8 * T], F32)
    wbuf = [sb(f"wbuf{i}", [128, 8 * 1024], BF16) for i in range(2)]
    st_a = sb("st_a", [128, 8 * 1024], BF16)
    st_b = sb("st_b", [128, 8 * 1024], BF16)
    st_c = sb("st_c", [128, 8 * 1024], BF16)
    st_d = sb("st_d", [128, 8 * 1024], BF16)
    st_f = sb("st_f", [128, 2 * 1024], BF16)
    f_a = sb("f_a", [128, T], F32)
    f_b = sb("f_b", [128, T], F32)
    iv_bf = sb("iv_bf", [128, T], BF16)
    miv_bf = sb("miv_bf", [128, T], BF16)
    xh_sb = st_d
    bcol = sb("bcol", [128, 8], BF16)
    mur_sb = sb("mur_sb", [128, 8], F32)
    muk_sb = sb("muk_sb", [128, 8], F32)
    conv_sb = sb("conv_sb", [128, 96], F32)
    nw_sb = sb("nw_sb", [128, 1], F32)
    iden_bf = sb("iden_bf", [128, 128], BF16)
    iden_f = sb("iden_f", [128, 128], F32)
    cum_sb = sb("cum_sb", [128, 128], F32)
    rcum_sb = sb("rcum_sb", [128, 128], F32)
    selA_sb = sb("selA_sb", [128, 128], F32)
    selB_sb = sb("selB_sb", [128, 128], F32)
    selj_bf = sb("selj_bf", [16, 1024], BF16)
    selj_f = sb("selj_f", [16, 1024], F32)
    seg2_bf = sb("seg2_bf", [128, 2], BF16)
    sel2_bf = sb("sel2_bf", [2, 128], BF16)
    m_strict_bf = sb("m_strict_bf", [128, 64], BF16)
    mb_incl_f = sb("mb_incl_f", [128, 64], BF16)
    mbT_strict_f = sb("mbT_strict_f", [128, 64], BF16)
    bdi_bf = sb("bdi_bf", [128, 64], BF16)
    ones_r_bf = sb("ones_r_bf", [1, 128], BF16)
    ones_r_f = sb("ones_r_f", [1, 128], F32)
    ones_c_bf = sb("ones_c_bf", [128, 1], BF16)
    ones_all_bf = sb("ones_all_bf", [128, 128], BF16)
    dtb_sb = sb("dtb_sb", [1, 16], F32)
    alog_sb = sb("alog_sb", [1, 16], F32)
    cpool = sb("cpool", [128, 4], F32)
    wba_sb = sb("wba_sb", [128, 8 * 32], BF16)
    gfm_sb = sb("gfm_sb", [16, T], F32)


    PSA = psum("psa", [128, T], F32)
    PSB = psum("psb", [128, T], F32)

    sch = Sched(nc)
    for s in ("dg", "dx", "t", "v", "s", "g", "st"):
        sch.sem(s)

    def dma(engine, dst, src, sem="dg", waits=()):
        def fn(eng):
            return eng.dma_start(dst, src).then_inc(sch.sems[sem], 16)
        sch.count[sem] += 16
        sch.ops[engine].append((fn, tuple(waits), None, 0))
        return (sem, sch.count[sem])

    tok_x = dma("sync", x_sb[:, 0:8 * T].rearrange("p (c t) -> p c t", c=8),
                ap["x"].rearrange("(c p) t -> p c t", p=128), sem="dx")
    for dst, src in [
        (mur_sb[:], ap["mu_r"]), (muk_sb[:], ap["mu_k"]),
        (conv_sb[:], ap["conv"]), (nw_sb[:], ap["nw"]),
        (iden_bf[:], ap["iden"]), (iden_f[:], ap["iden"]),
        (cum_sb[:], ap["cum"]), (rcum_sb[:], ap["rcum"]),
        (selA_sb[:], ap["selA"]), (selB_sb[:], ap["selB"]),
        (selj_bf[:], ap["selj"]), (selj_f[:], ap["selj"]),
        (seg2_bf[:], ap["seg2"]), (sel2_bf[:], ap["sel2"]),
        (wba_sb[:, 0:256].rearrange("p (e c) -> p e c", e=8),
         ap["w_ba"].rearrange("(e p) c -> p e c", p=128)),
        (m_strict_bf[:], ap["m_strict"]),
        (bdi_bf[:], ap["bdi"]),
        (mb_incl_f[:], ap["mb_incl"]), (mbT_strict_f[:], ap["mbT_strict"]),
        (ones_r_bf[:], ap["ones_r"]), (ones_r_f[:], ap["ones_r"]),
        (ones_c_bf[:], ap["ones_c"]), (ones_all_bf[:], ap["ones_all"]),
        (dtb_sb[:], ap["dtb"]), (alog_sb[:], ap["alog"]),
    ]:
        tok_c = dma("gpsimd", dst, src)
    def memsets(e):
        e.memset(cpool[:, 0:1], EPS_LN)
        e.memset(cpool[:, 1:2], EPS_L2)
        e.memset(cpool[:, 2:3], 1.0)
        return e.memset(cpool[:, 3:4], float(np.log(0.125)))
    sch.add("gpsimd", memsets, inc="g")
    sch.emit("p0_load")

    wbuf_tok = {0: None, 1: None}

    def load_w(which, key, cols):
        dst = wbuf[which][:, 0:8 * cols].rearrange("p (e c) -> p e c", e=8)
        src = ap[key].rearrange("(e p) c -> p e c", p=128)
        waits = [wbuf_tok[which]] if wbuf_tok[which] is not None else []
        return dma("gpsimd", dst, src, waits=waits)

    def wview(which, cols):
        return lambda e: wbuf[which][:, e * cols:(e + 1) * cols]

    def emit_ln(src_cols, xhat_to_half):
        for ci in range(8):
            xc = x_sb[:, ci * T:(ci + 1) * T]
            xb = st_f[:, 0:T]
            sq = st_c[:, 0:T]
            tv = sch.add("vector", lambda e, xc=xc, xb=xb: e.tensor_copy(xb, xc),
                         waits=[sch.now("t")] if ci else [tok_x], inc="v")
            ts = sch.add("scalar", lambda e, xb=xb, sq=sq:
                         e.activation(sq, xb, AF.Square),
                         waits=[tv], inc="s")
            def mm_stats(e, ci=ci, xb=xb, sq=sq):
                last = None
                for w in range(4):
                    cols = slice(w * 512, (w + 1) * 512)
                    last = e.matmul(PSA[:, cols], ones_all_bf[:], xb[:, cols],
                                    start=(ci == 0), stop=(ci == 7),
                                    skip_group_check=True)
                    last = e.matmul(PSB[:, cols], ones_all_bf[:],
                                    sq[:, cols], start=(ci == 0),
                                    stop=(ci == 7), skip_group_check=True)
                return last
            tt = sch.add("tensor", mm_stats, waits=[ts, tv], inc="t")
        tv = sch.add("vector", lambda e: e.tensor_scalar_mul(
            f_a[:], PSA[:, 0:T], 1.0 / D), waits=[tt], inc="v")
        tv = sch.add("vector", lambda e: e.tensor_mul(f_b[:], f_a[:], f_a[:]),
                     waits=[tv], inc="v")
        tv = sch.add("vector", lambda e: e.scalar_tensor_tensor(
            f_b[:], PSB[:, 0:T], 1.0 / D, f_b[:], ALU.mult, ALU.subtract),
            waits=[tv], inc="v")
        ts = sch.add("scalar", lambda e: e.activation(
            f_b[:], f_b[:], AF.Ln, bias=cpool[:, 0:1]), waits=[tv], inc="s")
        ts = sch.add("scalar", lambda e: e.activation(
            f_b[:], f_b[:], AF.Exp, scale=-0.5), waits=[ts], inc="s")
        tv = sch.add("vector", lambda e: e.tensor_copy(iv_bf[:], f_b[:]),
                     waits=[ts], inc="v")
        tv = sch.add("vector", lambda e: e.tensor_mul(miv_bf[:], f_a[:],
                                                      f_b[:]),
                     waits=[tv], inc="v")
        return tv

    def emit_ffn(scale, tag):
        tok_ln = emit_ln(None, None)
        for hv in range(2):
            hc = slice(hv * 1024, hv * 1024 + 1024)

            tprev = tok_ln
            for ci in range(8):
                xc = x_sb[:, ci * T + hv * 1024: ci * T + hv * 1024 + 1024]
                xo = xh_sb[:, ci * 1024:(ci + 1) * 1024]
                t1 = f_a[:, 0:1024]
                tv = sch.add("vector", lambda e, xc=xc, t1=t1:
                             e.tensor_mul(t1, xc, iv_bf[:, hc]),
                             waits=[tprev], inc="v")
                tprev = sch.add("vector", lambda e, t1=t1, xo=xo:
                                e.tensor_sub(xo, t1, miv_bf[:, hc]),
                                waits=[tv], inc="v")
            tok_xh = tprev

            dcol = f_b[:, 0:1024]
            for ci in range(8):
                xo = xh_sb[:, ci * 1024:(ci + 1) * 1024]
                xr = st_a[:, ci * 1024:(ci + 1) * 1024]
                xk = st_b[:, ci * 1024:(ci + 1) * 1024]
                mr = mur_sb[:, ci:ci + 1]
                mk = muk_sb[:, ci:ci + 1]
                bc = bcol[:, ci:ci + 1]
                tv = sch.add("vector", lambda e, xo=xo, d=dcol:
                             e.tensor_sub(d[:, 1:1024], xo[:, 1:1024],
                                          xo[:, 0:1023]),
                             waits=[tok_xh], inc="v")
                if hv == 0:
                    tv = sch.add("vector", lambda e, xo=xo, d=dcol:
                                 e.tensor_copy(d[:, 0:1], xo[:, 0:1]),
                                 waits=[tv], inc="v")
                else:
                    tv = sch.add("vector", lambda e, xo=xo, d=dcol, bc=bc:
                                 e.tensor_sub(d[:, 0:1], xo[:, 0:1], bc),
                                 waits=[tv], inc="v")
                tv = sch.add("vector", lambda e, d=dcol, xo=xo, xr=xr, mr=mr:
                             e.scalar_tensor_tensor(
                                 xr[:, 1:1024], d[:, 1:1024], mr,
                                 xo[:, 0:1023], ALU.mult, ALU.add),
                             waits=[tv], inc="v")
                if hv == 0:
                    tv = sch.add("vector", lambda e, d=dcol, xr=xr, mr=mr:
                                 e.tensor_scalar_mul(xr[:, 0:1], d[:, 0:1], mr),
                                 waits=[tv], inc="v")
                else:
                    tv = sch.add("vector", lambda e, d=dcol, xr=xr, mr=mr, bc=bc:
                                 e.scalar_tensor_tensor(
                                     xr[:, 0:1], d[:, 0:1], mr, bc,
                                     ALU.mult, ALU.add),
                                 waits=[tv], inc="v")
                tv = sch.add("vector", lambda e, d=dcol, xo=xo, xk=xk, mk=mk:
                             e.scalar_tensor_tensor(
                                 xk[:, 1:1024], d[:, 1:1024], mk,
                                 xo[:, 0:1023], ALU.mult, ALU.add),
                             waits=[tv], inc="v")
                if hv == 0:
                    tv = sch.add("vector", lambda e, d=dcol, xk=xk, mk=mk:
                                 e.tensor_scalar_mul(xk[:, 0:1], d[:, 0:1], mk),
                                 waits=[tv], inc="v")
                else:
                    tv = sch.add("vector", lambda e, d=dcol, xk=xk, mk=mk, bc=bc:
                                 e.scalar_tensor_tensor(
                                     xk[:, 0:1], d[:, 0:1], mk, bc,
                                     ALU.mult, ALU.add),
                                 waits=[tv], inc="v")
                if hv == 0:
                    tv = sch.add("vector", lambda e, xo=xo, bc=bc:
                                 e.tensor_copy(bc, xo[:, 1023:1024]),
                                 waits=[tv], inc="v")
            tok_mix = tv

            tok_w = load_w(0, "wkT", 1024)
            wk = wview(0, 1024)
            tok_g = _gemm(sch, PSA, PSB, wk, st_b, 8, tok_w, tok_mix,
                          consumer=_mk_relu2(sch, st_c))
            wbuf_tok[0] = tok_g
            tok_w = load_w(1, "wrT", 1024)
            wr = wview(1, 1024)
            tok_g2 = _gemm(sch, PSA, PSB, wr, st_a, 8, tok_w, tok_mix,
                           consumer=_mk_sigmoid(sch, st_d))
            wbuf_tok[1] = tok_g2
            tok_w = load_w(0, "wvT", 1024)
            wv = wview(0, 1024)

            def mk_gate(hv=hv, scale=scale):
                def consume(dci, ps, tok_mm):
                    g1 = f_a[:, 0:1024]
                    xc = x_sb[:, dci * T + hv * 1024: dci * T + hv * 1024 + 1024]
                    r = st_d[:, dci * 1024:(dci + 1) * 1024]
                    tv = sch.add("vector", lambda e, ps=ps, g1=g1, r=r:
                                 e.tensor_mul(g1, ps, r),
                                 waits=[tok_mm], inc="v")
                    tv = sch.add("vector", lambda e, g1=g1, xc=xc:
                                 e.scalar_tensor_tensor(xc, g1, scale, xc,
                                                        ALU.mult, ALU.add),
                                 waits=[tv], inc="v")
                    return tv
                return consume
            tok_g3 = _gemm(sch, PSA, PSB, wv, st_c, 8, tok_w, tok_g2,
                           consumer=mk_gate())
            wbuf_tok[0] = tok_g3
        return tok_g3

    def _mk_sigmoid(sch, dst):
        def consume(dci, ps, tok_mm):
            o = dst[:, dci * 1024:(dci + 1) * 1024]
            return sch.add("scalar", lambda e, ps=ps, o=o:
                           e.activation(o, ps, AF.Sigmoid),
                           waits=[tok_mm], inc="s")
        return consume

    def _mk_relu2(sch, dst):
        vtok = [None, None]

        def consume(dci, ps, tok_mm):
            o = dst[:, dci * 1024:(dci + 1) * 1024]
            t1 = f_b[:, (dci % 2) * 1024:(dci % 2) * 1024 + 1024]
            waits = [tok_mm]
            if vtok[dci % 2] is not None:
                waits.append(vtok[dci % 2])
            ts = sch.add("scalar", lambda e, ps=ps, t1=t1:
                         e.activation(t1, ps, AF.Relu),
                         waits=waits, inc="s")
            tv = sch.add("vector", lambda e, t1=t1, o=o:
                         e.tensor_mul(o, t1, t1), waits=[ts], inc="v")
            vtok[dci % 2] = tv
            return tv
        return consume

    gemm_prev = [None, None]

    def _gemm(sch, PS0, PS1, wv, rhs, nd, tok_w, tok_rhs, consumer):
        tok_c = None
        for dci in range(nd):
            ps = (PS0 if dci % 2 == 0 else PS1)[:, 0:1024]
            waits = [tok_w, tok_rhs]
            if gemm_prev[dci % 2] is not None:
                waits.append(gemm_prev[dci % 2])

            def mm(e, dci=dci, ps=ps):
                last = None
                for ec in range(8):
                    for w in range(2):
                        cols = slice(w * 512, (w + 1) * 512)
                        last = e.matmul(
                            ps[:, cols], wv(ec)[:, dci * 128:(dci + 1) * 128],
                            rhs[:, ec * 1024 + w * 512:
                                ec * 1024 + w * 512 + 512],
                            start=(ec == 0), stop=(ec == 7))
                return last
            tok_mm = sch.add("tensor", mm, waits=waits, inc="t")
            tok_c = consumer(dci, ps, tok_mm)
            gemm_prev[dci % 2] = tok_c
        return tok_c


    def bar():
        return [sch.now("t"), sch.now("v"), sch.now("s"), sch.now("dg")]

    def emit_attn():
        k_st = st_c[:, 0:2048]
        q_st = st_c[:, 2048:4096]
        v_st = st_c[:, 4096:6144]
        z_st = st_c[:, 6144:8192]
        wo_buf = wbuf[1][:, 5120:6144]
        att_st = st_d[:, 2048:4096]
        qg_st = wbuf[0][:, 1024:3072]
        uv_st = wbuf[0][:, 3072:5120]
        wt_st = wbuf[0][:, 5120:7168]
        h_st = wbuf[1][:, 1024:3072]
        pl_st = wbuf[1][:, 3072:5120]
        conv_acc = iv_bf[:, 0:2048]
        conv_pre = st_d[:, 6144:8192]
        x1g = iv_bf[:, 0:256]
        m1g = iv_bf[:, 256:512]
        xp_g = [iv_bf[:, 512 + 256 * i:768 + 256 * i] for i in range(5)]
        mp_g = [iv_bf[:, 1792:2048], miv_bf[:, 0:256]]
        ktm_g = miv_bf[:, 256:512]
        vtm_g = miv_bf[:, 512:768]
        kh_g = miv_bf[:, 768:1024]
        zw_g = [miv_bf[:, 1024:1536], miv_bf[:, 1536:2048]]
        dmp_g = st_d[:, 4096:4352]
        dmm_g = st_d[:, 4352:4608]
        grep_c = st_d[:, 4608:4672]
        wfm_scr = st_d[:, 4672:4736]
        utm_scr = st_d[:, 4736:4800]
        sq_scr = iv_bf[:, 0:2048]
        invn_v = st_d[0:2, 6144:8192]
        bfm = st_f[0:16, 0:2048]
        ba_f = f_a[0:32, 0:2048]
        bt_f = f_b[:, 0:512]
        btm_f = f_b[:, 512:768]
        gsc_f = f_b[:, 768:1024]
        gtm_f = f_b[:, 1024:1280]
        rtm_f = f_b[:, 1280:1536]
        dtbr_f = f_b[:, 1536:1552]
        nega_f = f_b[:, 1552:1568]
        dup_sb = f_b[:, 0:768]
        dup2_sb = f_b[:, 768:1792]
        smast = f_b[:, 1792:1856]
        sbf3 = [miv_bf[:, 1024:1088], miv_bf[:, 1088:1152],
                miv_bf[:, 1152:1216]]
        utm2 = [miv_bf[:, 1216:1280], miv_bf[:, 1280:1344]]

        tok_ln = emit_ln(None, None)
        def x2h(ec):
            return (st_a if ec < 4 else st_b)[:, (ec % 4) * 2048:
                                              (ec % 4) * 2048 + 2048]
        tprev = tok_ln
        for ci in range(8):
            xc = x_sb[:, ci * T:(ci + 1) * T]
            xo = x2h(ci)
            tv = sch.add("vector", lambda e, xc=xc: e.tensor_mul(
                f_a[:, 0:T], xc, iv_bf[:, 0:T]), waits=[tprev], inc="v")
            tprev = sch.add("vector", lambda e, xo=xo: e.tensor_sub(
                xo, f_a[:, 0:T], miv_bf[:, 0:T]), waits=[tv], inc="v")
        tok_x2h = tprev

        def mm_ba(e):
            last = None
            for ec in range(8):
                for w in range(4):
                    cols = slice(w * 512, (w + 1) * 512)
                    last = e.matmul(PSA[0:32, cols],
                                    wba_sb[:, ec * 32:(ec + 1) * 32],
                                    x2h(ec)[:, cols] if False else
                                    x2h(ec)[:, cols],
                                    start=(ec == 0), stop=(ec == 7),
                                    skip_group_check=True)
            return last
        tt = sch.add("tensor", mm_ba, waits=[tok_x2h], inc="t")
        tv = sch.add("vector", lambda e: e.tensor_copy(ba_f, PSA[0:32, 0:2048]),
                     waits=[tt], inc="v")
        def mm_bt(e):
            last = None
            for tau in range(16):
                last = e.transpose(PSB[0:128, 32 * tau:32 * tau + 32],
                                   ba_f[0:32, 128 * tau:128 * tau + 128],
                                   iden_f[0:32, 0:32])
            return last
        tt = sch.add("tensor", mm_bt, waits=[tv], inc="t")
        tv = sch.add("vector", lambda e: e.tensor_copy(bt_f, PSB[:, 0:512]),
                     waits=[tt], inc="v")
        def mm_reps(e):
            e.matmul(PSB[:, 512:528], ones_r_f[:], dtb_sb[:],
                     start=True, stop=True)
            return e.matmul(PSB[:, 528:544], ones_r_f[:], alog_sb[:],
                            start=True, stop=True)
        tt = sch.add("tensor", mm_reps, waits=[tv], inc="t")
        tv = sch.add("vector", lambda e: e.tensor_copy(dtbr_f, PSB[:, 512:528]),
                     waits=[tt], inc="v")
        tv = sch.add("vector", lambda e: e.tensor_copy(nega_f, PSB[:, 528:544]),
                     waits=[tv], inc="v")
        tv = sch.add("vector", lambda e: e.tensor_scalar_max(
            nega_f, nega_f, -2.3), waits=[tv], inc="v")
        ts = sch.add("scalar", lambda e: e.activation(nega_f, nega_f, AF.Exp),
                     waits=[tv], inc="s")
        tv = sch.add("vector", lambda e: e.tensor_scalar_mul(
            nega_f, nega_f, -1.0), waits=[ts], inc="v")
        tok_na = tv
        bt3 = bass.AP(f_b, 0, [[2048, 128], [32, 16], [1, 16]])
        btm3 = bass.AP(f_b, 512, [[2048, 128], [16, 16], [1, 16]])
        at3 = bass.AP(f_b, 16, [[2048, 128], [32, 16], [1, 16]])
        gs3 = bass.AP(f_b, 768, [[2048, 128], [16, 16], [1, 16]])
        ts = sch.add("scalar", lambda e: e.activation(btm3, bt3, AF.Sigmoid),
                     waits=[sch.now("v")], inc="s")
        tok_btm = ts
        dtbr_b = bass.AP(f_b, 1536, [[2048, 128], [0, 16], [1, 16]])
        nega_b = bass.AP(f_b, 1552, [[2048, 128], [0, 16], [1, 16]])
        tv = sch.add("vector", lambda e: e.tensor_tensor(
            gs3, at3, dtbr_b, ALU.add), waits=[tok_na], inc="v")
        ts = sch.add("scalar", lambda e: e.activation(gs3, gs3, AF.Exp),
                     waits=[tv], inc="s")
        ts = sch.add("scalar", lambda e: e.activation(
            gs3, gs3, AF.Ln, bias=cpool[:, 2:3]), waits=[ts], inc="s")
        tv = sch.add("vector", lambda e: e.tensor_tensor(
            gs3, gs3, nega_b, ALU.mult), waits=[ts], inc="v")
        tok_g = tv
        def mm_cum(e):
            last = None
            for tau in range(16):
                last = e.matmul(PSB[:, 1024 + 16 * tau:1040 + 16 * tau],
                                cum_sb[:], gsc_f[:, 16 * tau:16 * tau + 16],
                                start=True, stop=True)
            for tau in range(16):
                last = e.matmul(PSB[:, 1280 + 16 * tau:1296 + 16 * tau],
                                rcum_sb[:], gsc_f[:, 16 * tau:16 * tau + 16],
                                start=True, stop=True)
            return last
        tt = sch.add("tensor", mm_cum, waits=[tok_g], inc="t")
        tv = sch.add("vector", lambda e: e.tensor_copy(gtm_f, PSB[:, 1024:1280]),
                     waits=[tt], inc="v")
        tv = sch.add("vector", lambda e: e.tensor_copy(rtm_f, PSB[:, 1280:1536]),
                     waits=[tv], inc="v")
        tok_cum = tv
        def mm_gfm(e):
            last = None
            for tau in range(16):
                last = e.transpose(PSB[0:16, 128 * tau:128 * tau + 128],
                                   gtm_f[:, 16 * tau:16 * tau + 16],
                                   iden_f[:])
            return last
        tt = sch.add("tensor", mm_gfm, waits=[tok_cum], inc="t")
        tv = sch.add("vector", lambda e: e.tensor_copy(
            gfm_sb[:], PSB[0:16, 0:2048]), waits=[tt], inc="v")
        def mm_bfm(e):
            last = None
            for tau in range(16):
                last = e.transpose(PSB[0:16, 128 * tau:128 * tau + 128],
                                   btm_f[:, 16 * tau:16 * tau + 16],
                                   iden_f[:])
            return last
        tt = sch.add("tensor", mm_bfm, waits=[tv], inc="t")
        tv = sch.add("vector", lambda e: e.tensor_copy(
            bfm[:], PSB[0:16, 0:2048]), waits=[tt], inc="v")
        tok_fm = tv
        def mm_dup(e):
            last = None
            for n in range(NCH):
                tau, half = n // 2, (n % 2) * 64
                rows = slice(half, half + 64)
                gsl = bass.AP(f_b, 1024 + 16 * tau + half * 2048,
                              [[2048, 64], [2, 8]])
                bsl = bass.AP(f_b, 512 + 16 * tau + half * 2048,
                              [[2048, 64], [2, 8]])
                rsl = bass.AP(f_b, 1280 + 16 * tau + half * 2048,
                              [[2048, 64], [2, 8]])
                gsl1 = bass.AP(f_b, 1024 + 16 * tau + half * 2048 + 1,
                               [[2048, 64], [2, 8]])
                bsl1 = bass.AP(f_b, 512 + 16 * tau + half * 2048 + 1,
                               [[2048, 64], [2, 8]])
                rsl1 = bass.AP(f_b, 1280 + 16 * tau + half * 2048 + 1,
                               [[2048, 64], [2, 8]])
                pc = 1280 + 24 * n
                sa = selA_sb[half:half + 64, :]
                sb_ = selB_sb[half:half + 64, :]
                e.matmul(PSB[:, pc:pc + 8], sa, gsl,
                         start=True, stop=False, skip_group_check=True)
                e.matmul(PSB[:, pc:pc + 8], sb_, gsl1,
                         start=False, stop=True, skip_group_check=True)
                e.matmul(PSB[:, pc + 8:pc + 16], sa, bsl,
                         start=True, stop=False, skip_group_check=True)
                e.matmul(PSB[:, pc + 8:pc + 16], sb_, bsl1,
                         start=False, stop=True, skip_group_check=True)
                e.matmul(PSB[:, pc + 16:pc + 24], sa, rsl,
                         start=True, stop=False, skip_group_check=True)
                last = e.matmul(PSB[:, pc + 16:pc + 24], sb_, rsl1,
                                start=False, stop=True, skip_group_check=True)
            return last
        tt = sch.add("tensor", mm_dup, waits=[tok_fm], inc="t")
        tv = sch.add("vector", lambda e: e.tensor_copy(
            dup_sb, PSB[:, 1280:2048]), waits=[tt], inc="v")
        tok_dup = tv
        d_G = bass.AP(f_b, 0, [[2048, 128], [24, 32], [1, 8]])
        d_b = bass.AP(f_b, 8, [[2048, 128], [24, 32], [1, 8]])
        d_R = bass.AP(f_b, 16, [[2048, 128], [24, 32], [1, 8]])
        d2_G = bass.AP(f_b, 768, [[2048, 128], [32, 32], [1, 8]])
        d2_r = bass.AP(f_b, 768 + 8, [[2048, 128], [32, 32], [1, 8]])
        d2_bG = bass.AP(f_b, 768 + 16, [[2048, 128], [32, 32], [1, 8]])
        d2_Ge = bass.AP(f_b, 768 + 24, [[2048, 128], [32, 32], [1, 8]])
        ts = sch.add("scalar", lambda e: e.activation(d2_G, d_G, AF.Exp),
                     waits=[tok_dup], inc="s")
        ts = sch.add("scalar", lambda e: e.activation(d2_r, d_R, AF.Exp),
                     waits=[ts], inc="s")
        tv = sch.add("vector", lambda e: e.tensor_tensor(
            d2_bG, d_b, d2_G, ALU.mult), waits=[ts], inc="v")
        tv = sch.add("vector", lambda e: e.tensor_tensor(
            d2_Ge, d2_G, d2_r, ALU.mult), waits=[tv], inc="v")
        tok_gmach = tv


        def load_win(which, coloff):
            dst = wbuf[which][:, 0:1024].rearrange("p (e c) -> p e c", e=8)
            s2 = ap["w_in"][:, coloff:coloff + 128].rearrange(
                "(e p) c -> p e c", p=128)
            waits = [wbuf_tok[which]] if wbuf_tok[which] is not None else []
            return dma("gpsimd", dst, s2, waits=waits)

        def mm_qkvz_ech(e, which, ec, h):
            last = None
            wsl = wbuf[which][:, ec * 128:(ec + 1) * 128]
            for w in range(2):
                cols = slice(h * 1024 + w * 512, h * 1024 + w * 512 + 512)
                last = e.matmul(PSA[:, cols], wsl, x2h(ec)[:, cols],
                                start=(ec == 0), stop=(ec == 7),
                                skip_group_check=True)
            return last

        def emit_conv(cidx, dst, pre_toks):
            wj = lambda j: conv_sb[:, cidx * 4 + j:cidx * 4 + j + 1]
            tv = sch.add("vector", lambda e: e.tensor_scalar_mul(
                conv_acc[:, 0:1024], conv_pre[:, 0:1024], wj(3)),
                waits=list(pre_toks), inc="v")
            tv = sch.add("vector", lambda e: e.tensor_scalar_mul(
                conv_acc[:, 1024:2048], conv_pre[:, 1024:2048], wj(3)),
                waits=[tv], inc="v")
            for j, sh in ((2, 1), (1, 2), (0, 3)):
                tv = sch.add("vector", lambda e, j=j, sh=sh:
                             e.scalar_tensor_tensor(
                                 conv_acc[:, sh:1024], conv_pre[:, 0:1024 - sh],
                                 wj(j), conv_acc[:, sh:1024],
                                 ALU.mult, ALU.add), waits=[tv], inc="v")
                tv = sch.add("vector", lambda e, j=j, sh=sh:
                             e.scalar_tensor_tensor(
                                 conv_acc[:, 1024:2048],
                                 conv_pre[:, 1024 - sh:2048 - sh],
                                 wj(j), conv_acc[:, 1024:2048],
                                 ALU.mult, ALU.add), waits=[tv], inc="v")
            ts = sch.add("scalar", lambda e: e.activation(
                dst[:, 0:1024], conv_acc[:, 0:1024], AF.Silu),
                waits=[tv], inc="s")
            ts = sch.add("scalar", lambda e: e.activation(
                dst[:, 1024:2048], conv_acc[:, 1024:2048], AF.Silu),
                waits=[ts], inc="s")
            return ts

        def emit_l2norm(dst, qmode, tok_silu, extra=()):
            ts = sch.add("scalar", lambda e: e.activation(
                sq_scr, dst, AF.Square), waits=[tok_silu] + list(extra),
                inc="s")
            def mm_seg(e):
                last = None
                for w in range(4):
                    cols = slice(w * 512, (w + 1) * 512)
                    last = e.matmul(PSA[0:2, cols], seg2_bf[:],
                                    sq_scr[:, cols], start=True, stop=True,
                                    skip_group_check=True)
                return last
            tt = sch.add("tensor", mm_seg, waits=[ts], inc="t")
            ts = sch.add("scalar", lambda e: e.activation(
                invn_v, PSA[0:2, 0:2048], AF.Ln, bias=cpool[0:2, 1:2]),
                waits=[tt], inc="s")
            if qmode:
                ts = sch.add("scalar", lambda e: e.activation(
                    invn_v, invn_v, AF.Exp, scale=-0.5, bias=cpool[0:2, 3:4]),
                    waits=[ts], inc="s")
            else:
                ts = sch.add("scalar", lambda e: e.activation(
                    invn_v, invn_v, AF.Exp, scale=-0.5), waits=[ts], inc="s")
            def mm_rep2(e):
                last = None
                for w in range(4):
                    cols = slice(w * 512, (w + 1) * 512)
                    last = e.matmul(PSA[:, cols], sel2_bf[:],
                                    invn_v[:, cols], start=True, stop=True,
                                    skip_group_check=True)
                return last
            tt = sch.add("tensor", mm_rep2, waits=[ts], inc="t")
            tv = sch.add("vector", lambda e: e.tensor_mul(
                dst, dst, PSA[:, 0:2048]), waits=[tt], inc="v")
            return tv

        def emit_pair(ci, tok_gmach, prev):
            subs = [("k", D + 128 * ci, k_st, 8 + ci),
                    ("q", 128 * ci, q_st, ci),
                    ("v", 2 * D + 128 * ci, v_st, 16 + ci),
                    ("z", 3 * D + 128 * ci, z_st, None)]
            tok_cons = bar()
            pend = []
            evac_h = [list(tok_cons), list(tok_cons)]
            z_silu_h = []
            for si, (nm, off, dst, cidx) in enumerate(subs):
                tok_w = load_win(si % 2, off)
                pre_toks = []
                for h in range(2):
                    for ec in range(8):
                        w_ = ([tok_w] + list(evac_h[h])) if ec == 0 else []
                        tok_mm = sch.add("tensor",
                                         lambda e, which=si % 2, ec=ec, h=h:
                                         mm_qkvz_ech(e, which, ec, h),
                                         waits=w_, inc="t")
                        if prev is not None and ec % 2 == 1:
                            next(prev["seq"], None)
                    if nm == "z":
                        z_silu_h.append(tok_mm)
                    else:
                        tpre = sch.add("scalar",
                                       lambda e, h=h: e.copy(
                                           conv_pre[:, h * 1024:h * 1024 + 1024],
                                           PSA[:, h * 1024:h * 1024 + 1024]),
                                       waits=[tok_mm], inc="s")
                    evac_h[h] = [tpre]
                    pre_toks.append(tpre)
                wbuf_tok[si % 2] = tok_mm
                if nm != "z":
                    ts = emit_conv(cidx, dst, pre_toks)
                if nm in ("k", "q"):
                    pend.append((dst, nm == "q", ts))
            if prev is not None:
                for _ in range(3):
                    next(prev["seq"], None)
                prev["ygate"]([])
            zs_toks = []
            for h in range(2):
                zw_waits = [z_silu_h[h]]
                if prev is not None:
                    zw_waits.append(prev["y_done"])
                zs_toks.append(sch.add("scalar", lambda e, h=h: e.activation(
                    z_st[:, h * 1024:h * 1024 + 1024],
                    PSA[:, h * 1024:h * 1024 + 1024], AF.Silu),
                    waits=zw_waits, inc="s"))
            tz = sch.add("vector", lambda e: e.tensor_scalar_mul(
                z_st, z_st, nw_sb[:]), waits=zs_toks, inc="v")
            tok_cons = [tz, sch.now("s")]
            last_l2 = None
            if prev is not None and prev.get("segy_tok") is not None:
                last_l2 = prev["segy_tok"]
            for p_ in pend:
                ex = list(tok_cons) + ([last_l2] if last_l2 is not None else [])
                last_l2 = emit_l2norm(*p_, extra=ex)
            tok_kqvz = bar()

            def mm_grow(e, ci=ci):
                last = None
                for w in range(4):
                    cols = slice(w * 512, (w + 1) * 512)
                    last = e.matmul(PSA[:, cols],
                                    selj_f[:, ci * 128:(ci + 1) * 128],
                                    gfm_sb[:, cols], start=True, stop=True,
                                    skip_group_check=True)
                return last
            tok_gr = sch.add("tensor", mm_grow,
                             waits=tok_kqvz + [tok_gmach], inc="t")
            qgam = st_d[:, 6144:8192]
            ts = sch.add("scalar", lambda e: e.activation(
                qgam, PSA[:, 0:2048], AF.Exp), waits=[tok_gr], inc="s")
            gdup_b = bass.AP(f_b, ci, [[2048, 128], [24, 32], [0, 64]])
            tok_arg = sch.add("vector", lambda e: e.tensor_tensor(
                f_a[:], PSA[:, 0:2048], gdup_b, ALU.subtract),
                waits=[tok_gr, ts], inc="v")
            qg_waits = [ts, tok_arg]
            if prev is not None:
                qg_waits.append(prev["st"]["o_last"])
            tok_qg = sch.add("vector", lambda e: e.tensor_mul(
                qg_st, q_st, qgam), waits=qg_waits, inc="v")

            if prev is not None:
                prev["wo"]()
            xslt = [iv_bf[:, 0:1024], iv_bf[:, 1024:2048]]
            mslt = [miv_bf[:, 0:1024], miv_bf[:, 1024:2048]]
            zw_all = st_d[:, 0:2048]
            kh_all = st_d[:, 4096:5120]
            vtm_g = st_d[:, 5120:5376]
            ktm_g = st_d[:, 5376:5632]
            dmp_g = st_d[:, 5632:5888]
            dmm_g = st_d[:, 5888:6144]
            tok_h = tok_wt = t_ld = None
            for sg in range(2):
                sgc = 16 * sg
                sgbase = 1024 * sg
                tok_kh_l = []
                tok_zw_l = []
                tok_x1_l = []
                tok_m1_l = []
                for g in range(4):
                    cks = list(range(sgc + 4 * g, sgc + 4 * g + 4))
                    gbase = sgbase + 256 * g
                    lbase = 256 * g

                    def mm_kkqt(e, cks=cks):
                        last = None
                        for i, n in enumerate(cks):
                            rhs_a = bass.AP(st_c, n * 64,
                                            [[8192, 64], [2048, 2], [1, 64]])
                            rhs_b = bass.AP(st_c, 64 * 8192 + n * 64,
                                            [[8192, 64], [2048, 2], [1, 64]])
                            pc = 128 * i
                            e.matmul(PSA[0:64, pc:pc + 128],
                                     k_st[0:64, n * 64:(n + 1) * 64], rhs_a,
                                     start=True, stop=True,
                                     tile_position=(0, 0),
                                     skip_group_check=True)
                            last = e.matmul(PSA[64:128, pc:pc + 128],
                                            k_st[64:128, n * 64:(n + 1) * 64],
                                            rhs_b, start=True, stop=True,
                                            tile_position=(64, 64),
                                            skip_group_check=True)
                        return last
                    if sg == 0 and g == 0:
                        kk_waits = [tok_qg] + (prev["wo_pv"] if prev else [])
                        if prev is not None:
                            kk_waits.append(prev["y_done"])
                    elif g > 0:
                        kk_waits = [tok_att, tok_zw_l[-1]]
                    else:
                        kk_waits = [tok_h, tok_wt, t_ld]
                    tok_kk = sch.add("tensor", mm_kkqt, waits=kk_waits,
                                     inc="t")

                    def mm_brep(e, g=g, sgbase=sgbase, gbase=gbase):
                        return e.matmul(PSA[:, 512:768],
                                        selj_bf[:, ci * 128:(ci + 1) * 128],
                                        bfm[:, gbase:gbase + 256],
                                        start=True, stop=True,
                                        skip_group_check=True)
                    tok_br = sch.add("tensor", mm_brep, waits=[tok_kk],
                                     inc="t")

                    dsl = f_a[:, gbase:gbase + 256]
                    mbi_b = bass.AP(mb_incl_f, 0, [[64, 128], [0, 4], [1, 64]])
                    mbT_b = bass.AP(mbT_strict_f, 0,
                                    [[64, 128], [0, 4], [1, 64]])
                    tv0 = sch.add("vector", lambda e, d=dsl, m=mbi_b:
                                  e.scalar_tensor_tensor(
                                      dmp_g, d, 1.0, m, ALU.mult, ALU.min),
                                  waits=[tok_arg], inc="v")
                    tok_dmp = sch.add("scalar", lambda e: e.activation(
                        dmp_g, dmp_g, AF.Exp), waits=[tv0], inc="s")
                    tv0 = sch.add("vector", lambda e, d=dsl, m=mbT_b:
                                  e.scalar_tensor_tensor(
                                      dmm_g, d, 1.0, m, ALU.mult, ALU.max),
                                  waits=[tok_arg, tok_dmp], inc="v")
                    tok_dmm = sch.add("scalar", lambda e: e.activation(
                        dmm_g, dmm_g, AF.Exp, scale=-1.0),
                        waits=[tv0], inc="s")

                    kk3 = bass.AP(PSA, 0, [[2048, 128], [128, 4], [1, 64]])
                    kq3 = bass.AP(PSA, 64, [[2048, 128], [128, 4], [1, 64]])
                    br3 = bass.AP(PSA, 512, [[2048, 128], [64, 4], [1, 64]])
                    mstr_b = bass.AP(m_strict_bf, 0,
                                     [[64, 128], [0, 4], [1, 64]])
                    bdup_b = bass.AP(f_b, 24 * (sgc + 4 * g) + 8 + ci,
                                     [[2048, 128], [24, 4], [0, 64]])
                    bgdup_b = bass.AP(f_b, 768 + 32 * (sgc + 4 * g) + 16 + ci,
                                      [[2048, 128], [32, 4], [0, 64]])
                    ratdup_b = bass.AP(f_b, 768 + 32 * (sgc + 4 * g) + 8 + ci,
                                       [[2048, 128], [32, 4], [0, 64]])
                    x1sl = xslt[0][:, lbase:lbase + 256]
                    m1sl = mslt[0][:, lbase:lbase + 256]

                    tv = sch.add("vector", lambda e, a=kk3, o=x1sl:
                                 e.tensor_tensor(o, dmp_g, a, ALU.mult),
                                 waits=[tok_kk, tok_dmp], inc="v")
                    tv = sch.add("vector", lambda e, a=br3, o=x1sl:
                                 e.tensor_tensor(o, o, a, ALU.mult),
                                 waits=[tv, tok_br], inc="v")
                    tok_x1 = sch.add("vector", lambda e, a=mstr_b, o=x1sl:
                                     e.tensor_tensor(o, o, a, ALU.mult),
                                     waits=[tv], inc="v")
                    tok_x1_l.append(tok_x1)
                    tv = sch.add("vector", lambda e, a=kk3, o=m1sl:
                                 e.tensor_tensor(o, dmm_g, a, ALU.mult),
                                 waits=[tok_kk, tok_dmm], inc="v")
                    tok_m1 = sch.add("vector", lambda e, a=bdup_b, o=m1sl:
                                     e.tensor_tensor(o, o, a, ALU.mult),
                                     waits=[tv], inc="v")
                    tok_m1_l.append(tok_m1)
                    attsl = att_st[:, gbase:gbase + 256]
                    tok_att = sch.add("vector", lambda e, a=attsl, b=kq3:
                                      e.tensor_tensor(a, dmp_g, b, ALU.mult),
                                      waits=[tok_kk, tok_dmp], inc="v")

                    def mm_trs(e, cks=cks):
                        last = None
                        for i, n in enumerate(cks):
                            for (srcst, base) in ((v_st, 1024), (k_st, 1280)):
                                e.matmul(
                                    PSA[0:64, base + 64 * i:base + 64 * i + 64],
                                    srcst[0:64, n * 64:(n + 1) * 64],
                                    iden_bf[0:64, 0:64], start=True,
                                    stop=True, tile_position=(0, 0),
                                    skip_group_check=True)
                                last = e.matmul(
                                    PSA[64:128, base + 64 * i:base + 64 * i + 64],
                                    srcst[64:128, n * 64:(n + 1) * 64],
                                    iden_bf[64:128, 64:128], start=True,
                                    stop=True, tile_position=(64, 64),
                                    skip_group_check=True)
                        return last
                    tok_tr = sch.add("tensor", mm_trs,
                                     waits=[tok_x1, tok_att], inc="t")
                    tok_vtm = sch.add("vector", lambda e: e.tensor_copy(
                        vtm_g, PSA[:, 1024:1280]), waits=[tok_tr], inc="v")
                    tok_ktm = sch.add("vector", lambda e: e.tensor_copy(
                        ktm_g, PSA[:, 1280:1536]), waits=[tok_tr], inc="v")

                    zwv = bass.AP(st_d, 128 * 4 * g,
                                  [[8192, 128], [128, 4], [1, 64]])
                    zwd = bass.AP(st_d, 128 * 4 * g + 64,
                                  [[8192, 128], [128, 4], [1, 64]])
                    vtm3 = bass.AP(st_d, 5120, [[8192, 128], [64, 4], [1, 64]])
                    ktm3 = bass.AP(st_d, 5376, [[8192, 128], [64, 4], [1, 64]])
                    tv = sch.add("vector", lambda e, a=zwv, b=vtm3, c=bdup_b:
                                 e.tensor_tensor(a, b, c, ALU.mult),
                                 waits=[tok_vtm], inc="v")
                    tok_zw = sch.add("vector",
                                     lambda e, a=zwd, b=ktm3, c=bgdup_b:
                                     e.tensor_tensor(a, b, c, ALU.mult),
                                     waits=[tv, tok_ktm], inc="v")
                    tok_zw_l.append(tok_zw)
                    khsl = kh_all[:, 64 * 4 * g:64 * 4 * g + 256]
                    tok_kh = sch.add("vector",
                                     lambda e, o=khsl, b=ktm3, c=ratdup_b:
                                     e.tensor_tensor(o, b, c, ALU.mult),
                                     waits=[tok_zw], inc="v")
                    tok_kh_l.append(tok_kh)

                def prod16(e, lh, rh, slot):
                    last = None
                    for i in range(16):
                        c = slice(64 * i, 64 * i + 64)
                        e.matmul(slot[0:64, c], lh[0:64, c], rh[0:64, c],
                                 start=True, stop=True, tile_position=(0, 0),
                                 skip_group_check=True)
                        last = e.matmul(slot[64:128, c], lh[64:128, c],
                                        rh[64:128, c], start=True, stop=True,
                                        tile_position=(64, 64),
                                        skip_group_check=True)
                    return last

                tok_bld = [tok_x1_l[-1], tok_m1_l[-1], tok_zw_l[-1]]
                ap_toks = []
                tok_zcur = tok_zw_l[-1]
                tokx = {0: tok_x1_l[-1]}
                tokm = {0: tok_m1_l[-1]}
                for w in range(2):
                    zps = PSB[:, 1024 * w:1024 * w + 1024]
                    def mmz(e, w=w, zps=zps, xs=xslt[0]):
                        last = None
                        for i in range(8):
                            ii = 8 * w + i
                            xc = slice(64 * ii, 64 * ii + 64)
                            zc = slice(128 * ii, 128 * ii + 128)
                            zc2 = slice(128 * i, 128 * i + 128)
                            e.matmul(zps[0:64, zc2], xs[0:64, xc],
                                     zw_all[0:64, zc], start=True, stop=True,
                                     tile_position=(0, 0),
                                     skip_group_check=True)
                            last = e.matmul(zps[64:128, zc2], xs[64:128, xc],
                                            zw_all[64:128, zc], start=True,
                                            stop=True,
                                            tile_position=(64, 64),
                                            skip_group_check=True)
                        return last
                    tm = sch.add("tensor", mmz,
                                 waits=tok_bld + ([] if w == 0 else
                                                  [ap_toks[-1]]), inc="t")
                    ap_toks.append(tm)
                    tok_zcur = sch.add("vector",
                                       lambda e, w=w, zps=zps:
                                       e.tensor_tensor(
                                           zw_all[:, 1024 * w:1024 * w + 1024],
                                           zw_all[:, 1024 * w:1024 * w + 1024],
                                           zps, ALU.subtract),
                                       waits=[tm, tok_zcur], inc="v")
                for lvl in range(5):
                    ssrc, sdst = lvl % 2, 1 - lvl % 2
                    pps = PSA[:, 1024:2048]
                    def mmp(e, ssrc=ssrc):
                        return prod16(e, xslt[ssrc], mslt[ssrc], pps)
                    w_p = [tokx[lvl], tokm[lvl]]
                    if lvl >= 1:
                        w_p.append(tok_evx)
                        w_p.append(tok_evm)
                    tok_pm = None
                    if lvl < 4:
                        tok_pm = sch.add("tensor", mmp, waits=w_p, inc="t")
                        tok_evm = sch.add("vector",
                                          lambda e, sdst=sdst: e.tensor_copy(
                                              mslt[sdst][:], pps[:, 0:1024]),
                                          waits=[tok_pm], inc="v")
                        tokm[lvl + 1] = tok_evm
                    pps2 = PSB[:, 0:512]
                    def mmp2(e, ssrc=ssrc):
                        return prod16(e, mslt[ssrc], xslt[ssrc],
                                      PSA[:, 0:1024])
                    w_p2 = list(w_p) + ([tok_pm] if tok_pm else [])
                    tok_px = sch.add("tensor", mmp2, waits=w_p2, inc="t")
                    tok_evx = sch.add("scalar",
                                      lambda e, sdst=sdst: e.copy(
                                          xslt[sdst][:], PSA[:, 0:1024]),
                                      waits=[tok_px], inc="s")
                    tokx[lvl + 1] = tok_evx
                    for w in range(2):
                        zps = PSB[:, 1024 * w:1024 * w + 1024]
                        def mmz(e, w=w, zps=zps, sdst=sdst):
                            last = None
                            xs = xslt[sdst]
                            for i in range(8):
                                ii = 8 * w + i
                                xc = slice(64 * ii, 64 * ii + 64)
                                zc = slice(128 * ii, 128 * ii + 128)
                                zc2 = slice(128 * i, 128 * i + 128)
                                e.matmul(zps[0:64, zc2], xs[0:64, xc],
                                         zw_all[0:64, zc], start=True,
                                         stop=False, tile_position=(0, 0),
                                         skip_group_check=True)
                                e.matmul(zps[0:64, zc2],
                                         iden_bf[0:64, 0:64],
                                         zw_all[0:64, zc], start=False,
                                         stop=True, tile_position=(0, 0),
                                         skip_group_check=True)
                                e.matmul(zps[64:128, zc2], xs[64:128, xc],
                                         zw_all[64:128, zc], start=True,
                                         stop=False, tile_position=(64, 64),
                                         skip_group_check=True)
                                last = e.matmul(zps[64:128, zc2],
                                                iden_bf[64:128, 64:128],
                                                zw_all[64:128, zc],
                                                start=False, stop=True,
                                                tile_position=(64, 64),
                                                skip_group_check=True)
                            return last
                        tm = sch.add("tensor", mmz,
                                     waits=[tokx[lvl + 1], tok_zcur], inc="t")
                        tok_zcur = sch.add("scalar",
                                           lambda e, w=w, zps=zps:
                                           e.copy(
                                               zw_all[:, 1024 * w:
                                                      1024 * w + 1024],
                                               zps),
                                           waits=[tm, tok_zcur], inc="s")
                tok_z = tok_zcur

                uv3o = bass.AP(wbuf[0], 3072 + sgbase,
                               [[8192, 128], [64, 16], [1, 64]])
                zv3 = bass.AP(st_d, 0, [[8192, 128], [128, 16], [1, 64]])
                tok_uv = sch.add("vector", lambda e, a=uv3o, b=zv3:
                                 e.tensor_copy(a, b), waits=[tok_z], inc="v")

                def mm_wf(e):
                    last = None
                    for i in range(16):
                        zd = zw_all[:, 128 * i + 64:128 * i + 128]
                        e.matmul(PSA[0:64, 64 * i:64 * i + 64], zd[0:64, :],
                                 iden_bf[0:64, 0:64], start=True, stop=True,
                                 tile_position=(0, 0), skip_group_check=True)
                        last = e.matmul(PSA[64:128, 64 * i:64 * i + 64],
                                        zd[64:128, :],
                                        iden_bf[64:128, 64:128], start=True,
                                        stop=True, tile_position=(64, 64),
                                        skip_group_check=True)
                    return last
                tok_wfm = sch.add("tensor", mm_wf, waits=[tok_z, tok_evx], inc="t")
                wfo = bass.AP(wbuf[0], 5120 + sgbase,
                              [[8192, 128], [64, 16], [1, 64]])
                tok_wt = sch.add("scalar", lambda e, a=wfo: e.copy(
                    a, PSA[:, 0:1024]), waits=[tok_wfm], inc="s")

                def mm_pl(e):
                    last = None
                    for i in range(16):
                        c = slice(64 * i, 64 * i + 64)
                        e.matmul(PSA[0:64, 1024 + 64 * i:1088 + 64 * i],
                                 kh_all[0:64, c],
                                 zw_all[0:64, 128 * i:128 * i + 64],
                                 start=True, stop=True, tile_position=(0, 0),
                                 skip_group_check=True)
                        last = e.matmul(PSA[64:128, 1024 + 64 * i:1088 + 64 * i],
                                        kh_all[64:128, c],
                                        zw_all[64:128, 128 * i:128 * i + 64],
                                        start=True, stop=True,
                                        tile_position=(64, 64),
                                        skip_group_check=True)
                    return last
                tok_plm = sch.add("tensor", mm_pl,
                                  waits=[tok_z] + tok_kh_l, inc="t")
                plo = bass.AP(wbuf[1], 3072 + sgbase,
                              [[8192, 128], [64, 16], [1, 64]])
                tok_pl = sch.add("vector", lambda e, a=plo: e.tensor_copy(
                    a, PSA[:, 1024:2048]), waits=[tok_plm], inc="v")

                def mm_h(e):
                    last = None
                    for i in range(16):
                        c = slice(64 * i, 64 * i + 64)
                        e.matmul(PSB[0:64, 64 * i:64 * i + 64],
                                 zw_all[0:64, 128 * i + 64:128 * i + 128],
                                 kh_all[0:64, c], start=True, stop=True,
                                 tile_position=(0, 0), skip_group_check=True)
                        last = e.matmul(PSB[64:128, 64 * i:64 * i + 64],
                                        zw_all[64:128, 128 * i + 64:128 * i + 128],
                                        kh_all[64:128, c], start=True,
                                        stop=True, tile_position=(64, 64),
                                        skip_group_check=True)
                    return last
                tok_hm = sch.add("tensor", mm_h,
                                 waits=[tok_z, tok_pl] + tok_kh_l, inc="t")
                tprev = tok_hm
                for i in range(16):
                    n = sgc + i
                    gecol = f_b[:, 768 + 32 * n + 24 + ci:
                                768 + 32 * n + 25 + ci]
                    tprev = sch.add("vector",
                                    lambda e, i=i, n=n, gecol=gecol:
                                    e.scalar_tensor_tensor(
                                        h_st[:, 64 * n:64 * n + 64],
                                        bdi_bf[:], gecol,
                                        PSB[:, 64 * i:64 * i + 64],
                                        ALU.mult, ALU.subtract),
                                    waits=[tprev], inc="v")
                tok_h = tprev
                t_ld = tok_evx

            stt = {}
            back = {"st": stt, "wo_pv": [], "seq": None,
                    "y_done": None, "ygate": None, "wo": None}
            ypair = st_d[:, 0:2048]

            def gen_seq(tok_h=tok_h, tok_wt=tok_wt, t_ld=t_ld, stt=stt):
                tv = sch.add("vector", lambda e: e.memset(smast[:], 0.0),
                             waits=[tok_h, tok_wt, t_ld], inc="v")
                tok_cast = sch.add("vector", lambda e: e.memset(sbf3[2][:], 0.0),
                                   waits=[tv], inc="v")
                tok_su_prev = None
                tok_ws = {}
                tok_u = {}
                tok_o = {}
                tok_yv = {}
                for n in range(NCH + 1):
                    if n >= 1:
                        yield
                    if n < NCH:
                        nc64 = slice(64 * n, 64 * n + 64)
                        sprev = sbf3[(n - 1) % 3]
                        snext = sbf3[n % 3]
                        def mm_su(e, nc64=nc64, sprev=sprev):
                            e.matmul(PSB[0:64, 512:576], h_st[0:64, nc64],
                                     sprev[0:64, :], start=True, stop=True,
                                     tile_position=(0, 0), skip_group_check=True)
                            return e.matmul(PSB[64:128, 512:576],
                                            h_st[64:128, nc64], sprev[64:128, :],
                                            start=True, stop=True,
                                            tile_position=(64, 64),
                                            skip_group_check=True)
                        w_su = [tok_cast]
                        if n >= 1:
                            w_su.append(tok_smadd)
                        tok_su = sch.add("tensor", mm_su, waits=w_su, inc="t")
                        tok_smadd = sch.add("vector", lambda e, n=n:
                                            e.tensor_tensor(
                                                smast[:], PSB[:, 512:576],
                                                wbuf[1][:, 3072 + 64 * n:
                                                        3072 + 64 * n + 64],
                                                ALU.add),
                                            waits=[tok_su], inc="v")
                        tok_cast_next = sch.add("vector",
                                                lambda e, snext=snext:
                                                e.tensor_copy(snext[:], smast[:]),
                                                waits=[tok_smadd], inc="v")
                        def mm_ws2(e, n=n, sprev=sprev):
                            wsl = wbuf[0][:, 5120 + 64 * n:5184 + 64 * n]
                            e.matmul(PSB[0:64, 1024:1088], wsl[0:64, :],
                                     sprev[0:64, :], start=True, stop=True,
                                     tile_position=(0, 0), skip_group_check=True)
                            return e.matmul(PSB[64:128, 1024:1088],
                                            wsl[64:128, :], sprev[64:128, :],
                                            start=True, stop=True,
                                            tile_position=(64, 64),
                                            skip_group_check=True)
                        w_ws = [tok_cast]
                        if n >= 1:
                            w_ws.append(tok_u[n - 1])
                        tok_ws[n] = sch.add("tensor", mm_ws2, waits=w_ws, inc="t")
                        tok_u[n] = sch.add("vector", lambda e, n=n, u=utm2[n % 2]:
                                           e.tensor_sub(
                                               u, wbuf[0][:, 3072 + 64 * n:
                                                          3072 + 64 * n + 64],
                                               PSB[:, 1024:1088]),
                                           waits=[tok_ws[n]], inc="v")
                        tok_cast = tok_cast_next
                    if n >= 1:
                        m = n - 1
                        mc64 = slice(64 * m, 64 * m + 64)
                        sm = sbf3[(m - 1) % 3]
                        def mm_o(e, mc64=mc64, sm=sm, u=utm2[(n - 1) % 2]):
                            e.matmul(PSB[0:64, 1536:1600], u[0:64, :],
                                     att_st[0:64, mc64], start=True, stop=False,
                                     tile_position=(0, 0), skip_group_check=True)
                            e.matmul(PSB[0:64, 1536:1600], sm[0:64, :],
                                     qg_st[0:64, mc64], start=False, stop=True,
                                     tile_position=(0, 0), skip_group_check=True)
                            e.matmul(PSB[64:128, 1536:1600], u[64:128, :],
                                     att_st[64:128, mc64], start=True, stop=False,
                                     tile_position=(64, 64),
                                     skip_group_check=True)
                            return e.matmul(PSB[64:128, 1536:1600], sm[64:128, :],
                                            qg_st[64:128, mc64], start=False,
                                            stop=True, tile_position=(64, 64),
                                            skip_group_check=True)
                        w_o = [tok_u[m]]
                        if m >= 1:
                            w_o.append(tok_yv[m - 1])
                        tok_o[m] = sch.add("tensor", mm_o, waits=w_o, inc="t")
                        tok_yv[m] = sch.add("scalar",
                                            lambda e, mc64=mc64: e.copy(
                                                ypair[:, mc64], PSB[:, 1536:1600]),
                                            waits=[tok_o[m]], inc="s")


                stt["o_last"] = tok_o[NCH - 1]
                stt["yv_last"] = tok_yv[NCH - 1]

            def do_ygate(extra=(), stt=stt):
                tok_sq = sch.add("scalar", lambda e: e.activation(
                    sq_scr, ypair, AF.Square),
                    waits=[stt["yv_last"], sch.now("v")] + list(extra), inc="s")
                def mm_seg_y(e):
                    last = None
                    for w in range(4):
                        cols = slice(w * 512, (w + 1) * 512)
                        last = e.matmul(PSB[0:2, cols], seg2_bf[:],
                                        sq_scr[:, cols], start=True, stop=True,
                                        skip_group_check=True)
                    return last
                tt = sch.add("tensor", mm_seg_y, waits=[tok_sq] + bar(), inc="t")
                back["segy_tok"] = tt
                ts = sch.add("scalar", lambda e: e.activation(
                    invn_v, PSB[0:2, 0:2048], AF.Ln, scale=1.0 / HD,
                    bias=cpool[0:2, 1:2]), waits=[tt], inc="s")
                ts = sch.add("scalar", lambda e: e.activation(
                    invn_v, invn_v, AF.Exp, scale=-0.5), waits=[ts], inc="s")
                def mm_rep_y(e):
                    last = None
                    for w in range(4):
                        cols = slice(w * 512, (w + 1) * 512)
                        last = e.matmul(PSB[:, cols], sel2_bf[:],
                                        invn_v[:, cols], start=True, stop=True,
                                        skip_group_check=True)
                    return last
                tt = sch.add("tensor", mm_rep_y, waits=[ts], inc="t")
                tv = sch.add("vector", lambda e: e.tensor_mul(
                    ypair, ypair, PSB[:, 0:2048]), waits=[tt], inc="v")
                tok_y = sch.add("vector", lambda e: e.tensor_mul(
                    ypair, ypair, z_st), waits=[tv], inc="v")


                back["y_done"] = tok_y
                back["tok_y"] = tok_y

            def do_wo():
                dst = wo_buf.rearrange("p (e c) -> p e c", e=1)
                s2 = ap["w_o"][128 * ci:128 * (ci + 1), :].rearrange(
                    "(e p) c -> p e c", p=128)
                tok_y = back["tok_y"]
                tok_wo = dma("gpsimd", dst, s2, waits=[tok_y])
                pv = [None, None]
                base0 = bar()
                for t8 in range(16):
                    dci, hv2 = t8 // 2, t8 % 2
                    ps = PSA[:, 1024 * (t8 % 2):1024 * (t8 % 2) + 1024]
                    def mm_wo(e, dci=dci, hv2=hv2, ps=ps):
                        last = None
                        wsl = wo_buf[:, dci * 128:dci * 128 + 128]
                        for w in range(2):
                            cols = slice(w * 512, (w + 1) * 512)
                            last = e.matmul(
                                ps[:, cols], wsl,
                                st_d[:, hv2 * 1024 + w * 512:
                                     hv2 * 1024 + w * 512 + 512],
                                start=True, stop=True,
                                skip_group_check=True)
                        return last
                    wts = [tok_wo, tok_y] + base0
                    if pv[t8 % 2] is not None:
                        wts.append(pv[t8 % 2])
                    tokm = sch.add("tensor", mm_wo, waits=wts, inc="t")
                    xc = x_sb[:, dci * T + hv2 * 1024:dci * T + hv2 * 1024 + 1024]
                    pv[t8 % 2] = sch.add("vector",
                                         lambda e, xc=xc, ps=ps: e.tensor_tensor(
                                             xc, xc, ps, ALU.add),
                                         waits=[tokm], inc="v")
                back["wo_pv"] = [pv[0], pv[1]]

            back["seq"] = gen_seq()
            back["ygate"] = do_ygate
            back["wo"] = do_wo
            return back


        sch.emit("p2_gmach")
        prev = None
        for ci in range(8):
            prev = emit_pair(ci, tok_gmach, prev)
        for _ in range(40):
            next(prev["seq"], None)
        prev["ygate"]([])
        prev["wo"]()
        sch.emit("p2_pairs")
        return None

    if 1 in parts:
        tok = emit_ffn(0.5, "ffn1")
        sch.emit("p1_ffn1")
    if 2 in parts:
        emit_attn()
    if 3 in parts:
        tok = emit_ffn(0.5, "ffn2")
        sch.emit("p3_ffn2")

    def store(eng):
        last = eng.dma_start(
            out.rearrange("(c p) t -> p c t", p=128),
            x_sb[:, 0:8 * T].rearrange("p (c t) -> p c t", c=8))
        return last.then_inc(sch.sems["st"], 16)
    sch.count["st"] += 16
    sch.ops["sync"].append((store, (), None, 0))
    sch.ops["sync"].append((
        lambda e: e.wait_ge(sch.sems["st"], sch.count["st"]) or None,
        (), None, 0))

    for k, dap in dbg_out.items():
        srct = {"x_sb": x_sb, "iv": iv_bf, "st_a": st_a, "st_b": st_b,
                "st_c": st_c, "st_d": st_d, "st_f": st_f,
                "miv": miv_bf, "fa": f_a, "fb": f_b, "gfm": gfm_sb,
                "wbuf0": wbuf[0], "wbuf1": wbuf[1]}[k]
        sh = dbg[k]
        def mkst(dap=dap, srct=srct, sh=sh):
            def fn(e):
                return e.dma_start(dap, srct[0:sh[0], 0:sh[1]]
                                   ).then_inc(sch.sems["st"], 16)
            return fn
        sch.count["st"] += 16
        sch.ops["gpsimd"].append((mkst(), (), None, 0))
    total_st = sch.count["st"]
    sch.ops["gpsimd"].append((
        lambda e: e.wait_ge(sch.sems["st"], total_st) or None,
        (), None, 0))
    sch.emit("p9_store")

    for t in reversed(ctx):
        t.__exit__(None, None, None)
    for s in list(sch.sems.values()):
        pass
    return nc


_CACHE = {}


def kernel(**inputs):
    maps = _host_layout(inputs)
    nc = build()
    res = run_bass_kernel_spmd(nc, maps, core_ids=list(range(B)))
    out = np.stack([np.ascontiguousarray(r["out"].T) for r in res.results], 0)
    return out.astype(np.float32)


# version 10
# speedup vs baseline: 1.0032x; 1.0032x over previous
import numpy as np
import concourse.bass as bass
import concourse.mybir as mybir
from concourse.bass_utils import run_bass_kernel_spmd

F32 = mybir.dt.float32
BF16 = mybir.dt.bfloat16
AF = mybir.ActivationFunctionType
ALU = mybir.AluOpType

B, T, D = 8, 2048, 1024
H, HD = 16, 64
C = 64
NCH = T // C
NPAIR = 8
EPS_LN = 1e-5
EPS_L2 = 1e-6


class Sched:

    ENGINES = ("tensor", "vector", "scalar", "gpsimd", "sync")

    def __init__(self, nc):
        self.nc = nc
        self.ops = {e: [] for e in self.ENGINES}
        self.sems = {}
        self.count = {}

    def sem(self, name):
        if name not in self.sems:
            self.sems[name] = self.nc.semaphore(name).__enter__()
            self.count[name] = 0
        return self.sems[name]

    def add(self, engine, fn, waits=(), inc=None, dinc=0):
        for w in waits:
            assert w is not None, (engine, "None wait")
            s, cnt = w
            assert cnt <= self.count[s], (
                "wait on future token", engine, s, cnt, self.count[s])
        tok = None
        if inc is not None:
            self.sem(inc)
            self.count[inc] += (dinc if dinc else 1)
            tok = (inc, self.count[inc])
        self.ops[engine].append((fn, tuple(waits), inc, dinc))
        return tok

    def now(self, sem):
        return (sem, self.count[sem])

    def emit(self, name):
        nc = self.nc
        with nc.Block(name) as blk:
            for engine in self.ENGINES:
                if not self.ops[engine]:
                    continue

                def make(engine):
                    oplist = self.ops[engine]

                    def body(eng):
                        waited = {}
                        for fn, waits, inc, dinc in oplist:
                            for (s, cnt) in waits:
                                if waited.get(s, -1) >= cnt:
                                    continue
                                eng.wait_ge(self.sems[s], cnt)
                                waited[s] = cnt
                            last = fn(eng)
                            if inc is not None and dinc == 0:
                                assert last is not None, (engine, inc)
                                last.then_inc(self.sems[inc], 1)
                    return body

                getattr(blk, engine)(make(engine))
        self.ops = {e: [] for e in self.ENGINES}


def _host_layout(inputs):
    f = np.float32
    x = np.asarray(inputs["L0_attn_x"], f)
    w_in = np.asarray(inputs["L0_attn_W_in"], f)
    w_in2 = np.zeros((8, 4, 128, 1024), f)
    for ci in range(8):
        for si, coloff in enumerate((1024 + 128 * ci, 128 * ci,
                                     2048 + 128 * ci, 3072 + 128 * ci)):
            blk = w_in[:, coloff:coloff + 128]
            w_in2[ci, si] = blk.reshape(8, 128, 128).transpose(
                1, 0, 2).reshape(128, 1024)
    w_in2 = np.ascontiguousarray(w_in2.reshape(32, 128, 1024))
    w_ba = np.ascontiguousarray(np.asarray(inputs["L0_attn_W_ba"], f))
    w_o = np.ascontiguousarray(np.asarray(inputs["L0_attn_W_o"], f))
    wrT = np.ascontiguousarray(np.asarray(inputs["L0_ffn_W_r"], f).T)
    wkT = np.ascontiguousarray(np.asarray(inputs["L0_ffn_W_k"], f).T)
    wvT = np.ascontiguousarray(np.asarray(inputs["L0_ffn_W_v"], f).T)
    mu_r = np.ascontiguousarray(
        np.asarray(inputs["L0_ffn_mu_r"], f).reshape(8, 128).T)
    mu_k = np.ascontiguousarray(
        np.asarray(inputs["L0_ffn_mu_k"], f).reshape(8, 128).T)
    conv = np.asarray(inputs["L0_attn_conv_w"], f)[:, 0, :]
    conv = np.ascontiguousarray(
        conv.reshape(24, 128, 4).transpose(1, 0, 2).reshape(128, 96))
    dtb = np.asarray(inputs["L0_attn_dt_bias"], f).reshape(1, 16)
    alog = np.asarray(inputs["L0_attn_A_log"], f).reshape(1, 16)
    nw = np.ascontiguousarray(
        np.tile(np.asarray(inputs["L0_attn_norm_w"], f), 2).reshape(128, 1))

    iden = np.eye(128, dtype=f)
    cum = np.zeros((128, 128), f)
    for blk in range(2):
        for s in range(64):
            for t in range(s, 64):
                cum[blk * 64 + s, blk * 64 + t] = 1.0
    rcum = np.zeros((128, 128), f)
    for blk in range(2):
        for s in range(64):
            for t in range(s):
                rcum[blk * 64 + s, blk * 64 + t] = 1.0
    selA = np.zeros((128, 128), f)
    selB = np.zeros((128, 128), f)
    for s in range(64):
        for h in (0, 64):
            selA[h + s, s] = 1.0
            selB[h + s, 64 + s] = 1.0
    selj = np.zeros((16, 8 * 128), f)
    for j in range(8):
        selj[2 * j, j * 128:j * 128 + 64] = 1.0
        selj[2 * j + 1, j * 128 + 64:(j + 1) * 128] = 1.0
    seg2 = np.zeros((128, 2), f)
    seg2[0:64, 0] = 1.0
    seg2[64:128, 1] = 1.0
    sel2 = np.zeros((2, 128), f)
    sel2[0, 0:64] = 1.0
    sel2[1, 64:128] = 1.0
    m_incl = np.zeros((128, 64), f)
    m_strict = np.zeros((128, 64), f)
    for p in range(128):
        s = p % 64
        for q in range(64):
            if s <= q:
                m_incl[p, q] = 1.0
            if s < q:
                m_strict[p, q] = 1.0
    mT_strict = np.zeros((128, 64), f)
    for p in range(128):
        q = p % 64
        for s in range(64):
            if s < q:
                mT_strict[p, s] = 1.0
    mb_incl = np.where(m_incl > 0, 0.0, -1e30).astype(f)
    mbT_strict = np.where(mT_strict > 0, 0.0, 1e30).astype(f)
    bdi = np.zeros((128, 64), f)
    for p in range(128):
        bdi[p, p % 64] = 1.0
    ones_r = np.ones((1, 128), f)
    ones_c = np.ones((128, 1), f)
    ones_all = np.ones((128, 128), f)

    shared = dict(
        w_in2=w_in2, w_ba=w_ba, w_o=w_o, wrT=wrT, wkT=wkT, wvT=wvT,
        mu_r=mu_r, mu_k=mu_k, conv=conv, dtb=dtb, alog=alog, nw=nw,
        iden=iden, cum=cum, rcum=rcum, selA=selA, selB=selB,
        selj=selj, seg2=seg2, sel2=sel2,
        m_incl=m_incl, m_strict=m_strict, mT_strict=mT_strict, bdi=bdi,
        mb_incl=mb_incl, mbT_strict=mbT_strict,
        ones_r=ones_r, ones_c=ones_c, ones_all=ones_all,
    )
    maps = []
    for bi in range(B):
        m = dict(shared)
        m["x"] = np.ascontiguousarray(x[bi].T)
        maps.append(m)
    return maps


def build(dbg=None, parts=(1, 2, 3)):
    dbg = dbg or {}
    nc = bass.Bass("TRN2", target_bir_lowering=False)

    ap = {}
    def din(name, shape):
        ap[name] = nc.dram_tensor(name, list(shape), F32,
                                  kind="ExternalInput").ap()
    din("x", (D, T))
    din("w_in2", (32, 128, 1024))
    din("w_ba", (D, 2 * H)); din("w_o", (D, D))
    din("wrT", (D, D)); din("wkT", (D, D)); din("wvT", (D, D))
    din("mu_r", (128, 8)); din("mu_k", (128, 8))
    din("conv", (128, 96)); din("dtb", (1, 16)); din("alog", (1, 16))
    din("nw", (128, 1))
    din("iden", (128, 128)); din("cum", (128, 128)); din("rcum", (128, 128))
    din("selA", (128, 128)); din("selB", (128, 128))
    din("selj", (16, 1024)); din("seg2", (128, 2)); din("sel2", (2, 128))
    din("m_incl", (128, 64)); din("m_strict", (128, 64))
    din("mT_strict", (128, 64)); din("bdi", (128, 64))
    din("mb_incl", (128, 64)); din("mbT_strict", (128, 64))
    din("ones_r", (1, 128)); din("ones_c", (128, 1))
    din("ones_all", (128, 128))
    out = nc.dram_tensor("out", [D, T], F32, kind="ExternalOutput").ap()
    dbg_out = {k: nc.dram_tensor("dbg_" + k, list(v), F32,
                                 kind="ExternalOutput").ap()
               for k, v in dbg.items()}

    ctx = []
    def sb(name, shape, dtype):
        t = nc.sbuf_tensor(name, list(shape), dtype)
        ctx.append(t)
        return t.__enter__()
    def psum(name, shape, dtype=F32):
        t = nc.psum_tensor(name, list(shape), dtype)
        ctx.append(t)
        return t.__enter__()

    x_sb = sb("x_sb", [128, 8 * T], F32)
    wbuf = [sb(f"wbuf{i}", [128, 8 * 1024], BF16) for i in range(2)]
    st_a = sb("st_a", [128, 8 * 1024], BF16)
    st_b = sb("st_b", [128, 8 * 1024], BF16)
    st_c = sb("st_c", [128, 8 * 1024], BF16)
    st_d = sb("st_d", [128, 8 * 1024], BF16)
    st_f = sb("st_f", [128, 2 * 1024], BF16)
    f_a = sb("f_a", [128, T], F32)
    f_b = sb("f_b", [128, T], F32)
    iv_bf = sb("iv_bf", [128, T], BF16)
    miv_bf = sb("miv_bf", [128, T], BF16)
    xh_sb = st_d
    bcol = sb("bcol", [128, 8], BF16)
    mur_sb = sb("mur_sb", [128, 8], F32)
    muk_sb = sb("muk_sb", [128, 8], F32)
    conv_sb = sb("conv_sb", [128, 96], F32)
    nw_sb = sb("nw_sb", [128, 1], F32)
    iden_bf = sb("iden_bf", [128, 128], BF16)
    iden_f = sb("iden_f", [128, 128], F32)
    cum_sb = sb("cum_sb", [128, 128], F32)
    rcum_sb = sb("rcum_sb", [128, 128], F32)
    selA_sb = sb("selA_sb", [128, 128], F32)
    selB_sb = sb("selB_sb", [128, 128], F32)
    selj_bf = sb("selj_bf", [16, 1024], BF16)
    selj_f = sb("selj_f", [16, 1024], F32)
    seg2_bf = sb("seg2_bf", [128, 2], BF16)
    sel2_bf = sb("sel2_bf", [2, 128], BF16)
    m_strict_bf = sb("m_strict_bf", [128, 64], BF16)
    mb_incl_f = sb("mb_incl_f", [128, 64], BF16)
    mbT_strict_f = sb("mbT_strict_f", [128, 64], BF16)
    bdi_bf = sb("bdi_bf", [128, 64], BF16)
    ones_r_bf = sb("ones_r_bf", [1, 128], BF16)
    ones_r_f = sb("ones_r_f", [1, 128], F32)
    ones_c_bf = sb("ones_c_bf", [128, 1], BF16)
    ones_all_bf = sb("ones_all_bf", [128, 128], BF16)
    dtb_sb = sb("dtb_sb", [1, 16], F32)
    alog_sb = sb("alog_sb", [1, 16], F32)
    cpool = sb("cpool", [128, 4], F32)
    wba_sb = sb("wba_sb", [128, 8 * 32], BF16)
    gfm_sb = sb("gfm_sb", [16, T], F32)


    PSA = psum("psa", [128, T], F32)
    PSB = psum("psb", [128, T], F32)

    sch = Sched(nc)
    for s in ("dg", "dx", "t", "v", "s", "g", "st"):
        sch.sem(s)

    def dma(engine, dst, src, sem="dg", waits=()):
        def fn(eng):
            return eng.dma_start(dst, src).then_inc(sch.sems[sem], 16)
        sch.count[sem] += 16
        sch.ops[engine].append((fn, tuple(waits), None, 0))
        return (sem, sch.count[sem])

    tok_x = dma("sync", x_sb[:, 0:8 * T].rearrange("p (c t) -> p c t", c=8),
                ap["x"].rearrange("(c p) t -> p c t", p=128), sem="dx")
    for dst, src in [
        (mur_sb[:], ap["mu_r"]), (muk_sb[:], ap["mu_k"]),
        (conv_sb[:], ap["conv"]), (nw_sb[:], ap["nw"]),
        (iden_bf[:], ap["iden"]), (iden_f[:], ap["iden"]),
        (cum_sb[:], ap["cum"]), (rcum_sb[:], ap["rcum"]),
        (selA_sb[:], ap["selA"]), (selB_sb[:], ap["selB"]),
        (selj_bf[:], ap["selj"]), (selj_f[:], ap["selj"]),
        (seg2_bf[:], ap["seg2"]), (sel2_bf[:], ap["sel2"]),
        (wba_sb[:, 0:256].rearrange("p (e c) -> p e c", e=8),
         ap["w_ba"].rearrange("(e p) c -> p e c", p=128)),
        (m_strict_bf[:], ap["m_strict"]),
        (bdi_bf[:], ap["bdi"]),
        (mb_incl_f[:], ap["mb_incl"]), (mbT_strict_f[:], ap["mbT_strict"]),
        (ones_r_bf[:], ap["ones_r"]), (ones_r_f[:], ap["ones_r"]),
        (ones_c_bf[:], ap["ones_c"]), (ones_all_bf[:], ap["ones_all"]),
        (dtb_sb[:], ap["dtb"]), (alog_sb[:], ap["alog"]),
    ]:
        tok_c = dma("gpsimd", dst, src)
    def memsets(e):
        e.memset(cpool[:, 0:1], EPS_LN)
        e.memset(cpool[:, 1:2], EPS_L2)
        e.memset(cpool[:, 2:3], 1.0)
        return e.memset(cpool[:, 3:4], float(np.log(0.125)))
    sch.add("gpsimd", memsets, inc="g")
    sch.emit("p0_load")

    wbuf_tok = {0: None, 1: None}

    def load_w(which, key, cols):
        dst = wbuf[which][:, 0:8 * cols].rearrange("p (e c) -> p e c", e=8)
        src = ap[key].rearrange("(e p) c -> p e c", p=128)
        waits = [wbuf_tok[which]] if wbuf_tok[which] is not None else []
        return dma("gpsimd", dst, src, waits=waits)

    def wview(which, cols):
        return lambda e: wbuf[which][:, e * cols:(e + 1) * cols]

    def emit_ln(src_cols, xhat_to_half):
        for ci in range(8):
            xc = x_sb[:, ci * T:(ci + 1) * T]
            xb = st_f[:, 0:T]
            sq = st_c[:, 0:T]
            tv = sch.add("vector", lambda e, xc=xc, xb=xb: e.tensor_copy(xb, xc),
                         waits=[sch.now("t")] if ci else [tok_x], inc="v")
            ts = sch.add("scalar", lambda e, xb=xb, sq=sq:
                         e.activation(sq, xb, AF.Square),
                         waits=[tv], inc="s")
            def mm_stats(e, ci=ci, xb=xb, sq=sq):
                last = None
                for w in range(4):
                    cols = slice(w * 512, (w + 1) * 512)
                    last = e.matmul(PSA[:, cols], ones_all_bf[:], xb[:, cols],
                                    start=(ci == 0), stop=(ci == 7),
                                    skip_group_check=True)
                    last = e.matmul(PSB[:, cols], ones_all_bf[:],
                                    sq[:, cols], start=(ci == 0),
                                    stop=(ci == 7), skip_group_check=True)
                return last
            tt = sch.add("tensor", mm_stats, waits=[ts, tv], inc="t")
        tv = sch.add("vector", lambda e: e.tensor_scalar_mul(
            f_a[:], PSA[:, 0:T], 1.0 / D), waits=[tt], inc="v")
        tv = sch.add("vector", lambda e: e.tensor_mul(f_b[:], f_a[:], f_a[:]),
                     waits=[tv], inc="v")
        tv = sch.add("vector", lambda e: e.scalar_tensor_tensor(
            f_b[:], PSB[:, 0:T], 1.0 / D, f_b[:], ALU.mult, ALU.subtract),
            waits=[tv], inc="v")
        ts = sch.add("scalar", lambda e: e.activation(
            f_b[:], f_b[:], AF.Ln, bias=cpool[:, 0:1]), waits=[tv], inc="s")
        ts = sch.add("scalar", lambda e: e.activation(
            f_b[:], f_b[:], AF.Exp, scale=-0.5), waits=[ts], inc="s")
        tv = sch.add("vector", lambda e: e.tensor_copy(iv_bf[:], f_b[:]),
                     waits=[ts], inc="v")
        tv = sch.add("vector", lambda e: e.tensor_mul(miv_bf[:], f_a[:],
                                                      f_b[:]),
                     waits=[tv], inc="v")
        return tv

    def emit_ffn(scale, tag):
        tok_ln = emit_ln(None, None)
        for hv in range(2):
            hc = slice(hv * 1024, hv * 1024 + 1024)

            tprev = tok_ln
            for ci in range(8):
                xc = x_sb[:, ci * T + hv * 1024: ci * T + hv * 1024 + 1024]
                xo = xh_sb[:, ci * 1024:(ci + 1) * 1024]
                t1 = f_a[:, 0:1024]
                tv = sch.add("vector", lambda e, xc=xc, t1=t1:
                             e.tensor_mul(t1, xc, iv_bf[:, hc]),
                             waits=[tprev], inc="v")
                tprev = sch.add("vector", lambda e, t1=t1, xo=xo:
                                e.tensor_sub(xo, t1, miv_bf[:, hc]),
                                waits=[tv], inc="v")
            tok_xh = tprev

            dcol = f_b[:, 0:1024]
            for ci in range(8):
                xo = xh_sb[:, ci * 1024:(ci + 1) * 1024]
                xr = st_a[:, ci * 1024:(ci + 1) * 1024]
                xk = st_b[:, ci * 1024:(ci + 1) * 1024]
                mr = mur_sb[:, ci:ci + 1]
                mk = muk_sb[:, ci:ci + 1]
                bc = bcol[:, ci:ci + 1]
                tv = sch.add("vector", lambda e, xo=xo, d=dcol:
                             e.tensor_sub(d[:, 1:1024], xo[:, 1:1024],
                                          xo[:, 0:1023]),
                             waits=[tok_xh], inc="v")
                if hv == 0:
                    tv = sch.add("vector", lambda e, xo=xo, d=dcol:
                                 e.tensor_copy(d[:, 0:1], xo[:, 0:1]),
                                 waits=[tv], inc="v")
                else:
                    tv = sch.add("vector", lambda e, xo=xo, d=dcol, bc=bc:
                                 e.tensor_sub(d[:, 0:1], xo[:, 0:1], bc),
                                 waits=[tv], inc="v")
                tv = sch.add("vector", lambda e, d=dcol, xo=xo, xr=xr, mr=mr:
                             e.scalar_tensor_tensor(
                                 xr[:, 1:1024], d[:, 1:1024], mr,
                                 xo[:, 0:1023], ALU.mult, ALU.add),
                             waits=[tv], inc="v")
                if hv == 0:
                    tv = sch.add("vector", lambda e, d=dcol, xr=xr, mr=mr:
                                 e.tensor_scalar_mul(xr[:, 0:1], d[:, 0:1], mr),
                                 waits=[tv], inc="v")
                else:
                    tv = sch.add("vector", lambda e, d=dcol, xr=xr, mr=mr, bc=bc:
                                 e.scalar_tensor_tensor(
                                     xr[:, 0:1], d[:, 0:1], mr, bc,
                                     ALU.mult, ALU.add),
                                 waits=[tv], inc="v")
                tv = sch.add("vector", lambda e, d=dcol, xo=xo, xk=xk, mk=mk:
                             e.scalar_tensor_tensor(
                                 xk[:, 1:1024], d[:, 1:1024], mk,
                                 xo[:, 0:1023], ALU.mult, ALU.add),
                             waits=[tv], inc="v")
                if hv == 0:
                    tv = sch.add("vector", lambda e, d=dcol, xk=xk, mk=mk:
                                 e.tensor_scalar_mul(xk[:, 0:1], d[:, 0:1], mk),
                                 waits=[tv], inc="v")
                else:
                    tv = sch.add("vector", lambda e, d=dcol, xk=xk, mk=mk, bc=bc:
                                 e.scalar_tensor_tensor(
                                     xk[:, 0:1], d[:, 0:1], mk, bc,
                                     ALU.mult, ALU.add),
                                 waits=[tv], inc="v")
                if hv == 0:
                    tv = sch.add("vector", lambda e, xo=xo, bc=bc:
                                 e.tensor_copy(bc, xo[:, 1023:1024]),
                                 waits=[tv], inc="v")
            tok_mix = tv

            tok_w = load_w(0, "wkT", 1024)
            wk = wview(0, 1024)
            tok_g = _gemm(sch, PSA, PSB, wk, st_b, 8, tok_w, tok_mix,
                          consumer=_mk_relu2(sch, st_c))
            wbuf_tok[0] = tok_g
            tok_w = load_w(1, "wrT", 1024)
            wr = wview(1, 1024)
            tok_g2 = _gemm(sch, PSA, PSB, wr, st_a, 8, tok_w, tok_mix,
                           consumer=_mk_sigmoid(sch, st_d))
            wbuf_tok[1] = tok_g2
            tok_w = load_w(0, "wvT", 1024)
            wv = wview(0, 1024)

            def mk_gate(hv=hv, scale=scale):
                def consume(dci, ps, tok_mm):
                    g1 = f_a[:, 0:1024]
                    xc = x_sb[:, dci * T + hv * 1024: dci * T + hv * 1024 + 1024]
                    r = st_d[:, dci * 1024:(dci + 1) * 1024]
                    tv = sch.add("vector", lambda e, ps=ps, g1=g1, r=r:
                                 e.tensor_mul(g1, ps, r),
                                 waits=[tok_mm], inc="v")
                    tv = sch.add("vector", lambda e, g1=g1, xc=xc:
                                 e.scalar_tensor_tensor(xc, g1, scale, xc,
                                                        ALU.mult, ALU.add),
                                 waits=[tv], inc="v")
                    return tv
                return consume
            tok_g3 = _gemm(sch, PSA, PSB, wv, st_c, 8, tok_w, tok_g2,
                           consumer=mk_gate())
            wbuf_tok[0] = tok_g3
        return tok_g3

    def _mk_sigmoid(sch, dst):
        def consume(dci, ps, tok_mm):
            o = dst[:, dci * 1024:(dci + 1) * 1024]
            return sch.add("scalar", lambda e, ps=ps, o=o:
                           e.activation(o, ps, AF.Sigmoid),
                           waits=[tok_mm], inc="s")
        return consume

    def _mk_relu2(sch, dst):
        vtok = [None, None]

        def consume(dci, ps, tok_mm):
            o = dst[:, dci * 1024:(dci + 1) * 1024]
            t1 = f_b[:, (dci % 2) * 1024:(dci % 2) * 1024 + 1024]
            waits = [tok_mm]
            if vtok[dci % 2] is not None:
                waits.append(vtok[dci % 2])
            ts = sch.add("scalar", lambda e, ps=ps, t1=t1:
                         e.activation(t1, ps, AF.Relu),
                         waits=waits, inc="s")
            tv = sch.add("vector", lambda e, t1=t1, o=o:
                         e.tensor_mul(o, t1, t1), waits=[ts], inc="v")
            vtok[dci % 2] = tv
            return tv
        return consume

    gemm_prev = [None, None]

    def _gemm(sch, PS0, PS1, wv, rhs, nd, tok_w, tok_rhs, consumer):
        tok_c = None
        for dci in range(nd):
            ps = (PS0 if dci % 2 == 0 else PS1)[:, 0:1024]
            waits = [tok_w, tok_rhs]
            if gemm_prev[dci % 2] is not None:
                waits.append(gemm_prev[dci % 2])

            def mm(e, dci=dci, ps=ps):
                last = None
                for ec in range(8):
                    for w in range(2):
                        cols = slice(w * 512, (w + 1) * 512)
                        last = e.matmul(
                            ps[:, cols], wv(ec)[:, dci * 128:(dci + 1) * 128],
                            rhs[:, ec * 1024 + w * 512:
                                ec * 1024 + w * 512 + 512],
                            start=(ec == 0), stop=(ec == 7))
                return last
            tok_mm = sch.add("tensor", mm, waits=waits, inc="t")
            tok_c = consumer(dci, ps, tok_mm)
            gemm_prev[dci % 2] = tok_c
        return tok_c


    def bar():
        return [sch.now("t"), sch.now("v"), sch.now("s"), sch.now("dg")]

    def emit_attn():
        k_st = st_c[:, 0:2048]
        q_st = st_c[:, 2048:4096]
        v_st = st_c[:, 4096:6144]
        z_st = st_c[:, 6144:8192]
        wo_buf = wbuf[1][:, 5120:6144]
        att_st = st_d[:, 2048:4096]
        qg_st = wbuf[0][:, 1024:3072]
        uv_st = wbuf[0][:, 3072:5120]
        wt_st = wbuf[0][:, 5120:7168]
        h_st = wbuf[1][:, 1024:3072]
        pl_st = wbuf[1][:, 3072:5120]
        conv_acc = iv_bf[:, 0:2048]
        conv_pre = st_d[:, 6144:8192]
        x1g = iv_bf[:, 0:256]
        m1g = iv_bf[:, 256:512]
        xp_g = [iv_bf[:, 512 + 256 * i:768 + 256 * i] for i in range(5)]
        mp_g = [iv_bf[:, 1792:2048], miv_bf[:, 0:256]]
        ktm_g = miv_bf[:, 256:512]
        vtm_g = miv_bf[:, 512:768]
        kh_g = miv_bf[:, 768:1024]
        zw_g = [miv_bf[:, 1024:1536], miv_bf[:, 1536:2048]]
        dmp_g = st_d[:, 4096:4352]
        dmm_g = st_d[:, 4352:4608]
        grep_c = st_d[:, 4608:4672]
        wfm_scr = st_d[:, 4672:4736]
        utm_scr = st_d[:, 4736:4800]
        sq_scr = iv_bf[:, 0:2048]
        invn_v = st_d[0:2, 6144:8192]
        bfm = st_f[0:16, 0:2048]
        ba_f = f_a[0:32, 0:2048]
        bt_f = f_b[:, 0:512]
        btm_f = f_b[:, 512:768]
        gsc_f = f_b[:, 768:1024]
        gtm_f = f_b[:, 1024:1280]
        rtm_f = f_b[:, 1280:1536]
        dtbr_f = f_b[:, 1536:1552]
        nega_f = f_b[:, 1552:1568]
        dup_sb = f_b[:, 0:768]
        dup2_sb = f_b[:, 768:1792]
        smast = f_b[:, 1792:1856]
        sbf3 = [miv_bf[:, 1024:1088], miv_bf[:, 1088:1152],
                miv_bf[:, 1152:1216]]
        utm2 = [miv_bf[:, 1216:1280], miv_bf[:, 1280:1344]]

        tok_ln = emit_ln(None, None)
        def x2h(ec):
            return (st_a if ec < 4 else st_b)[:, (ec % 4) * 2048:
                                              (ec % 4) * 2048 + 2048]
        tprev = tok_ln
        for ci in range(8):
            xc = x_sb[:, ci * T:(ci + 1) * T]
            xo = x2h(ci)
            tv = sch.add("vector", lambda e, xc=xc: e.tensor_mul(
                f_a[:, 0:T], xc, iv_bf[:, 0:T]), waits=[tprev], inc="v")
            tprev = sch.add("vector", lambda e, xo=xo: e.tensor_sub(
                xo, f_a[:, 0:T], miv_bf[:, 0:T]), waits=[tv], inc="v")
        tok_x2h = tprev

        def mm_ba(e):
            last = None
            for ec in range(8):
                for w in range(4):
                    cols = slice(w * 512, (w + 1) * 512)
                    last = e.matmul(PSA[0:32, cols],
                                    wba_sb[:, ec * 32:(ec + 1) * 32],
                                    x2h(ec)[:, cols] if False else
                                    x2h(ec)[:, cols],
                                    start=(ec == 0), stop=(ec == 7),
                                    skip_group_check=True)
            return last
        tt = sch.add("tensor", mm_ba, waits=[tok_x2h], inc="t")
        tv = sch.add("vector", lambda e: e.tensor_copy(ba_f, PSA[0:32, 0:2048]),
                     waits=[tt], inc="v")
        def mm_bt(e):
            last = None
            for tau in range(16):
                last = e.transpose(PSB[0:128, 32 * tau:32 * tau + 32],
                                   ba_f[0:32, 128 * tau:128 * tau + 128],
                                   iden_f[0:32, 0:32])
            return last
        tt = sch.add("tensor", mm_bt, waits=[tv], inc="t")
        tv = sch.add("vector", lambda e: e.tensor_copy(bt_f, PSB[:, 0:512]),
                     waits=[tt], inc="v")
        def mm_reps(e):
            e.matmul(PSB[:, 512:528], ones_r_f[:], dtb_sb[:],
                     start=True, stop=True)
            return e.matmul(PSB[:, 528:544], ones_r_f[:], alog_sb[:],
                            start=True, stop=True)
        tt = sch.add("tensor", mm_reps, waits=[tv], inc="t")
        tv = sch.add("vector", lambda e: e.tensor_copy(dtbr_f, PSB[:, 512:528]),
                     waits=[tt], inc="v")
        tv = sch.add("vector", lambda e: e.tensor_copy(nega_f, PSB[:, 528:544]),
                     waits=[tv], inc="v")
        tv = sch.add("vector", lambda e: e.tensor_scalar_max(
            nega_f, nega_f, -2.3), waits=[tv], inc="v")
        ts = sch.add("scalar", lambda e: e.activation(nega_f, nega_f, AF.Exp),
                     waits=[tv], inc="s")
        tv = sch.add("vector", lambda e: e.tensor_scalar_mul(
            nega_f, nega_f, -1.0), waits=[ts], inc="v")
        tok_na = tv
        bt3 = bass.AP(f_b, 0, [[2048, 128], [32, 16], [1, 16]])
        btm3 = bass.AP(f_b, 512, [[2048, 128], [16, 16], [1, 16]])
        at3 = bass.AP(f_b, 16, [[2048, 128], [32, 16], [1, 16]])
        gs3 = bass.AP(f_b, 768, [[2048, 128], [16, 16], [1, 16]])
        ts = sch.add("scalar", lambda e: e.activation(btm3, bt3, AF.Sigmoid),
                     waits=[sch.now("v")], inc="s")
        tok_btm = ts
        dtbr_b = bass.AP(f_b, 1536, [[2048, 128], [0, 16], [1, 16]])
        nega_b = bass.AP(f_b, 1552, [[2048, 128], [0, 16], [1, 16]])
        tv = sch.add("vector", lambda e: e.tensor_tensor(
            gs3, at3, dtbr_b, ALU.add), waits=[tok_na], inc="v")
        ts = sch.add("scalar", lambda e: e.activation(gs3, gs3, AF.Exp),
                     waits=[tv], inc="s")
        ts = sch.add("scalar", lambda e: e.activation(
            gs3, gs3, AF.Ln, bias=cpool[:, 2:3]), waits=[ts], inc="s")
        tv = sch.add("vector", lambda e: e.tensor_tensor(
            gs3, gs3, nega_b, ALU.mult), waits=[ts], inc="v")
        tok_g = tv
        def mm_cum(e):
            last = None
            for tau in range(16):
                last = e.matmul(PSB[:, 1024 + 16 * tau:1040 + 16 * tau],
                                cum_sb[:], gsc_f[:, 16 * tau:16 * tau + 16],
                                start=True, stop=True)
            for tau in range(16):
                last = e.matmul(PSB[:, 1280 + 16 * tau:1296 + 16 * tau],
                                rcum_sb[:], gsc_f[:, 16 * tau:16 * tau + 16],
                                start=True, stop=True)
            return last
        tt = sch.add("tensor", mm_cum, waits=[tok_g], inc="t")
        tv = sch.add("vector", lambda e: e.tensor_copy(gtm_f, PSB[:, 1024:1280]),
                     waits=[tt], inc="v")
        tv = sch.add("vector", lambda e: e.tensor_copy(rtm_f, PSB[:, 1280:1536]),
                     waits=[tv], inc="v")
        tok_cum = tv
        def mm_gfm(e):
            last = None
            for tau in range(16):
                last = e.transpose(PSB[0:16, 128 * tau:128 * tau + 128],
                                   gtm_f[:, 16 * tau:16 * tau + 16],
                                   iden_f[:])
            return last
        tt = sch.add("tensor", mm_gfm, waits=[tok_cum], inc="t")
        tv = sch.add("vector", lambda e: e.tensor_copy(
            gfm_sb[:], PSB[0:16, 0:2048]), waits=[tt], inc="v")
        def mm_bfm(e):
            last = None
            for tau in range(16):
                last = e.transpose(PSB[0:16, 128 * tau:128 * tau + 128],
                                   btm_f[:, 16 * tau:16 * tau + 16],
                                   iden_f[:])
            return last
        tt = sch.add("tensor", mm_bfm, waits=[tv], inc="t")
        tv = sch.add("vector", lambda e: e.tensor_copy(
            bfm[:], PSB[0:16, 0:2048]), waits=[tt], inc="v")
        tok_fm = tv
        def mm_dup(e):
            last = None
            for n in range(NCH):
                tau, half = n // 2, (n % 2) * 64
                rows = slice(half, half + 64)
                gsl = bass.AP(f_b, 1024 + 16 * tau + half * 2048,
                              [[2048, 64], [2, 8]])
                bsl = bass.AP(f_b, 512 + 16 * tau + half * 2048,
                              [[2048, 64], [2, 8]])
                rsl = bass.AP(f_b, 1280 + 16 * tau + half * 2048,
                              [[2048, 64], [2, 8]])
                gsl1 = bass.AP(f_b, 1024 + 16 * tau + half * 2048 + 1,
                               [[2048, 64], [2, 8]])
                bsl1 = bass.AP(f_b, 512 + 16 * tau + half * 2048 + 1,
                               [[2048, 64], [2, 8]])
                rsl1 = bass.AP(f_b, 1280 + 16 * tau + half * 2048 + 1,
                               [[2048, 64], [2, 8]])
                pc = 1280 + 24 * n
                sa = selA_sb[half:half + 64, :]
                sb_ = selB_sb[half:half + 64, :]
                e.matmul(PSB[:, pc:pc + 8], sa, gsl,
                         start=True, stop=False, skip_group_check=True)
                e.matmul(PSB[:, pc:pc + 8], sb_, gsl1,
                         start=False, stop=True, skip_group_check=True)
                e.matmul(PSB[:, pc + 8:pc + 16], sa, bsl,
                         start=True, stop=False, skip_group_check=True)
                e.matmul(PSB[:, pc + 8:pc + 16], sb_, bsl1,
                         start=False, stop=True, skip_group_check=True)
                e.matmul(PSB[:, pc + 16:pc + 24], sa, rsl,
                         start=True, stop=False, skip_group_check=True)
                last = e.matmul(PSB[:, pc + 16:pc + 24], sb_, rsl1,
                                start=False, stop=True, skip_group_check=True)
            return last
        tt = sch.add("tensor", mm_dup, waits=[tok_fm], inc="t")
        tv = sch.add("vector", lambda e: e.tensor_copy(
            dup_sb, PSB[:, 1280:2048]), waits=[tt], inc="v")
        tok_dup = tv
        d_G = bass.AP(f_b, 0, [[2048, 128], [24, 32], [1, 8]])
        d_b = bass.AP(f_b, 8, [[2048, 128], [24, 32], [1, 8]])
        d_R = bass.AP(f_b, 16, [[2048, 128], [24, 32], [1, 8]])
        d2_G = bass.AP(f_b, 768, [[2048, 128], [32, 32], [1, 8]])
        d2_r = bass.AP(f_b, 768 + 8, [[2048, 128], [32, 32], [1, 8]])
        d2_bG = bass.AP(f_b, 768 + 16, [[2048, 128], [32, 32], [1, 8]])
        d2_Ge = bass.AP(f_b, 768 + 24, [[2048, 128], [32, 32], [1, 8]])
        ts = sch.add("scalar", lambda e: e.activation(d2_G, d_G, AF.Exp),
                     waits=[tok_dup], inc="s")
        ts = sch.add("scalar", lambda e: e.activation(d2_r, d_R, AF.Exp),
                     waits=[ts], inc="s")
        tv = sch.add("vector", lambda e: e.tensor_tensor(
            d2_bG, d_b, d2_G, ALU.mult), waits=[ts], inc="v")
        tv = sch.add("vector", lambda e: e.tensor_tensor(
            d2_Ge, d2_G, d2_r, ALU.mult), waits=[tv], inc="v")
        tok_gmach = tv


        def load_win(which, widx):
            dst = wbuf[which][:, 0:1024]
            s2 = ap["w_in2"][widx]
            waits = [wbuf_tok[which]] if wbuf_tok[which] is not None else []
            return dma("gpsimd", dst, s2, waits=waits)

        def mm_qkvz_ech(e, which, ec, h):
            last = None
            wsl = wbuf[which][:, ec * 128:(ec + 1) * 128]
            for w in range(2):
                cols = slice(h * 1024 + w * 512, h * 1024 + w * 512 + 512)
                last = e.matmul(PSA[:, cols], wsl, x2h(ec)[:, cols],
                                start=(ec == 0), stop=(ec == 7),
                                skip_group_check=True)
            return last

        def emit_conv(cidx, dst, pre_toks):
            wj = lambda j: conv_sb[:, cidx * 4 + j:cidx * 4 + j + 1]
            tv = sch.add("vector", lambda e: e.tensor_scalar_mul(
                conv_acc[:, 0:1024], conv_pre[:, 0:1024], wj(3)),
                waits=list(pre_toks), inc="v")
            tv = sch.add("vector", lambda e: e.tensor_scalar_mul(
                conv_acc[:, 1024:2048], conv_pre[:, 1024:2048], wj(3)),
                waits=[tv], inc="v")
            for j, sh in ((2, 1), (1, 2), (0, 3)):
                tv = sch.add("vector", lambda e, j=j, sh=sh:
                             e.scalar_tensor_tensor(
                                 conv_acc[:, sh:1024], conv_pre[:, 0:1024 - sh],
                                 wj(j), conv_acc[:, sh:1024],
                                 ALU.mult, ALU.add), waits=[tv], inc="v")
                tv = sch.add("vector", lambda e, j=j, sh=sh:
                             e.scalar_tensor_tensor(
                                 conv_acc[:, 1024:2048],
                                 conv_pre[:, 1024 - sh:2048 - sh],
                                 wj(j), conv_acc[:, 1024:2048],
                                 ALU.mult, ALU.add), waits=[tv], inc="v")
            ts = sch.add("scalar", lambda e: e.activation(
                dst[:, 0:1024], conv_acc[:, 0:1024], AF.Silu),
                waits=[tv], inc="s")
            ts = sch.add("scalar", lambda e: e.activation(
                dst[:, 1024:2048], conv_acc[:, 1024:2048], AF.Silu),
                waits=[ts], inc="s")
            return ts

        def emit_l2norm(dst, qmode, tok_silu, extra=()):
            ts = sch.add("scalar", lambda e: e.activation(
                sq_scr, dst, AF.Square), waits=[tok_silu] + list(extra),
                inc="s")
            def mm_seg(e):
                last = None
                for w in range(4):
                    cols = slice(w * 512, (w + 1) * 512)
                    last = e.matmul(PSA[0:2, cols], seg2_bf[:],
                                    sq_scr[:, cols], start=True, stop=True,
                                    skip_group_check=True)
                return last
            tt = sch.add("tensor", mm_seg, waits=[ts], inc="t")
            ts = sch.add("scalar", lambda e: e.activation(
                invn_v, PSA[0:2, 0:2048], AF.Ln, bias=cpool[0:2, 1:2]),
                waits=[tt], inc="s")
            if qmode:
                ts = sch.add("scalar", lambda e: e.activation(
                    invn_v, invn_v, AF.Exp, scale=-0.5, bias=cpool[0:2, 3:4]),
                    waits=[ts], inc="s")
            else:
                ts = sch.add("scalar", lambda e: e.activation(
                    invn_v, invn_v, AF.Exp, scale=-0.5), waits=[ts], inc="s")
            def mm_rep2(e):
                last = None
                for w in range(4):
                    cols = slice(w * 512, (w + 1) * 512)
                    last = e.matmul(PSA[:, cols], sel2_bf[:],
                                    invn_v[:, cols], start=True, stop=True,
                                    skip_group_check=True)
                return last
            tt = sch.add("tensor", mm_rep2, waits=[ts], inc="t")
            tv = sch.add("vector", lambda e: e.tensor_mul(
                dst, dst, PSA[:, 0:2048]), waits=[tt], inc="v")
            return tv

        def emit_pair(ci, tok_gmach, prev):
            subs = [("k", 4 * ci + 0, k_st, 8 + ci),
                    ("q", 4 * ci + 1, q_st, ci),
                    ("v", 4 * ci + 2, v_st, 16 + ci),
                    ("z", 4 * ci + 3, z_st, None)]
            tok_cons = bar()
            pend = []
            evac_h = [list(tok_cons), list(tok_cons)]
            z_silu_h = []
            for si, (nm, off, dst, cidx) in enumerate(subs):
                tok_w = load_win(si % 2, off)
                pre_toks = []
                for h in range(2):
                    for ec in range(8):
                        w_ = ([tok_w] + list(evac_h[h])) if ec == 0 else []
                        tok_mm = sch.add("tensor",
                                         lambda e, which=si % 2, ec=ec, h=h:
                                         mm_qkvz_ech(e, which, ec, h),
                                         waits=w_, inc="t")
                        if prev is not None and ec % 2 == 1:
                            next(prev["seq"], None)
                    if nm == "z":
                        z_silu_h.append(tok_mm)
                    else:
                        tpre = sch.add("scalar",
                                       lambda e, h=h: e.copy(
                                           conv_pre[:, h * 1024:h * 1024 + 1024],
                                           PSA[:, h * 1024:h * 1024 + 1024]),
                                       waits=[tok_mm], inc="s")
                    evac_h[h] = [tpre]
                    pre_toks.append(tpre)
                wbuf_tok[si % 2] = tok_mm
                if nm != "z":
                    ts = emit_conv(cidx, dst, pre_toks)
                if nm in ("k", "q"):
                    pend.append((dst, nm == "q", ts))
            if prev is not None:
                for _ in range(3):
                    next(prev["seq"], None)
                prev["ygate"]([])
            zs_toks = []
            for h in range(2):
                zw_waits = [z_silu_h[h]]
                if prev is not None:
                    zw_waits.append(prev["y_done"])
                zs_toks.append(sch.add("scalar", lambda e, h=h: e.activation(
                    z_st[:, h * 1024:h * 1024 + 1024],
                    PSA[:, h * 1024:h * 1024 + 1024], AF.Silu),
                    waits=zw_waits, inc="s"))
            tz = sch.add("vector", lambda e: e.tensor_scalar_mul(
                z_st, z_st, nw_sb[:]), waits=zs_toks, inc="v")
            tok_cons = [tz, sch.now("s")]
            last_l2 = None
            if prev is not None and prev.get("segy_tok") is not None:
                last_l2 = prev["segy_tok"]
            for p_ in pend:
                ex = list(tok_cons) + ([last_l2] if last_l2 is not None else [])
                last_l2 = emit_l2norm(*p_, extra=ex)
            tok_kqvz = bar()

            def mm_grow(e, ci=ci):
                last = None
                for w in range(4):
                    cols = slice(w * 512, (w + 1) * 512)
                    last = e.matmul(PSA[:, cols],
                                    selj_f[:, ci * 128:(ci + 1) * 128],
                                    gfm_sb[:, cols], start=True, stop=True,
                                    skip_group_check=True)
                return last
            tok_gr = sch.add("tensor", mm_grow,
                             waits=tok_kqvz + [tok_gmach], inc="t")
            qgam = st_d[:, 6144:8192]
            ts = sch.add("scalar", lambda e: e.activation(
                qgam, PSA[:, 0:2048], AF.Exp), waits=[tok_gr], inc="s")
            gdup_b = bass.AP(f_b, ci, [[2048, 128], [24, 32], [0, 64]])
            tok_arg = sch.add("vector", lambda e: e.tensor_tensor(
                f_a[:], PSA[:, 0:2048], gdup_b, ALU.subtract),
                waits=[tok_gr, ts], inc="v")
            qg_waits = [ts, tok_arg]
            if prev is not None:
                qg_waits.append(prev["st"]["o_last"])
            tok_qg = sch.add("vector", lambda e: e.tensor_mul(
                qg_st, q_st, qgam), waits=qg_waits, inc="v")

            if prev is not None:
                prev["wo"]()
            xslt = [iv_bf[:, 0:1024], iv_bf[:, 1024:2048]]
            mslt = [miv_bf[:, 0:1024], miv_bf[:, 1024:2048]]
            zw_all = st_d[:, 0:2048]
            kh_all = st_d[:, 4096:5120]
            vtm_g = st_d[:, 5120:5376]
            ktm_g = st_d[:, 5376:5632]
            dmp_g = st_d[:, 5632:5888]
            dmm_g = st_d[:, 5888:6144]
            tok_h = tok_wt = t_ld = None
            for sg in range(2):
                sgc = 16 * sg
                sgbase = 1024 * sg
                tok_kh_l = []
                tok_zw_l = []
                tok_x1_l = []
                tok_m1_l = []
                for g in range(4):
                    cks = list(range(sgc + 4 * g, sgc + 4 * g + 4))
                    gbase = sgbase + 256 * g
                    lbase = 256 * g

                    def mm_kkqt(e, cks=cks):
                        last = None
                        for i, n in enumerate(cks):
                            rhs_a = bass.AP(st_c, n * 64,
                                            [[8192, 64], [2048, 2], [1, 64]])
                            rhs_b = bass.AP(st_c, 64 * 8192 + n * 64,
                                            [[8192, 64], [2048, 2], [1, 64]])
                            pc = 128 * i
                            e.matmul(PSA[0:64, pc:pc + 128],
                                     k_st[0:64, n * 64:(n + 1) * 64], rhs_a,
                                     start=True, stop=True,
                                     tile_position=(0, 0),
                                     skip_group_check=True)
                            last = e.matmul(PSA[64:128, pc:pc + 128],
                                            k_st[64:128, n * 64:(n + 1) * 64],
                                            rhs_b, start=True, stop=True,
                                            tile_position=(64, 64),
                                            skip_group_check=True)
                        return last
                    if sg == 0 and g == 0:
                        kk_waits = [tok_qg] + (prev["wo_pv"] if prev else [])
                        if prev is not None:
                            kk_waits.append(prev["y_done"])
                    elif g > 0:
                        kk_waits = [tok_att, tok_zw_l[-1]]
                    else:
                        kk_waits = [tok_h, tok_wt, t_ld]
                    tok_kk = sch.add("tensor", mm_kkqt, waits=kk_waits,
                                     inc="t")

                    def mm_brep(e, g=g, sgbase=sgbase, gbase=gbase):
                        return e.matmul(PSA[:, 512:768],
                                        selj_bf[:, ci * 128:(ci + 1) * 128],
                                        bfm[:, gbase:gbase + 256],
                                        start=True, stop=True,
                                        skip_group_check=True)
                    tok_br = sch.add("tensor", mm_brep, waits=[tok_kk],
                                     inc="t")

                    dsl = f_a[:, gbase:gbase + 256]
                    mbi_b = bass.AP(mb_incl_f, 0, [[64, 128], [0, 4], [1, 64]])
                    mbT_b = bass.AP(mbT_strict_f, 0,
                                    [[64, 128], [0, 4], [1, 64]])
                    tv0 = sch.add("vector", lambda e, d=dsl, m=mbi_b:
                                  e.scalar_tensor_tensor(
                                      dmp_g, d, 1.0, m, ALU.mult, ALU.min),
                                  waits=[tok_arg], inc="v")
                    tok_dmp = sch.add("scalar", lambda e: e.activation(
                        dmp_g, dmp_g, AF.Exp), waits=[tv0], inc="s")
                    tv0 = sch.add("vector", lambda e, d=dsl, m=mbT_b:
                                  e.scalar_tensor_tensor(
                                      dmm_g, d, 1.0, m, ALU.mult, ALU.max),
                                  waits=[tok_arg, tok_dmp], inc="v")
                    tok_dmm = sch.add("scalar", lambda e: e.activation(
                        dmm_g, dmm_g, AF.Exp, scale=-1.0),
                        waits=[tv0], inc="s")

                    kk3 = bass.AP(PSA, 0, [[2048, 128], [128, 4], [1, 64]])
                    kq3 = bass.AP(PSA, 64, [[2048, 128], [128, 4], [1, 64]])
                    br3 = bass.AP(PSA, 512, [[2048, 128], [64, 4], [1, 64]])
                    mstr_b = bass.AP(m_strict_bf, 0,
                                     [[64, 128], [0, 4], [1, 64]])
                    bdup_b = bass.AP(f_b, 24 * (sgc + 4 * g) + 8 + ci,
                                     [[2048, 128], [24, 4], [0, 64]])
                    bgdup_b = bass.AP(f_b, 768 + 32 * (sgc + 4 * g) + 16 + ci,
                                      [[2048, 128], [32, 4], [0, 64]])
                    ratdup_b = bass.AP(f_b, 768 + 32 * (sgc + 4 * g) + 8 + ci,
                                       [[2048, 128], [32, 4], [0, 64]])
                    x1sl = xslt[0][:, lbase:lbase + 256]
                    m1sl = mslt[0][:, lbase:lbase + 256]

                    tv = sch.add("vector", lambda e, a=kk3, o=x1sl:
                                 e.tensor_tensor(o, dmp_g, a, ALU.mult),
                                 waits=[tok_kk, tok_dmp], inc="v")
                    tv = sch.add("vector", lambda e, a=br3, o=x1sl:
                                 e.tensor_tensor(o, o, a, ALU.mult),
                                 waits=[tv, tok_br], inc="v")
                    tok_x1 = sch.add("vector", lambda e, a=mstr_b, o=x1sl:
                                     e.tensor_tensor(o, o, a, ALU.mult),
                                     waits=[tv], inc="v")
                    tok_x1_l.append(tok_x1)
                    tv = sch.add("vector", lambda e, a=kk3, o=m1sl:
                                 e.tensor_tensor(o, dmm_g, a, ALU.mult),
                                 waits=[tok_kk, tok_dmm], inc="v")
                    tok_m1 = sch.add("vector", lambda e, a=bdup_b, o=m1sl:
                                     e.tensor_tensor(o, o, a, ALU.mult),
                                     waits=[tv], inc="v")
                    tok_m1_l.append(tok_m1)
                    attsl = att_st[:, gbase:gbase + 256]
                    tok_att = sch.add("vector", lambda e, a=attsl, b=kq3:
                                      e.tensor_tensor(a, dmp_g, b, ALU.mult),
                                      waits=[tok_kk, tok_dmp], inc="v")

                    def mm_trs(e, cks=cks):
                        last = None
                        for i, n in enumerate(cks):
                            for (srcst, base) in ((v_st, 1024), (k_st, 1280)):
                                e.matmul(
                                    PSA[0:64, base + 64 * i:base + 64 * i + 64],
                                    srcst[0:64, n * 64:(n + 1) * 64],
                                    iden_bf[0:64, 0:64], start=True,
                                    stop=True, tile_position=(0, 0),
                                    skip_group_check=True)
                                last = e.matmul(
                                    PSA[64:128, base + 64 * i:base + 64 * i + 64],
                                    srcst[64:128, n * 64:(n + 1) * 64],
                                    iden_bf[64:128, 64:128], start=True,
                                    stop=True, tile_position=(64, 64),
                                    skip_group_check=True)
                        return last
                    tok_tr = sch.add("tensor", mm_trs,
                                     waits=[tok_x1, tok_att], inc="t")
                    tok_vtm = sch.add("vector", lambda e: e.tensor_copy(
                        vtm_g, PSA[:, 1024:1280]), waits=[tok_tr], inc="v")
                    tok_ktm = sch.add("vector", lambda e: e.tensor_copy(
                        ktm_g, PSA[:, 1280:1536]), waits=[tok_tr], inc="v")

                    zwv = bass.AP(st_d, 128 * 4 * g,
                                  [[8192, 128], [128, 4], [1, 64]])
                    zwd = bass.AP(st_d, 128 * 4 * g + 64,
                                  [[8192, 128], [128, 4], [1, 64]])
                    vtm3 = bass.AP(st_d, 5120, [[8192, 128], [64, 4], [1, 64]])
                    ktm3 = bass.AP(st_d, 5376, [[8192, 128], [64, 4], [1, 64]])
                    tv = sch.add("vector", lambda e, a=zwv, b=vtm3, c=bdup_b:
                                 e.tensor_tensor(a, b, c, ALU.mult),
                                 waits=[tok_vtm], inc="v")
                    tok_zw = sch.add("vector",
                                     lambda e, a=zwd, b=ktm3, c=bgdup_b:
                                     e.tensor_tensor(a, b, c, ALU.mult),
                                     waits=[tv, tok_ktm], inc="v")
                    tok_zw_l.append(tok_zw)
                    khsl = kh_all[:, 64 * 4 * g:64 * 4 * g + 256]
                    tok_kh = sch.add("vector",
                                     lambda e, o=khsl, b=ktm3, c=ratdup_b:
                                     e.tensor_tensor(o, b, c, ALU.mult),
                                     waits=[tok_zw], inc="v")
                    tok_kh_l.append(tok_kh)

                def prod16(e, lh, rh, slot):
                    last = None
                    for i in range(16):
                        c = slice(64 * i, 64 * i + 64)
                        e.matmul(slot[0:64, c], lh[0:64, c], rh[0:64, c],
                                 start=True, stop=True, tile_position=(0, 0),
                                 skip_group_check=True)
                        last = e.matmul(slot[64:128, c], lh[64:128, c],
                                        rh[64:128, c], start=True, stop=True,
                                        tile_position=(64, 64),
                                        skip_group_check=True)
                    return last

                tok_bld = [tok_x1_l[-1], tok_m1_l[-1], tok_zw_l[-1]]
                ap_toks = []
                tok_zcur = tok_zw_l[-1]
                tokx = {0: tok_x1_l[-1]}
                tokm = {0: tok_m1_l[-1]}
                for w in range(2):
                    zps = PSB[:, 1024 * w:1024 * w + 1024]
                    def mmz(e, w=w, zps=zps, xs=xslt[0]):
                        last = None
                        for i in range(8):
                            ii = 8 * w + i
                            xc = slice(64 * ii, 64 * ii + 64)
                            zc = slice(128 * ii, 128 * ii + 128)
                            zc2 = slice(128 * i, 128 * i + 128)
                            e.matmul(zps[0:64, zc2], xs[0:64, xc],
                                     zw_all[0:64, zc], start=True, stop=True,
                                     tile_position=(0, 0),
                                     skip_group_check=True)
                            last = e.matmul(zps[64:128, zc2], xs[64:128, xc],
                                            zw_all[64:128, zc], start=True,
                                            stop=True,
                                            tile_position=(64, 64),
                                            skip_group_check=True)
                        return last
                    tm = sch.add("tensor", mmz,
                                 waits=tok_bld + ([] if w == 0 else
                                                  [ap_toks[-1]]), inc="t")
                    ap_toks.append(tm)
                    tok_zcur = sch.add("vector",
                                       lambda e, w=w, zps=zps:
                                       e.tensor_tensor(
                                           zw_all[:, 1024 * w:1024 * w + 1024],
                                           zw_all[:, 1024 * w:1024 * w + 1024],
                                           zps, ALU.subtract),
                                       waits=[tm, tok_zcur], inc="v")
                for lvl in range(5):
                    ssrc, sdst = lvl % 2, 1 - lvl % 2
                    pps = PSA[:, 1024:2048]
                    def mmp(e, ssrc=ssrc):
                        return prod16(e, xslt[ssrc], mslt[ssrc], pps)
                    w_p = [tokx[lvl], tokm[lvl]]
                    if lvl >= 1:
                        w_p.append(tok_evx)
                        w_p.append(tok_evm)
                    tok_pm = None
                    if lvl < 4:
                        tok_pm = sch.add("tensor", mmp, waits=w_p, inc="t")
                        tok_evm = sch.add("vector",
                                          lambda e, sdst=sdst: e.tensor_copy(
                                              mslt[sdst][:], pps[:, 0:1024]),
                                          waits=[tok_pm], inc="v")
                        tokm[lvl + 1] = tok_evm
                    pps2 = PSB[:, 0:512]
                    def mmp2(e, ssrc=ssrc):
                        return prod16(e, mslt[ssrc], xslt[ssrc],
                                      PSA[:, 0:1024])
                    w_p2 = list(w_p) + ([tok_pm] if tok_pm else [])
                    tok_px = sch.add("tensor", mmp2, waits=w_p2, inc="t")
                    tok_evx = sch.add("scalar",
                                      lambda e, sdst=sdst: e.copy(
                                          xslt[sdst][:], PSA[:, 0:1024]),
                                      waits=[tok_px], inc="s")
                    tokx[lvl + 1] = tok_evx
                    for w in range(2):
                        zps = PSB[:, 1024 * w:1024 * w + 1024]
                        def mmz(e, w=w, zps=zps, sdst=sdst):
                            last = None
                            xs = xslt[sdst]
                            for i in range(8):
                                ii = 8 * w + i
                                xc = slice(64 * ii, 64 * ii + 64)
                                zc = slice(128 * ii, 128 * ii + 128)
                                zc2 = slice(128 * i, 128 * i + 128)
                                e.matmul(zps[0:64, zc2], xs[0:64, xc],
                                         zw_all[0:64, zc], start=True,
                                         stop=False, tile_position=(0, 0),
                                         skip_group_check=True)
                                e.matmul(zps[0:64, zc2],
                                         iden_bf[0:64, 0:64],
                                         zw_all[0:64, zc], start=False,
                                         stop=True, tile_position=(0, 0),
                                         skip_group_check=True)
                                e.matmul(zps[64:128, zc2], xs[64:128, xc],
                                         zw_all[64:128, zc], start=True,
                                         stop=False, tile_position=(64, 64),
                                         skip_group_check=True)
                                last = e.matmul(zps[64:128, zc2],
                                                iden_bf[64:128, 64:128],
                                                zw_all[64:128, zc],
                                                start=False, stop=True,
                                                tile_position=(64, 64),
                                                skip_group_check=True)
                            return last
                        tm = sch.add("tensor", mmz,
                                     waits=[tokx[lvl + 1], tok_zcur], inc="t")
                        tok_zcur = sch.add("scalar",
                                           lambda e, w=w, zps=zps:
                                           e.copy(
                                               zw_all[:, 1024 * w:
                                                      1024 * w + 1024],
                                               zps),
                                           waits=[tm, tok_zcur], inc="s")
                tok_z = tok_zcur

                uv3o = bass.AP(wbuf[0], 3072 + sgbase,
                               [[8192, 128], [64, 16], [1, 64]])
                zv3 = bass.AP(st_d, 0, [[8192, 128], [128, 16], [1, 64]])
                tok_uv = sch.add("vector", lambda e, a=uv3o, b=zv3:
                                 e.tensor_copy(a, b), waits=[tok_z], inc="v")

                def mm_wf(e):
                    last = None
                    for i in range(16):
                        zd = zw_all[:, 128 * i + 64:128 * i + 128]
                        e.matmul(PSA[0:64, 64 * i:64 * i + 64], zd[0:64, :],
                                 iden_bf[0:64, 0:64], start=True, stop=True,
                                 tile_position=(0, 0), skip_group_check=True)
                        last = e.matmul(PSA[64:128, 64 * i:64 * i + 64],
                                        zd[64:128, :],
                                        iden_bf[64:128, 64:128], start=True,
                                        stop=True, tile_position=(64, 64),
                                        skip_group_check=True)
                    return last
                tok_wfm = sch.add("tensor", mm_wf, waits=[tok_z, tok_evx], inc="t")
                wfo = bass.AP(wbuf[0], 5120 + sgbase,
                              [[8192, 128], [64, 16], [1, 64]])
                tok_wt = sch.add("scalar", lambda e, a=wfo: e.copy(
                    a, PSA[:, 0:1024]), waits=[tok_wfm], inc="s")

                def mm_pl(e):
                    last = None
                    for i in range(16):
                        c = slice(64 * i, 64 * i + 64)
                        e.matmul(PSA[0:64, 1024 + 64 * i:1088 + 64 * i],
                                 kh_all[0:64, c],
                                 zw_all[0:64, 128 * i:128 * i + 64],
                                 start=True, stop=True, tile_position=(0, 0),
                                 skip_group_check=True)
                        last = e.matmul(PSA[64:128, 1024 + 64 * i:1088 + 64 * i],
                                        kh_all[64:128, c],
                                        zw_all[64:128, 128 * i:128 * i + 64],
                                        start=True, stop=True,
                                        tile_position=(64, 64),
                                        skip_group_check=True)
                    return last
                tok_plm = sch.add("tensor", mm_pl,
                                  waits=[tok_z] + tok_kh_l, inc="t")
                plo = bass.AP(wbuf[1], 3072 + sgbase,
                              [[8192, 128], [64, 16], [1, 64]])
                tok_pl = sch.add("vector", lambda e, a=plo: e.tensor_copy(
                    a, PSA[:, 1024:2048]), waits=[tok_plm], inc="v")

                def mm_h(e):
                    last = None
                    for i in range(16):
                        c = slice(64 * i, 64 * i + 64)
                        e.matmul(PSB[0:64, 64 * i:64 * i + 64],
                                 zw_all[0:64, 128 * i + 64:128 * i + 128],
                                 kh_all[0:64, c], start=True, stop=True,
                                 tile_position=(0, 0), skip_group_check=True)
                        last = e.matmul(PSB[64:128, 64 * i:64 * i + 64],
                                        zw_all[64:128, 128 * i + 64:128 * i + 128],
                                        kh_all[64:128, c], start=True,
                                        stop=True, tile_position=(64, 64),
                                        skip_group_check=True)
                    return last
                tok_hm = sch.add("tensor", mm_h,
                                 waits=[tok_z, tok_pl] + tok_kh_l, inc="t")
                tprev = tok_hm
                for i in range(16):
                    n = sgc + i
                    gecol = f_b[:, 768 + 32 * n + 24 + ci:
                                768 + 32 * n + 25 + ci]
                    tprev = sch.add("vector",
                                    lambda e, i=i, n=n, gecol=gecol:
                                    e.scalar_tensor_tensor(
                                        h_st[:, 64 * n:64 * n + 64],
                                        bdi_bf[:], gecol,
                                        PSB[:, 64 * i:64 * i + 64],
                                        ALU.mult, ALU.subtract),
                                    waits=[tprev], inc="v")
                tok_h = tprev
                t_ld = tok_evx

            stt = {}
            back = {"st": stt, "wo_pv": [], "seq": None,
                    "y_done": None, "ygate": None, "wo": None}
            ypair = st_d[:, 0:2048]

            def gen_seq(tok_h=tok_h, tok_wt=tok_wt, t_ld=t_ld, stt=stt):
                tv = sch.add("vector", lambda e: e.memset(smast[:], 0.0),
                             waits=[tok_h, tok_wt, t_ld], inc="v")
                tok_cast = sch.add("vector", lambda e: e.memset(sbf3[2][:], 0.0),
                                   waits=[tv], inc="v")
                tok_su_prev = None
                tok_ws = {}
                tok_u = {}
                tok_o = {}
                tok_yv = {}
                for n in range(NCH + 1):
                    if n >= 1:
                        yield
                    if n < NCH:
                        nc64 = slice(64 * n, 64 * n + 64)
                        sprev = sbf3[(n - 1) % 3]
                        snext = sbf3[n % 3]
                        def mm_su(e, nc64=nc64, sprev=sprev):
                            e.matmul(PSB[0:64, 512:576], h_st[0:64, nc64],
                                     sprev[0:64, :], start=True, stop=True,
                                     tile_position=(0, 0), skip_group_check=True)
                            return e.matmul(PSB[64:128, 512:576],
                                            h_st[64:128, nc64], sprev[64:128, :],
                                            start=True, stop=True,
                                            tile_position=(64, 64),
                                            skip_group_check=True)
                        w_su = [tok_cast]
                        if n >= 1:
                            w_su.append(tok_smadd)
                        tok_su = sch.add("tensor", mm_su, waits=w_su, inc="t")
                        tok_smadd = sch.add("vector", lambda e, n=n:
                                            e.tensor_tensor(
                                                smast[:], PSB[:, 512:576],
                                                wbuf[1][:, 3072 + 64 * n:
                                                        3072 + 64 * n + 64],
                                                ALU.add),
                                            waits=[tok_su], inc="v")
                        tok_cast_next = sch.add("vector",
                                                lambda e, snext=snext:
                                                e.tensor_copy(snext[:], smast[:]),
                                                waits=[tok_smadd], inc="v")
                        def mm_ws2(e, n=n, sprev=sprev):
                            wsl = wbuf[0][:, 5120 + 64 * n:5184 + 64 * n]
                            e.matmul(PSB[0:64, 1024:1088], wsl[0:64, :],
                                     sprev[0:64, :], start=True, stop=True,
                                     tile_position=(0, 0), skip_group_check=True)
                            return e.matmul(PSB[64:128, 1024:1088],
                                            wsl[64:128, :], sprev[64:128, :],
                                            start=True, stop=True,
                                            tile_position=(64, 64),
                                            skip_group_check=True)
                        w_ws = [tok_cast]
                        if n >= 1:
                            w_ws.append(tok_u[n - 1])
                        tok_ws[n] = sch.add("tensor", mm_ws2, waits=w_ws, inc="t")
                        tok_u[n] = sch.add("vector", lambda e, n=n, u=utm2[n % 2]:
                                           e.tensor_sub(
                                               u, wbuf[0][:, 3072 + 64 * n:
                                                          3072 + 64 * n + 64],
                                               PSB[:, 1024:1088]),
                                           waits=[tok_ws[n]], inc="v")
                        tok_cast = tok_cast_next
                    if n >= 1:
                        m = n - 1
                        mc64 = slice(64 * m, 64 * m + 64)
                        sm = sbf3[(m - 1) % 3]
                        def mm_o(e, mc64=mc64, sm=sm, u=utm2[(n - 1) % 2]):
                            e.matmul(PSB[0:64, 1536:1600], u[0:64, :],
                                     att_st[0:64, mc64], start=True, stop=False,
                                     tile_position=(0, 0), skip_group_check=True)
                            e.matmul(PSB[0:64, 1536:1600], sm[0:64, :],
                                     qg_st[0:64, mc64], start=False, stop=True,
                                     tile_position=(0, 0), skip_group_check=True)
                            e.matmul(PSB[64:128, 1536:1600], u[64:128, :],
                                     att_st[64:128, mc64], start=True, stop=False,
                                     tile_position=(64, 64),
                                     skip_group_check=True)
                            return e.matmul(PSB[64:128, 1536:1600], sm[64:128, :],
                                            qg_st[64:128, mc64], start=False,
                                            stop=True, tile_position=(64, 64),
                                            skip_group_check=True)
                        w_o = [tok_u[m]]
                        if m >= 1:
                            w_o.append(tok_yv[m - 1])
                        tok_o[m] = sch.add("tensor", mm_o, waits=w_o, inc="t")
                        tok_yv[m] = sch.add("scalar",
                                            lambda e, mc64=mc64: e.copy(
                                                ypair[:, mc64], PSB[:, 1536:1600]),
                                            waits=[tok_o[m]], inc="s")


                stt["o_last"] = tok_o[NCH - 1]
                stt["yv_last"] = tok_yv[NCH - 1]

            def do_ygate(extra=(), stt=stt):
                tok_sq = sch.add("scalar", lambda e: e.activation(
                    sq_scr, ypair, AF.Square),
                    waits=[stt["yv_last"], sch.now("v")] + list(extra), inc="s")
                def mm_seg_y(e):
                    last = None
                    for w in range(4):
                        cols = slice(w * 512, (w + 1) * 512)
                        last = e.matmul(PSB[0:2, cols], seg2_bf[:],
                                        sq_scr[:, cols], start=True, stop=True,
                                        skip_group_check=True)
                    return last
                tt = sch.add("tensor", mm_seg_y, waits=[tok_sq] + bar(), inc="t")
                back["segy_tok"] = tt
                ts = sch.add("scalar", lambda e: e.activation(
                    invn_v, PSB[0:2, 0:2048], AF.Ln, scale=1.0 / HD,
                    bias=cpool[0:2, 1:2]), waits=[tt], inc="s")
                ts = sch.add("scalar", lambda e: e.activation(
                    invn_v, invn_v, AF.Exp, scale=-0.5), waits=[ts], inc="s")
                def mm_rep_y(e):
                    last = None
                    for w in range(4):
                        cols = slice(w * 512, (w + 1) * 512)
                        last = e.matmul(PSB[:, cols], sel2_bf[:],
                                        invn_v[:, cols], start=True, stop=True,
                                        skip_group_check=True)
                    return last
                tt = sch.add("tensor", mm_rep_y, waits=[ts], inc="t")
                tv = sch.add("vector", lambda e: e.tensor_mul(
                    ypair, ypair, PSB[:, 0:2048]), waits=[tt], inc="v")
                tok_y = sch.add("vector", lambda e: e.tensor_mul(
                    ypair, ypair, z_st), waits=[tv], inc="v")


                back["y_done"] = tok_y
                back["tok_y"] = tok_y

            def do_wo():
                dst = wo_buf.rearrange("p (e c) -> p e c", e=1)
                s2 = ap["w_o"][128 * ci:128 * (ci + 1), :].rearrange(
                    "(e p) c -> p e c", p=128)
                tok_y = back["tok_y"]
                tok_wo = dma("gpsimd", dst, s2, waits=[tok_y])
                pv = [None, None]
                base0 = bar()
                for t8 in range(16):
                    dci, hv2 = t8 // 2, t8 % 2
                    ps = PSA[:, 1024 * (t8 % 2):1024 * (t8 % 2) + 1024]
                    def mm_wo(e, dci=dci, hv2=hv2, ps=ps):
                        last = None
                        wsl = wo_buf[:, dci * 128:dci * 128 + 128]
                        for w in range(2):
                            cols = slice(w * 512, (w + 1) * 512)
                            last = e.matmul(
                                ps[:, cols], wsl,
                                st_d[:, hv2 * 1024 + w * 512:
                                     hv2 * 1024 + w * 512 + 512],
                                start=True, stop=True,
                                skip_group_check=True)
                        return last
                    wts = [tok_wo, tok_y] + base0
                    if pv[t8 % 2] is not None:
                        wts.append(pv[t8 % 2])
                    tokm = sch.add("tensor", mm_wo, waits=wts, inc="t")
                    xc = x_sb[:, dci * T + hv2 * 1024:dci * T + hv2 * 1024 + 1024]
                    pv[t8 % 2] = sch.add("vector",
                                         lambda e, xc=xc, ps=ps: e.tensor_tensor(
                                             xc, xc, ps, ALU.add),
                                         waits=[tokm], inc="v")
                back["wo_pv"] = [pv[0], pv[1]]

            back["seq"] = gen_seq()
            back["ygate"] = do_ygate
            back["wo"] = do_wo
            return back


        sch.emit("p2_gmach")
        prev = None
        for ci in range(8):
            prev = emit_pair(ci, tok_gmach, prev)
        for _ in range(40):
            next(prev["seq"], None)
        prev["ygate"]([])
        prev["wo"]()
        sch.emit("p2_pairs")
        return None

    if 1 in parts:
        tok = emit_ffn(0.5, "ffn1")
        sch.emit("p1_ffn1")
    if 2 in parts:
        emit_attn()
    if 3 in parts:
        tok = emit_ffn(0.5, "ffn2")
        sch.emit("p3_ffn2")

    def store(eng):
        last = eng.dma_start(
            out.rearrange("(c p) t -> p c t", p=128),
            x_sb[:, 0:8 * T].rearrange("p (c t) -> p c t", c=8))
        return last.then_inc(sch.sems["st"], 16)
    sch.count["st"] += 16
    sch.ops["sync"].append((store, (), None, 0))
    sch.ops["sync"].append((
        lambda e: e.wait_ge(sch.sems["st"], sch.count["st"]) or None,
        (), None, 0))

    for k, dap in dbg_out.items():
        srct = {"x_sb": x_sb, "iv": iv_bf, "st_a": st_a, "st_b": st_b,
                "st_c": st_c, "st_d": st_d, "st_f": st_f,
                "miv": miv_bf, "fa": f_a, "fb": f_b, "gfm": gfm_sb,
                "wbuf0": wbuf[0], "wbuf1": wbuf[1]}[k]
        sh = dbg[k]
        def mkst(dap=dap, srct=srct, sh=sh):
            def fn(e):
                return e.dma_start(dap, srct[0:sh[0], 0:sh[1]]
                                   ).then_inc(sch.sems["st"], 16)
            return fn
        sch.count["st"] += 16
        sch.ops["gpsimd"].append((mkst(), (), None, 0))
    total_st = sch.count["st"]
    sch.ops["gpsimd"].append((
        lambda e: e.wait_ge(sch.sems["st"], total_st) or None,
        (), None, 0))
    sch.emit("p9_store")

    for t in reversed(ctx):
        t.__exit__(None, None, None)
    for s in list(sch.sems.values()):
        pass
    return nc


_CACHE = {}


def kernel(**inputs):
    maps = _host_layout(inputs)
    nc = build()
    res = run_bass_kernel_spmd(nc, maps, core_ids=list(range(B)))
    out = np.stack([np.ascontiguousarray(r["out"].T) for r in res.results], 0)
    return out.astype(np.float32)


# version 12 (final)
# speedup vs baseline: 1.0056x; 1.0024x over previous
import numpy as np
import concourse.bass as bass
import concourse.mybir as mybir
from concourse.bass_utils import run_bass_kernel_spmd

F32 = mybir.dt.float32
BF16 = mybir.dt.bfloat16
AF = mybir.ActivationFunctionType
ALU = mybir.AluOpType

B, T, D = 8, 2048, 1024
H, HD = 16, 64
C = 64
NCH = T // C
NPAIR = 8
EPS_LN = 1e-5
EPS_L2 = 1e-6


class Sched:

    ENGINES = ("tensor", "vector", "scalar", "gpsimd", "sync")

    def __init__(self, nc):
        self.nc = nc
        self.ops = {e: [] for e in self.ENGINES}
        self.sems = {}
        self.count = {}

    def sem(self, name):
        if name not in self.sems:
            self.sems[name] = self.nc.semaphore(name).__enter__()
            self.count[name] = 0
        return self.sems[name]

    def add(self, engine, fn, waits=(), inc=None, dinc=0):
        for w in waits:
            assert w is not None, (engine, "None wait")
            s, cnt = w
            assert cnt <= self.count[s], (
                "wait on future token", engine, s, cnt, self.count[s])
        tok = None
        if inc is not None:
            self.sem(inc)
            self.count[inc] += (dinc if dinc else 1)
            tok = (inc, self.count[inc])
        self.ops[engine].append((fn, tuple(waits), inc, dinc))
        return tok

    def now(self, sem):
        return (sem, self.count[sem])

    def emit(self, name):
        nc = self.nc
        with nc.Block(name) as blk:
            for engine in self.ENGINES:
                if not self.ops[engine]:
                    continue

                def make(engine):
                    oplist = self.ops[engine]

                    def body(eng):
                        waited = {}
                        for fn, waits, inc, dinc in oplist:
                            for (s, cnt) in waits:
                                if waited.get(s, -1) >= cnt:
                                    continue
                                eng.wait_ge(self.sems[s], cnt)
                                waited[s] = cnt
                            last = fn(eng)
                            if inc is not None and dinc == 0:
                                assert last is not None, (engine, inc)
                                last.then_inc(self.sems[inc], 1)
                    return body

                getattr(blk, engine)(make(engine))
        self.ops = {e: [] for e in self.ENGINES}


def _host_layout(inputs):
    f = np.float32
    x = np.asarray(inputs["L0_attn_x"], f)
    w_in = np.ascontiguousarray(np.asarray(inputs["L0_attn_W_in"], f))
    w_ba = np.ascontiguousarray(np.asarray(inputs["L0_attn_W_ba"], f))
    w_o = np.ascontiguousarray(np.asarray(inputs["L0_attn_W_o"], f))
    wrT = np.ascontiguousarray(np.asarray(inputs["L0_ffn_W_r"], f).T)
    wkT = np.ascontiguousarray(np.asarray(inputs["L0_ffn_W_k"], f).T)
    wvT = np.ascontiguousarray(np.asarray(inputs["L0_ffn_W_v"], f).T)
    mu_r = np.ascontiguousarray(
        np.asarray(inputs["L0_ffn_mu_r"], f).reshape(8, 128).T)
    mu_k = np.ascontiguousarray(
        np.asarray(inputs["L0_ffn_mu_k"], f).reshape(8, 128).T)
    conv = np.asarray(inputs["L0_attn_conv_w"], f)[:, 0, :]
    conv = np.ascontiguousarray(
        conv.reshape(24, 128, 4).transpose(1, 0, 2).reshape(128, 96))
    dtb = np.asarray(inputs["L0_attn_dt_bias"], f).reshape(1, 16)
    alog = np.asarray(inputs["L0_attn_A_log"], f).reshape(1, 16)
    nw = np.ascontiguousarray(
        np.tile(np.asarray(inputs["L0_attn_norm_w"], f), 2).reshape(128, 1))

    iden = np.eye(128, dtype=f)
    cum = np.zeros((128, 128), f)
    for blk in range(2):
        for s in range(64):
            for t in range(s, 64):
                cum[blk * 64 + s, blk * 64 + t] = 1.0
    rcum = np.zeros((128, 128), f)
    for blk in range(2):
        for s in range(64):
            for t in range(s):
                rcum[blk * 64 + s, blk * 64 + t] = 1.0
    selA = np.zeros((128, 128), f)
    selB = np.zeros((128, 128), f)
    for s in range(64):
        for h in (0, 64):
            selA[h + s, s] = 1.0
            selB[h + s, 64 + s] = 1.0
    selj = np.zeros((16, 8 * 128), f)
    for j in range(8):
        selj[2 * j, j * 128:j * 128 + 64] = 1.0
        selj[2 * j + 1, j * 128 + 64:(j + 1) * 128] = 1.0
    seg2 = np.zeros((128, 2), f)
    seg2[0:64, 0] = 1.0
    seg2[64:128, 1] = 1.0
    sel2 = np.zeros((2, 128), f)
    sel2[0, 0:64] = 1.0
    sel2[1, 64:128] = 1.0
    m_incl = np.zeros((128, 64), f)
    m_strict = np.zeros((128, 64), f)
    for p in range(128):
        s = p % 64
        for q in range(64):
            if s <= q:
                m_incl[p, q] = 1.0
            if s < q:
                m_strict[p, q] = 1.0
    mT_strict = np.zeros((128, 64), f)
    for p in range(128):
        q = p % 64
        for s in range(64):
            if s < q:
                mT_strict[p, s] = 1.0
    mb_incl = np.where(m_incl > 0, 0.0, -1e30).astype(f)
    mbT_strict = np.where(mT_strict > 0, 0.0, 1e30).astype(f)
    bdi = np.zeros((128, 64), f)
    for p in range(128):
        bdi[p, p % 64] = 1.0
    ones_r = np.ones((1, 128), f)
    ones_c = np.ones((128, 1), f)
    ones_all = np.ones((128, 128), f)

    shared = dict(
        w_in=w_in, w_ba=w_ba, w_o=w_o, wrT=wrT, wkT=wkT, wvT=wvT,
        mu_r=mu_r, mu_k=mu_k, conv=conv, dtb=dtb, alog=alog, nw=nw,
        iden=iden, cum=cum, rcum=rcum, selA=selA, selB=selB,
        selj=selj, seg2=seg2, sel2=sel2,
        m_incl=m_incl, m_strict=m_strict, mT_strict=mT_strict, bdi=bdi,
        mb_incl=mb_incl, mbT_strict=mbT_strict,
        ones_r=ones_r, ones_c=ones_c, ones_all=ones_all,
    )
    maps = []
    for bi in range(B):
        m = dict(shared)
        m["x"] = np.ascontiguousarray(x[bi].T)
        maps.append(m)
    return maps


def build(dbg=None, parts=(1, 2, 3)):
    dbg = dbg or {}
    nc = bass.Bass("TRN2", target_bir_lowering=False)

    ap = {}
    def din(name, shape):
        ap[name] = nc.dram_tensor(name, list(shape), F32,
                                  kind="ExternalInput").ap()
    din("x", (D, T))
    din("w_in", (D, 4 * D)); din("w_ba", (D, 2 * H)); din("w_o", (D, D))
    din("wrT", (D, D)); din("wkT", (D, D)); din("wvT", (D, D))
    din("mu_r", (128, 8)); din("mu_k", (128, 8))
    din("conv", (128, 96)); din("dtb", (1, 16)); din("alog", (1, 16))
    din("nw", (128, 1))
    din("iden", (128, 128)); din("cum", (128, 128)); din("rcum", (128, 128))
    din("selA", (128, 128)); din("selB", (128, 128))
    din("selj", (16, 1024)); din("seg2", (128, 2)); din("sel2", (2, 128))
    din("m_incl", (128, 64)); din("m_strict", (128, 64))
    din("mT_strict", (128, 64)); din("bdi", (128, 64))
    din("mb_incl", (128, 64)); din("mbT_strict", (128, 64))
    din("ones_r", (1, 128)); din("ones_c", (128, 1))
    din("ones_all", (128, 128))
    out = nc.dram_tensor("out", [D, T], F32, kind="ExternalOutput").ap()
    dbg_out = {k: nc.dram_tensor("dbg_" + k, list(v), F32,
                                 kind="ExternalOutput").ap()
               for k, v in dbg.items()}

    ctx = []
    def sb(name, shape, dtype):
        t = nc.sbuf_tensor(name, list(shape), dtype)
        ctx.append(t)
        return t.__enter__()
    def psum(name, shape, dtype=F32):
        t = nc.psum_tensor(name, list(shape), dtype)
        ctx.append(t)
        return t.__enter__()

    x_sb = sb("x_sb", [128, 8 * T], F32)
    wbuf = [sb(f"wbuf{i}", [128, 8 * 1024], BF16) for i in range(2)]
    st_a = sb("st_a", [128, 8 * 1024], BF16)
    st_b = sb("st_b", [128, 8 * 1024], BF16)
    st_c = sb("st_c", [128, 8 * 1024], BF16)
    st_d = sb("st_d", [128, 8 * 1024], BF16)
    st_f = sb("st_f", [128, 2 * 1024], BF16)
    f_a = sb("f_a", [128, T], F32)
    f_b = sb("f_b", [128, T], F32)
    iv_bf = sb("iv_bf", [128, T], BF16)
    miv_bf = sb("miv_bf", [128, T], BF16)
    xh_sb = st_d
    bcol = sb("bcol", [128, 8], BF16)
    mur_sb = sb("mur_sb", [128, 8], F32)
    muk_sb = sb("muk_sb", [128, 8], F32)
    conv_sb = sb("conv_sb", [128, 96], F32)
    nw_sb = sb("nw_sb", [128, 1], F32)
    iden_bf = sb("iden_bf", [128, 128], BF16)
    iden_f = sb("iden_f", [128, 128], F32)
    cum_sb = sb("cum_sb", [128, 128], F32)
    rcum_sb = sb("rcum_sb", [128, 128], F32)
    selA_sb = sb("selA_sb", [128, 128], F32)
    selB_sb = sb("selB_sb", [128, 128], F32)
    selj_bf = sb("selj_bf", [16, 1024], BF16)
    selj_f = sb("selj_f", [16, 1024], F32)
    seg2_bf = sb("seg2_bf", [128, 2], BF16)
    sel2_bf = sb("sel2_bf", [2, 128], BF16)
    m_strict_bf = sb("m_strict_bf", [128, 64], BF16)
    mb_incl_f = sb("mb_incl_f", [128, 64], BF16)
    mbT_strict_f = sb("mbT_strict_f", [128, 64], BF16)
    bdi_bf = sb("bdi_bf", [128, 64], BF16)
    ones_r_bf = sb("ones_r_bf", [1, 128], BF16)
    ones_r_f = sb("ones_r_f", [1, 128], F32)
    ones_c_bf = sb("ones_c_bf", [128, 1], BF16)
    ones_all_bf = sb("ones_all_bf", [128, 128], BF16)
    dtb_sb = sb("dtb_sb", [1, 16], F32)
    alog_sb = sb("alog_sb", [1, 16], F32)
    cpool = sb("cpool", [128, 4], F32)
    wba_sb = sb("wba_sb", [128, 8 * 32], BF16)
    gfm_sb = sb("gfm_sb", [16, T], F32)


    PSA = psum("psa", [128, T], F32)
    PSB = psum("psb", [128, T], F32)

    sch = Sched(nc)
    for s in ("dg", "dx", "t", "v", "s", "g", "st"):
        sch.sem(s)

    def dma(engine, dst, src, sem="dg", waits=()):
        def fn(eng):
            return eng.dma_start(dst, src).then_inc(sch.sems[sem], 16)
        sch.count[sem] += 16
        sch.ops[engine].append((fn, tuple(waits), None, 0))
        return (sem, sch.count[sem])

    tok_x = dma("sync", x_sb[:, 0:8 * T].rearrange("p (c t) -> p c t", c=8),
                ap["x"].rearrange("(c p) t -> p c t", p=128), sem="dx")
    for dst, src in [
        (mur_sb[:], ap["mu_r"]), (muk_sb[:], ap["mu_k"]),
        (conv_sb[:], ap["conv"]), (nw_sb[:], ap["nw"]),
        (iden_bf[:], ap["iden"]), (iden_f[:], ap["iden"]),
        (cum_sb[:], ap["cum"]), (rcum_sb[:], ap["rcum"]),
        (selA_sb[:], ap["selA"]), (selB_sb[:], ap["selB"]),
        (selj_bf[:], ap["selj"]), (selj_f[:], ap["selj"]),
        (seg2_bf[:], ap["seg2"]), (sel2_bf[:], ap["sel2"]),
        (wba_sb[:, 0:256].rearrange("p (e c) -> p e c", e=8),
         ap["w_ba"].rearrange("(e p) c -> p e c", p=128)),
        (m_strict_bf[:], ap["m_strict"]),
        (bdi_bf[:], ap["bdi"]),
        (mb_incl_f[:], ap["mb_incl"]), (mbT_strict_f[:], ap["mbT_strict"]),
        (ones_r_bf[:], ap["ones_r"]), (ones_r_f[:], ap["ones_r"]),
        (ones_c_bf[:], ap["ones_c"]), (ones_all_bf[:], ap["ones_all"]),
        (dtb_sb[:], ap["dtb"]), (alog_sb[:], ap["alog"]),
    ]:
        tok_c = dma("gpsimd", dst, src)
    def memsets(e):
        e.memset(cpool[:, 0:1], EPS_LN)
        e.memset(cpool[:, 1:2], EPS_L2)
        e.memset(cpool[:, 2:3], 1.0)
        return e.memset(cpool[:, 3:4], float(np.log(0.125)))
    sch.add("gpsimd", memsets, inc="g")
    sch.emit("p0_load")

    wbuf_tok = {0: None, 1: None}

    def load_w(which, key, cols):
        dst = wbuf[which][:, 0:8 * cols].rearrange("p (e c) -> p e c", e=8)
        src = ap[key].rearrange("(e p) c -> p e c", p=128)
        waits = [wbuf_tok[which]] if wbuf_tok[which] is not None else []
        return dma("gpsimd", dst, src, waits=waits)

    def wview(which, cols):
        return lambda e: wbuf[which][:, e * cols:(e + 1) * cols]

    def emit_ln(src_cols, xhat_to_half):
        for ci in range(8):
            xc = x_sb[:, ci * T:(ci + 1) * T]
            xb = st_f[:, 0:T]
            sq = st_c[:, 0:T]
            tv = sch.add("vector", lambda e, xc=xc, xb=xb: e.tensor_copy(xb, xc),
                         waits=[sch.now("t")] if ci else [tok_x], inc="v")
            ts = sch.add("scalar", lambda e, xb=xb, sq=sq:
                         e.activation(sq, xb, AF.Square),
                         waits=[tv], inc="s")
            def mm_stats(e, ci=ci, xb=xb, sq=sq):
                last = None
                for w in range(4):
                    cols = slice(w * 512, (w + 1) * 512)
                    last = e.matmul(PSA[:, cols], ones_all_bf[:], xb[:, cols],
                                    start=(ci == 0), stop=(ci == 7),
                                    skip_group_check=True)
                    last = e.matmul(PSB[:, cols], ones_all_bf[:],
                                    sq[:, cols], start=(ci == 0),
                                    stop=(ci == 7), skip_group_check=True)
                return last
            tt = sch.add("tensor", mm_stats, waits=[ts, tv], inc="t")
        tv = sch.add("vector", lambda e: e.tensor_scalar_mul(
            f_a[:], PSA[:, 0:T], 1.0 / D), waits=[tt], inc="v")
        tv = sch.add("vector", lambda e: e.tensor_mul(f_b[:], f_a[:], f_a[:]),
                     waits=[tv], inc="v")
        tv = sch.add("vector", lambda e: e.scalar_tensor_tensor(
            f_b[:], PSB[:, 0:T], 1.0 / D, f_b[:], ALU.mult, ALU.subtract),
            waits=[tv], inc="v")
        ts = sch.add("scalar", lambda e: e.activation(
            f_b[:], f_b[:], AF.Ln, bias=cpool[:, 0:1]), waits=[tv], inc="s")
        ts = sch.add("scalar", lambda e: e.activation(
            f_b[:], f_b[:], AF.Exp, scale=-0.5), waits=[ts], inc="s")
        tv = sch.add("vector", lambda e: e.tensor_copy(iv_bf[:], f_b[:]),
                     waits=[ts], inc="v")
        tv = sch.add("vector", lambda e: e.tensor_mul(miv_bf[:], f_a[:],
                                                      f_b[:]),
                     waits=[tv], inc="v")
        return tv

    def emit_ffn(scale, tag):
        tok_ln = emit_ln(None, None)
        for hv in range(2):
            hc = slice(hv * 1024, hv * 1024 + 1024)

            tprev = tok_ln
            for ci in range(8):
                xc = x_sb[:, ci * T + hv * 1024: ci * T + hv * 1024 + 1024]
                xo = xh_sb[:, ci * 1024:(ci + 1) * 1024]
                t1 = f_a[:, 0:1024]
                tv = sch.add("vector", lambda e, xc=xc, t1=t1:
                             e.tensor_mul(t1, xc, iv_bf[:, hc]),
                             waits=[tprev], inc="v")
                tprev = sch.add("vector", lambda e, t1=t1, xo=xo:
                                e.tensor_sub(xo, t1, miv_bf[:, hc]),
                                waits=[tv], inc="v")
            tok_xh = tprev

            dcol = f_b[:, 0:1024]
            for ci in range(8):
                xo = xh_sb[:, ci * 1024:(ci + 1) * 1024]
                xr = st_a[:, ci * 1024:(ci + 1) * 1024]
                xk = st_b[:, ci * 1024:(ci + 1) * 1024]
                mr = mur_sb[:, ci:ci + 1]
                mk = muk_sb[:, ci:ci + 1]
                bc = bcol[:, ci:ci + 1]
                tv = sch.add("vector", lambda e, xo=xo, d=dcol:
                             e.tensor_sub(d[:, 1:1024], xo[:, 1:1024],
                                          xo[:, 0:1023]),
                             waits=[tok_xh], inc="v")
                if hv == 0:
                    tv = sch.add("vector", lambda e, xo=xo, d=dcol:
                                 e.tensor_copy(d[:, 0:1], xo[:, 0:1]),
                                 waits=[tv], inc="v")
                else:
                    tv = sch.add("vector", lambda e, xo=xo, d=dcol, bc=bc:
                                 e.tensor_sub(d[:, 0:1], xo[:, 0:1], bc),
                                 waits=[tv], inc="v")
                tv = sch.add("vector", lambda e, d=dcol, xo=xo, xr=xr, mr=mr:
                             e.scalar_tensor_tensor(
                                 xr[:, 1:1024], d[:, 1:1024], mr,
                                 xo[:, 0:1023], ALU.mult, ALU.add),
                             waits=[tv], inc="v")
                if hv == 0:
                    tv = sch.add("vector", lambda e, d=dcol, xr=xr, mr=mr:
                                 e.tensor_scalar_mul(xr[:, 0:1], d[:, 0:1], mr),
                                 waits=[tv], inc="v")
                else:
                    tv = sch.add("vector", lambda e, d=dcol, xr=xr, mr=mr, bc=bc:
                                 e.scalar_tensor_tensor(
                                     xr[:, 0:1], d[:, 0:1], mr, bc,
                                     ALU.mult, ALU.add),
                                 waits=[tv], inc="v")
                tv = sch.add("vector", lambda e, d=dcol, xo=xo, xk=xk, mk=mk:
                             e.scalar_tensor_tensor(
                                 xk[:, 1:1024], d[:, 1:1024], mk,
                                 xo[:, 0:1023], ALU.mult, ALU.add),
                             waits=[tv], inc="v")
                if hv == 0:
                    tv = sch.add("vector", lambda e, d=dcol, xk=xk, mk=mk:
                                 e.tensor_scalar_mul(xk[:, 0:1], d[:, 0:1], mk),
                                 waits=[tv], inc="v")
                else:
                    tv = sch.add("vector", lambda e, d=dcol, xk=xk, mk=mk, bc=bc:
                                 e.scalar_tensor_tensor(
                                     xk[:, 0:1], d[:, 0:1], mk, bc,
                                     ALU.mult, ALU.add),
                                 waits=[tv], inc="v")
                if hv == 0:
                    tv = sch.add("vector", lambda e, xo=xo, bc=bc:
                                 e.tensor_copy(bc, xo[:, 1023:1024]),
                                 waits=[tv], inc="v")
            tok_mix = tv

            tok_w = load_w(0, "wkT", 1024)
            wk = wview(0, 1024)
            tok_g = _gemm(sch, PSA, PSB, wk, st_b, 8, tok_w, tok_mix,
                          consumer=_mk_relu2(sch, st_c))
            wbuf_tok[0] = tok_g
            tok_w = load_w(1, "wrT", 1024)
            wr = wview(1, 1024)
            tok_g2 = _gemm(sch, PSA, PSB, wr, st_a, 8, tok_w, tok_mix,
                           consumer=_mk_sigmoid(sch, st_d))
            wbuf_tok[1] = tok_g2
            tok_w = load_w(0, "wvT", 1024)
            wv = wview(0, 1024)

            def mk_gate(hv=hv, scale=scale):
                def consume(dci, ps, tok_mm):
                    g1 = f_a[:, 0:1024]
                    xc = x_sb[:, dci * T + hv * 1024: dci * T + hv * 1024 + 1024]
                    r = st_d[:, dci * 1024:(dci + 1) * 1024]
                    tv = sch.add("vector", lambda e, ps=ps, g1=g1, r=r:
                                 e.tensor_mul(g1, ps, r),
                                 waits=[tok_mm], inc="v")
                    tv = sch.add("vector", lambda e, g1=g1, xc=xc:
                                 e.scalar_tensor_tensor(xc, g1, scale, xc,
                                                        ALU.mult, ALU.add),
                                 waits=[tv], inc="v")
                    return tv
                return consume
            tok_g3 = _gemm(sch, PSA, PSB, wv, st_c, 8, tok_w, tok_g2,
                           consumer=mk_gate())
            wbuf_tok[0] = tok_g3
        return tok_g3

    def _mk_sigmoid(sch, dst):
        def consume(dci, ps, tok_mm):
            o = dst[:, dci * 1024:(dci + 1) * 1024]
            return sch.add("scalar", lambda e, ps=ps, o=o:
                           e.activation(o, ps, AF.Sigmoid),
                           waits=[tok_mm], inc="s")
        return consume

    def _mk_relu2(sch, dst):
        vtok = [None, None]

        def consume(dci, ps, tok_mm):
            o = dst[:, dci * 1024:(dci + 1) * 1024]
            t1 = f_b[:, (dci % 2) * 1024:(dci % 2) * 1024 + 1024]
            waits = [tok_mm]
            if vtok[dci % 2] is not None:
                waits.append(vtok[dci % 2])
            ts = sch.add("scalar", lambda e, ps=ps, t1=t1:
                         e.activation(t1, ps, AF.Relu),
                         waits=waits, inc="s")
            tv = sch.add("vector", lambda e, t1=t1, o=o:
                         e.tensor_mul(o, t1, t1), waits=[ts], inc="v")
            vtok[dci % 2] = tv
            return tv
        return consume

    gemm_prev = [None, None, None, None]

    def _gemm(sch, PS0, PS1, wv, rhs, nd, tok_w, tok_rhs, consumer):
        tok_c = None
        qs = (PS0[:, 0:1024], PS0[:, 1024:2048],
              PS1[:, 0:1024], PS1[:, 1024:2048])
        for dci in range(nd):
            ps = qs[dci % 4]
            waits = [tok_w, tok_rhs]
            if gemm_prev[dci % 4] is not None:
                waits.append(gemm_prev[dci % 4])

            def mm(e, dci=dci, ps=ps):
                last = None
                for ec in range(8):
                    for w in range(2):
                        cols = slice(w * 512, (w + 1) * 512)
                        last = e.matmul(
                            ps[:, cols], wv(ec)[:, dci * 128:(dci + 1) * 128],
                            rhs[:, ec * 1024 + w * 512:
                                ec * 1024 + w * 512 + 512],
                            start=(ec == 0), stop=(ec == 7))
                return last
            tok_mm = sch.add("tensor", mm, waits=waits, inc="t")
            tok_c = consumer(dci, ps, tok_mm)
            gemm_prev[dci % 4] = tok_c
        return tok_c


    def bar():
        return [sch.now("t"), sch.now("v"), sch.now("s"), sch.now("dg")]

    def emit_attn():
        k_st = st_c[:, 0:2048]
        q_st = st_c[:, 2048:4096]
        v_st = st_c[:, 4096:6144]
        z_st = st_c[:, 6144:8192]
        wo_buf = wbuf[1][:, 5120:6144]
        att_st = st_d[:, 2048:4096]
        qg_st = wbuf[0][:, 1024:3072]
        uv_st = wbuf[0][:, 3072:5120]
        wt_st = wbuf[0][:, 5120:7168]
        h_st = wbuf[1][:, 1024:3072]
        pl_st = wbuf[1][:, 3072:5120]
        conv_acc = iv_bf[:, 0:2048]
        conv_pre = st_d[:, 6144:8192]
        x1g = iv_bf[:, 0:256]
        m1g = iv_bf[:, 256:512]
        xp_g = [iv_bf[:, 512 + 256 * i:768 + 256 * i] for i in range(5)]
        mp_g = [iv_bf[:, 1792:2048], miv_bf[:, 0:256]]
        ktm_g = miv_bf[:, 256:512]
        vtm_g = miv_bf[:, 512:768]
        kh_g = miv_bf[:, 768:1024]
        zw_g = [miv_bf[:, 1024:1536], miv_bf[:, 1536:2048]]
        dmp_g = st_d[:, 4096:4352]
        dmm_g = st_d[:, 4352:4608]
        grep_c = st_d[:, 4608:4672]
        wfm_scr = st_d[:, 4672:4736]
        utm_scr = st_d[:, 4736:4800]
        sq_scr = iv_bf[:, 0:2048]
        invn_v = st_d[0:2, 6144:8192]
        bfm = st_f[0:16, 0:2048]
        ba_f = f_a[0:32, 0:2048]
        bt_f = f_b[:, 0:512]
        btm_f = f_b[:, 512:768]
        gsc_f = f_b[:, 768:1024]
        gtm_f = f_b[:, 1024:1280]
        rtm_f = f_b[:, 1280:1536]
        dtbr_f = f_b[:, 1536:1552]
        nega_f = f_b[:, 1552:1568]
        dup_sb = f_b[:, 0:768]
        dup2_sb = f_b[:, 768:1792]
        smast = f_b[:, 1792:1856]
        sbf3 = [miv_bf[:, 1024:1088], miv_bf[:, 1088:1152],
                miv_bf[:, 1152:1216]]
        utm2 = [miv_bf[:, 1216:1280], miv_bf[:, 1280:1344]]

        tok_ln = emit_ln(None, None)
        def x2h(ec):
            return (st_a if ec < 4 else st_b)[:, (ec % 4) * 2048:
                                              (ec % 4) * 2048 + 2048]
        tprev = tok_ln
        for ci in range(8):
            xc = x_sb[:, ci * T:(ci + 1) * T]
            xo = x2h(ci)
            tv = sch.add("vector", lambda e, xc=xc: e.tensor_mul(
                f_a[:, 0:T], xc, iv_bf[:, 0:T]), waits=[tprev], inc="v")
            tprev = sch.add("vector", lambda e, xo=xo: e.tensor_sub(
                xo, f_a[:, 0:T], miv_bf[:, 0:T]), waits=[tv], inc="v")
        tok_x2h = tprev

        def mm_ba(e):
            last = None
            for ec in range(8):
                for w in range(4):
                    cols = slice(w * 512, (w + 1) * 512)
                    last = e.matmul(PSA[0:32, cols],
                                    wba_sb[:, ec * 32:(ec + 1) * 32],
                                    x2h(ec)[:, cols] if False else
                                    x2h(ec)[:, cols],
                                    start=(ec == 0), stop=(ec == 7),
                                    skip_group_check=True)
            return last
        tt = sch.add("tensor", mm_ba, waits=[tok_x2h], inc="t")
        tv = sch.add("vector", lambda e: e.tensor_copy(ba_f, PSA[0:32, 0:2048]),
                     waits=[tt], inc="v")
        def mm_bt(e):
            last = None
            for tau in range(16):
                last = e.transpose(PSB[0:128, 32 * tau:32 * tau + 32],
                                   ba_f[0:32, 128 * tau:128 * tau + 128],
                                   iden_f[0:32, 0:32])
            return last
        tt = sch.add("tensor", mm_bt, waits=[tv], inc="t")
        tv = sch.add("vector", lambda e: e.tensor_copy(bt_f, PSB[:, 0:512]),
                     waits=[tt], inc="v")
        def mm_reps(e):
            e.matmul(PSB[:, 512:528], ones_r_f[:], dtb_sb[:],
                     start=True, stop=True)
            return e.matmul(PSB[:, 528:544], ones_r_f[:], alog_sb[:],
                            start=True, stop=True)
        tt = sch.add("tensor", mm_reps, waits=[tv], inc="t")
        tv = sch.add("vector", lambda e: e.tensor_copy(dtbr_f, PSB[:, 512:528]),
                     waits=[tt], inc="v")
        tv = sch.add("vector", lambda e: e.tensor_copy(nega_f, PSB[:, 528:544]),
                     waits=[tv], inc="v")
        tv = sch.add("vector", lambda e: e.tensor_scalar_max(
            nega_f, nega_f, -2.3), waits=[tv], inc="v")
        ts = sch.add("scalar", lambda e: e.activation(nega_f, nega_f, AF.Exp),
                     waits=[tv], inc="s")
        tv = sch.add("vector", lambda e: e.tensor_scalar_mul(
            nega_f, nega_f, -1.0), waits=[ts], inc="v")
        tok_na = tv
        bt3 = bass.AP(f_b, 0, [[2048, 128], [32, 16], [1, 16]])
        btm3 = bass.AP(f_b, 512, [[2048, 128], [16, 16], [1, 16]])
        at3 = bass.AP(f_b, 16, [[2048, 128], [32, 16], [1, 16]])
        gs3 = bass.AP(f_b, 768, [[2048, 128], [16, 16], [1, 16]])
        ts = sch.add("scalar", lambda e: e.activation(btm3, bt3, AF.Sigmoid),
                     waits=[sch.now("v")], inc="s")
        tok_btm = ts
        dtbr_b = bass.AP(f_b, 1536, [[2048, 128], [0, 16], [1, 16]])
        nega_b = bass.AP(f_b, 1552, [[2048, 128], [0, 16], [1, 16]])
        tv = sch.add("vector", lambda e: e.tensor_tensor(
            gs3, at3, dtbr_b, ALU.add), waits=[tok_na], inc="v")
        ts = sch.add("scalar", lambda e: e.activation(gs3, gs3, AF.Exp),
                     waits=[tv], inc="s")
        ts = sch.add("scalar", lambda e: e.activation(
            gs3, gs3, AF.Ln, bias=cpool[:, 2:3]), waits=[ts], inc="s")
        tv = sch.add("vector", lambda e: e.tensor_tensor(
            gs3, gs3, nega_b, ALU.mult), waits=[ts], inc="v")
        tok_g = tv
        def mm_cum(e):
            last = None
            for tau in range(16):
                last = e.matmul(PSB[:, 1024 + 16 * tau:1040 + 16 * tau],
                                cum_sb[:], gsc_f[:, 16 * tau:16 * tau + 16],
                                start=True, stop=True)
            for tau in range(16):
                last = e.matmul(PSB[:, 1280 + 16 * tau:1296 + 16 * tau],
                                rcum_sb[:], gsc_f[:, 16 * tau:16 * tau + 16],
                                start=True, stop=True)
            return last
        tt = sch.add("tensor", mm_cum, waits=[tok_g], inc="t")
        tv = sch.add("vector", lambda e: e.tensor_copy(gtm_f, PSB[:, 1024:1280]),
                     waits=[tt], inc="v")
        tv = sch.add("vector", lambda e: e.tensor_copy(rtm_f, PSB[:, 1280:1536]),
                     waits=[tv], inc="v")
        tok_cum = tv
        def mm_gfm(e):
            last = None
            for tau in range(16):
                last = e.transpose(PSB[0:16, 128 * tau:128 * tau + 128],
                                   gtm_f[:, 16 * tau:16 * tau + 16],
                                   iden_f[:])
            return last
        tt = sch.add("tensor", mm_gfm, waits=[tok_cum], inc="t")
        tv = sch.add("vector", lambda e: e.tensor_copy(
            gfm_sb[:], PSB[0:16, 0:2048]), waits=[tt], inc="v")
        def mm_bfm(e):
            last = None
            for tau in range(16):
                last = e.transpose(PSB[0:16, 128 * tau:128 * tau + 128],
                                   btm_f[:, 16 * tau:16 * tau + 16],
                                   iden_f[:])
            return last
        tt = sch.add("tensor", mm_bfm, waits=[tv], inc="t")
        tv = sch.add("vector", lambda e: e.tensor_copy(
            bfm[:], PSB[0:16, 0:2048]), waits=[tt], inc="v")
        tok_fm = tv
        def mm_dup(e):
            last = None
            for n in range(NCH):
                tau, half = n // 2, (n % 2) * 64
                rows = slice(half, half + 64)
                gsl = bass.AP(f_b, 1024 + 16 * tau + half * 2048,
                              [[2048, 64], [2, 8]])
                bsl = bass.AP(f_b, 512 + 16 * tau + half * 2048,
                              [[2048, 64], [2, 8]])
                rsl = bass.AP(f_b, 1280 + 16 * tau + half * 2048,
                              [[2048, 64], [2, 8]])
                gsl1 = bass.AP(f_b, 1024 + 16 * tau + half * 2048 + 1,
                               [[2048, 64], [2, 8]])
                bsl1 = bass.AP(f_b, 512 + 16 * tau + half * 2048 + 1,
                               [[2048, 64], [2, 8]])
                rsl1 = bass.AP(f_b, 1280 + 16 * tau + half * 2048 + 1,
                               [[2048, 64], [2, 8]])
                pc = 1280 + 24 * n
                sa = selA_sb[half:half + 64, :]
                sb_ = selB_sb[half:half + 64, :]
                e.matmul(PSB[:, pc:pc + 8], sa, gsl,
                         start=True, stop=False, skip_group_check=True)
                e.matmul(PSB[:, pc:pc + 8], sb_, gsl1,
                         start=False, stop=True, skip_group_check=True)
                e.matmul(PSB[:, pc + 8:pc + 16], sa, bsl,
                         start=True, stop=False, skip_group_check=True)
                e.matmul(PSB[:, pc + 8:pc + 16], sb_, bsl1,
                         start=False, stop=True, skip_group_check=True)
                e.matmul(PSB[:, pc + 16:pc + 24], sa, rsl,
                         start=True, stop=False, skip_group_check=True)
                last = e.matmul(PSB[:, pc + 16:pc + 24], sb_, rsl1,
                                start=False, stop=True, skip_group_check=True)
            return last
        tt = sch.add("tensor", mm_dup, waits=[tok_fm], inc="t")
        tv = sch.add("vector", lambda e: e.tensor_copy(
            dup_sb, PSB[:, 1280:2048]), waits=[tt], inc="v")
        tok_dup = tv
        d_G = bass.AP(f_b, 0, [[2048, 128], [24, 32], [1, 8]])
        d_b = bass.AP(f_b, 8, [[2048, 128], [24, 32], [1, 8]])
        d_R = bass.AP(f_b, 16, [[2048, 128], [24, 32], [1, 8]])
        d2_G = bass.AP(f_b, 768, [[2048, 128], [32, 32], [1, 8]])
        d2_r = bass.AP(f_b, 768 + 8, [[2048, 128], [32, 32], [1, 8]])
        d2_bG = bass.AP(f_b, 768 + 16, [[2048, 128], [32, 32], [1, 8]])
        d2_Ge = bass.AP(f_b, 768 + 24, [[2048, 128], [32, 32], [1, 8]])
        ts = sch.add("scalar", lambda e: e.activation(d2_G, d_G, AF.Exp),
                     waits=[tok_dup], inc="s")
        ts = sch.add("scalar", lambda e: e.activation(d2_r, d_R, AF.Exp),
                     waits=[ts], inc="s")
        tv = sch.add("vector", lambda e: e.tensor_tensor(
            d2_bG, d_b, d2_G, ALU.mult), waits=[ts], inc="v")
        tv = sch.add("vector", lambda e: e.tensor_tensor(
            d2_Ge, d2_G, d2_r, ALU.mult), waits=[tv], inc="v")
        tok_gmach = tv


        def load_win(which, coloff):
            dst = wbuf[which][:, 0:1024].rearrange("p (e c) -> p e c", e=8)
            s2 = ap["w_in"][:, coloff:coloff + 128].rearrange(
                "(e p) c -> p e c", p=128)
            waits = [wbuf_tok[which]] if wbuf_tok[which] is not None else []
            return dma("gpsimd", dst, s2, waits=waits)

        def mm_qkvz_ec(e, which, ec):
            last = None
            wsl = wbuf[which][:, ec * 128:(ec + 1) * 128]
            for w in range(4):
                cols = slice(w * 512, (w + 1) * 512)
                last = e.matmul(PSA[:, cols], wsl, x2h(ec)[:, cols],
                                start=(ec == 0), stop=(ec == 7),
                                skip_group_check=True)
            return last

        def emit_conv(cidx, dst, tok_mm):
            tpre = sch.add("scalar", lambda e: e.copy(
                conv_pre, PSA[:, 0:2048]), waits=[tok_mm], inc="s")
            tv = tpre
            wj = lambda j: conv_sb[:, cidx * 4 + j:cidx * 4 + j + 1]
            tv = sch.add("vector", lambda e: e.tensor_scalar_mul(
                conv_acc, conv_pre, wj(3)), waits=[tv], inc="v")
            for j, sh in ((2, 1), (1, 2), (0, 3)):
                tv = sch.add("vector", lambda e, j=j, sh=sh:
                             e.scalar_tensor_tensor(
                                 conv_acc[:, sh:2048], conv_pre[:, 0:2048 - sh],
                                 wj(j), conv_acc[:, sh:2048],
                                 ALU.mult, ALU.add), waits=[tv], inc="v")
            ts = sch.add("scalar", lambda e: e.activation(
                dst, conv_acc, AF.Silu), waits=[tv], inc="s")
            return ts, tpre

        def emit_l2norm(dst, qmode, tok_silu, extra=()):
            ts = sch.add("scalar", lambda e: e.activation(
                sq_scr, dst, AF.Square), waits=[tok_silu] + list(extra),
                inc="s")
            def mm_seg(e):
                last = None
                for w in range(4):
                    cols = slice(w * 512, (w + 1) * 512)
                    last = e.matmul(PSA[0:2, cols], seg2_bf[:],
                                    sq_scr[:, cols], start=True, stop=True,
                                    skip_group_check=True)
                return last
            tt = sch.add("tensor", mm_seg, waits=[ts], inc="t")
            ts = sch.add("scalar", lambda e: e.activation(
                invn_v, PSA[0:2, 0:2048], AF.Ln, bias=cpool[0:2, 1:2]),
                waits=[tt], inc="s")
            if qmode:
                ts = sch.add("scalar", lambda e: e.activation(
                    invn_v, invn_v, AF.Exp, scale=-0.5, bias=cpool[0:2, 3:4]),
                    waits=[ts], inc="s")
            else:
                ts = sch.add("scalar", lambda e: e.activation(
                    invn_v, invn_v, AF.Exp, scale=-0.5), waits=[ts], inc="s")
            def mm_rep2(e):
                last = None
                for w in range(4):
                    cols = slice(w * 512, (w + 1) * 512)
                    last = e.matmul(PSA[:, cols], sel2_bf[:],
                                    invn_v[:, cols], start=True, stop=True,
                                    skip_group_check=True)
                return last
            tt = sch.add("tensor", mm_rep2, waits=[ts], inc="t")
            tv = sch.add("vector", lambda e: e.tensor_mul(
                dst, dst, PSA[:, 0:2048]), waits=[tt], inc="v")
            return tv

        def emit_pair(ci, tok_gmach, prev):
            subs = [("k", D + 128 * ci, k_st, 8 + ci),
                    ("q", 128 * ci, q_st, ci),
                    ("v", 2 * D + 128 * ci, v_st, 16 + ci),
                    ("z", 3 * D + 128 * ci, z_st, None)]
            tok_cons = bar()
            pend = []
            for si, (nm, off, dst, cidx) in enumerate(subs):
                tok_w = load_win(si % 2, off)
                for ec in range(8):
                    w_ = ([tok_w] + list(tok_cons)) if ec == 0 else []
                    tok_mm = sch.add("tensor",
                                     lambda e, which=si % 2, ec=ec:
                                     mm_qkvz_ec(e, which, ec),
                                     waits=w_, inc="t")
                    if prev is not None:
                        next(prev["seq"], None)
                wbuf_tok[si % 2] = tok_mm
                if nm == "z":
                    tok_zmm = tok_mm
                else:
                    ts, tpre = emit_conv(cidx, dst, tok_mm)
                    tok_cons = [tpre]
                if nm in ("k", "q"):
                    pend.append((dst, nm == "q", ts))
            if prev is not None:
                for _ in range(3):
                    next(prev["seq"], None)
                prev["ygate"]([])
            zw_waits = [tok_zmm]
            if prev is not None:
                zw_waits.append(prev["y_done"])
            ts = sch.add("scalar", lambda e: e.activation(
                z_st, PSA[:, 0:2048], AF.Silu),
                waits=zw_waits, inc="s")
            tz = sch.add("vector", lambda e: e.tensor_scalar_mul(
                z_st, z_st, nw_sb[:]), waits=[ts], inc="v")
            tok_cons = [tz, sch.now("s")]
            last_l2 = None
            if prev is not None and prev.get("segy_tok") is not None:
                last_l2 = prev["segy_tok"]
            for p_ in pend:
                ex = list(tok_cons) + ([last_l2] if last_l2 is not None else [])
                last_l2 = emit_l2norm(*p_, extra=ex)
            tok_kqvz = bar()

            def mm_grow(e, ci=ci):
                last = None
                for w in range(4):
                    cols = slice(w * 512, (w + 1) * 512)
                    last = e.matmul(PSA[:, cols],
                                    selj_f[:, ci * 128:(ci + 1) * 128],
                                    gfm_sb[:, cols], start=True, stop=True,
                                    skip_group_check=True)
                return last
            tok_gr = sch.add("tensor", mm_grow,
                             waits=tok_kqvz + [tok_gmach], inc="t")
            qgam = st_d[:, 6144:8192]
            ts = sch.add("scalar", lambda e: e.activation(
                qgam, PSA[:, 0:2048], AF.Exp), waits=[tok_gr], inc="s")
            gdup_b = bass.AP(f_b, ci, [[2048, 128], [24, 32], [0, 64]])
            tok_arg = sch.add("vector", lambda e: e.tensor_tensor(
                f_a[:], PSA[:, 0:2048], gdup_b, ALU.subtract),
                waits=[tok_gr, ts], inc="v")
            qg_waits = [ts, tok_arg]
            if prev is not None:
                qg_waits.append(prev["st"]["o_last"])
            tok_qg = sch.add("vector", lambda e: e.tensor_mul(
                qg_st, q_st, qgam), waits=qg_waits, inc="v")

            if prev is not None:
                prev["wo"]()
            xslt = [iv_bf[:, 0:1024], iv_bf[:, 1024:2048]]
            mslt = [miv_bf[:, 0:1024], miv_bf[:, 1024:2048]]
            zw_all = st_d[:, 0:2048]
            kh_all = st_d[:, 4096:5120]
            vtm_g = st_d[:, 5120:5376]
            ktm_g = st_d[:, 5376:5632]
            dmp_g = st_d[:, 5632:5888]
            dmm_g = st_d[:, 5888:6144]
            tok_h = tok_wt = t_ld = None
            for sg in range(2):
                sgc = 16 * sg
                sgbase = 1024 * sg
                tok_kh_l = []
                tok_zw_l = []
                tok_x1_l = []
                tok_m1_l = []
                for g in range(4):
                    cks = list(range(sgc + 4 * g, sgc + 4 * g + 4))
                    gbase = sgbase + 256 * g
                    lbase = 256 * g

                    def mm_kkqt(e, cks=cks):
                        last = None
                        for i, n in enumerate(cks):
                            rhs_a = bass.AP(st_c, n * 64,
                                            [[8192, 64], [2048, 2], [1, 64]])
                            rhs_b = bass.AP(st_c, 64 * 8192 + n * 64,
                                            [[8192, 64], [2048, 2], [1, 64]])
                            pc = 128 * i
                            e.matmul(PSA[0:64, pc:pc + 128],
                                     k_st[0:64, n * 64:(n + 1) * 64], rhs_a,
                                     start=True, stop=True,
                                     tile_position=(0, 0),
                                     skip_group_check=True)
                            last = e.matmul(PSA[64:128, pc:pc + 128],
                                            k_st[64:128, n * 64:(n + 1) * 64],
                                            rhs_b, start=True, stop=True,
                                            tile_position=(64, 64),
                                            skip_group_check=True)
                        return last
                    if sg == 0 and g == 0:
                        kk_waits = [tok_qg] + (prev["wo_pv"] if prev else [])
                        if prev is not None:
                            kk_waits.append(prev["y_done"])
                    elif g > 0:
                        kk_waits = [tok_att, tok_zw_l[-1]]
                    else:
                        kk_waits = [tok_h, tok_wt, t_ld]
                    tok_kk = sch.add("tensor", mm_kkqt, waits=kk_waits,
                                     inc="t")

                    def mm_brep(e, g=g, sgbase=sgbase, gbase=gbase):
                        return e.matmul(PSA[:, 512:768],
                                        selj_bf[:, ci * 128:(ci + 1) * 128],
                                        bfm[:, gbase:gbase + 256],
                                        start=True, stop=True,
                                        skip_group_check=True)
                    tok_br = sch.add("tensor", mm_brep, waits=[tok_kk],
                                     inc="t")

                    dsl = f_a[:, gbase:gbase + 256]
                    mbi_b = bass.AP(mb_incl_f, 0, [[64, 128], [0, 4], [1, 64]])
                    mbT_b = bass.AP(mbT_strict_f, 0,
                                    [[64, 128], [0, 4], [1, 64]])
                    tv0 = sch.add("vector", lambda e, d=dsl, m=mbi_b:
                                  e.scalar_tensor_tensor(
                                      dmp_g, d, 1.0, m, ALU.mult, ALU.min),
                                  waits=[tok_arg], inc="v")
                    tok_dmp = sch.add("scalar", lambda e: e.activation(
                        dmp_g, dmp_g, AF.Exp), waits=[tv0], inc="s")
                    tv0 = sch.add("vector", lambda e, d=dsl, m=mbT_b:
                                  e.scalar_tensor_tensor(
                                      dmm_g, d, 1.0, m, ALU.mult, ALU.max),
                                  waits=[tok_arg, tok_dmp], inc="v")
                    tok_dmm = sch.add("scalar", lambda e: e.activation(
                        dmm_g, dmm_g, AF.Exp, scale=-1.0),
                        waits=[tv0], inc="s")

                    kk3 = bass.AP(PSA, 0, [[2048, 128], [128, 4], [1, 64]])
                    kq3 = bass.AP(PSA, 64, [[2048, 128], [128, 4], [1, 64]])
                    br3 = bass.AP(PSA, 512, [[2048, 128], [64, 4], [1, 64]])
                    mstr_b = bass.AP(m_strict_bf, 0,
                                     [[64, 128], [0, 4], [1, 64]])
                    bdup_b = bass.AP(f_b, 24 * (sgc + 4 * g) + 8 + ci,
                                     [[2048, 128], [24, 4], [0, 64]])
                    bgdup_b = bass.AP(f_b, 768 + 32 * (sgc + 4 * g) + 16 + ci,
                                      [[2048, 128], [32, 4], [0, 64]])
                    ratdup_b = bass.AP(f_b, 768 + 32 * (sgc + 4 * g) + 8 + ci,
                                       [[2048, 128], [32, 4], [0, 64]])
                    x1sl = xslt[0][:, lbase:lbase + 256]
                    m1sl = mslt[0][:, lbase:lbase + 256]

                    tv = sch.add("vector", lambda e, a=kk3, o=x1sl:
                                 e.tensor_tensor(o, dmp_g, a, ALU.mult),
                                 waits=[tok_kk, tok_dmp], inc="v")
                    tv = sch.add("vector", lambda e, a=br3, o=x1sl:
                                 e.tensor_tensor(o, o, a, ALU.mult),
                                 waits=[tv, tok_br], inc="v")
                    tok_x1 = sch.add("vector", lambda e, a=mstr_b, o=x1sl:
                                     e.tensor_tensor(o, o, a, ALU.mult),
                                     waits=[tv], inc="v")
                    tok_x1_l.append(tok_x1)
                    tv = sch.add("vector", lambda e, a=kk3, o=m1sl:
                                 e.tensor_tensor(o, dmm_g, a, ALU.mult),
                                 waits=[tok_kk, tok_dmm], inc="v")
                    tok_m1 = sch.add("vector", lambda e, a=bdup_b, o=m1sl:
                                     e.tensor_tensor(o, o, a, ALU.mult),
                                     waits=[tv], inc="v")
                    tok_m1_l.append(tok_m1)
                    attsl = att_st[:, gbase:gbase + 256]
                    tok_att = sch.add("vector", lambda e, a=attsl, b=kq3:
                                      e.tensor_tensor(a, dmp_g, b, ALU.mult),
                                      waits=[tok_kk, tok_dmp], inc="v")

                    def mm_trs(e, cks=cks):
                        last = None
                        for i, n in enumerate(cks):
                            for (srcst, base) in ((v_st, 1024), (k_st, 1280)):
                                e.matmul(
                                    PSA[0:64, base + 64 * i:base + 64 * i + 64],
                                    srcst[0:64, n * 64:(n + 1) * 64],
                                    iden_bf[0:64, 0:64], start=True,
                                    stop=True, tile_position=(0, 0),
                                    skip_group_check=True)
                                last = e.matmul(
                                    PSA[64:128, base + 64 * i:base + 64 * i + 64],
                                    srcst[64:128, n * 64:(n + 1) * 64],
                                    iden_bf[64:128, 64:128], start=True,
                                    stop=True, tile_position=(64, 64),
                                    skip_group_check=True)
                        return last
                    tok_tr = sch.add("tensor", mm_trs,
                                     waits=[tok_x1, tok_att], inc="t")
                    tok_vtm = sch.add("vector", lambda e: e.tensor_copy(
                        vtm_g, PSA[:, 1024:1280]), waits=[tok_tr], inc="v")
                    tok_ktm = sch.add("vector", lambda e: e.tensor_copy(
                        ktm_g, PSA[:, 1280:1536]), waits=[tok_tr], inc="v")

                    zwv = bass.AP(st_d, 128 * 4 * g,
                                  [[8192, 128], [128, 4], [1, 64]])
                    zwd = bass.AP(st_d, 128 * 4 * g + 64,
                                  [[8192, 128], [128, 4], [1, 64]])
                    vtm3 = bass.AP(st_d, 5120, [[8192, 128], [64, 4], [1, 64]])
                    ktm3 = bass.AP(st_d, 5376, [[8192, 128], [64, 4], [1, 64]])
                    tv = sch.add("vector", lambda e, a=zwv, b=vtm3, c=bdup_b:
                                 e.tensor_tensor(a, b, c, ALU.mult),
                                 waits=[tok_vtm], inc="v")
                    tok_zw = sch.add("vector",
                                     lambda e, a=zwd, b=ktm3, c=bgdup_b:
                                     e.tensor_tensor(a, b, c, ALU.mult),
                                     waits=[tv, tok_ktm], inc="v")
                    tok_zw_l.append(tok_zw)
                    khsl = kh_all[:, 64 * 4 * g:64 * 4 * g + 256]
                    tok_kh = sch.add("vector",
                                     lambda e, o=khsl, b=ktm3, c=ratdup_b:
                                     e.tensor_tensor(o, b, c, ALU.mult),
                                     waits=[tok_zw], inc="v")
                    tok_kh_l.append(tok_kh)

                def prod16(e, lh, rh, slot):
                    last = None
                    for i in range(16):
                        c = slice(64 * i, 64 * i + 64)
                        e.matmul(slot[0:64, c], lh[0:64, c], rh[0:64, c],
                                 start=True, stop=True, tile_position=(0, 0),
                                 skip_group_check=True)
                        last = e.matmul(slot[64:128, c], lh[64:128, c],
                                        rh[64:128, c], start=True, stop=True,
                                        tile_position=(64, 64),
                                        skip_group_check=True)
                    return last

                tok_bld = [tok_x1_l[-1], tok_m1_l[-1], tok_zw_l[-1]]
                ap_toks = []
                tok_zcur = tok_zw_l[-1]
                tokx = {0: tok_x1_l[-1]}
                tokm = {0: tok_m1_l[-1]}
                for w in range(2):
                    zps = PSB[:, 1024 * w:1024 * w + 1024]
                    def mmz(e, w=w, zps=zps, xs=xslt[0]):
                        last = None
                        for i in range(8):
                            ii = 8 * w + i
                            xc = slice(64 * ii, 64 * ii + 64)
                            zc = slice(128 * ii, 128 * ii + 128)
                            zc2 = slice(128 * i, 128 * i + 128)
                            e.matmul(zps[0:64, zc2], xs[0:64, xc],
                                     zw_all[0:64, zc], start=True, stop=True,
                                     tile_position=(0, 0),
                                     skip_group_check=True)
                            last = e.matmul(zps[64:128, zc2], xs[64:128, xc],
                                            zw_all[64:128, zc], start=True,
                                            stop=True,
                                            tile_position=(64, 64),
                                            skip_group_check=True)
                        return last
                    tm = sch.add("tensor", mmz,
                                 waits=tok_bld + ([] if w == 0 else
                                                  [ap_toks[-1]]), inc="t")
                    ap_toks.append(tm)
                    tok_zcur = sch.add("vector",
                                       lambda e, w=w, zps=zps:
                                       e.tensor_tensor(
                                           zw_all[:, 1024 * w:1024 * w + 1024],
                                           zw_all[:, 1024 * w:1024 * w + 1024],
                                           zps, ALU.subtract),
                                       waits=[tm, tok_zcur], inc="v")
                for lvl in range(5):
                    ssrc, sdst = lvl % 2, 1 - lvl % 2
                    pps = PSA[:, 1024:2048]
                    def mmp(e, ssrc=ssrc):
                        return prod16(e, xslt[ssrc], mslt[ssrc], pps)
                    w_p = [tokx[lvl], tokm[lvl]]
                    if lvl >= 1:
                        w_p.append(tok_evx)
                        w_p.append(tok_evm)
                    tok_pm = None
                    if lvl < 4:
                        tok_pm = sch.add("tensor", mmp, waits=w_p, inc="t")
                        tok_evm = sch.add("vector",
                                          lambda e, sdst=sdst: e.tensor_copy(
                                              mslt[sdst][:], pps[:, 0:1024]),
                                          waits=[tok_pm], inc="v")
                        tokm[lvl + 1] = tok_evm
                    pps2 = PSB[:, 0:512]
                    def mmp2(e, ssrc=ssrc):
                        return prod16(e, mslt[ssrc], xslt[ssrc],
                                      PSA[:, 0:1024])
                    w_p2 = list(w_p) + ([tok_pm] if tok_pm else [])
                    tok_px = sch.add("tensor", mmp2, waits=w_p2, inc="t")
                    tok_evx = sch.add("scalar",
                                      lambda e, sdst=sdst: e.copy(
                                          xslt[sdst][:], PSA[:, 0:1024]),
                                      waits=[tok_px], inc="s")
                    tokx[lvl + 1] = tok_evx
                    for w in range(2):
                        zps = PSB[:, 1024 * w:1024 * w + 1024]
                        def mmz(e, w=w, zps=zps, sdst=sdst):
                            last = None
                            xs = xslt[sdst]
                            for i in range(8):
                                ii = 8 * w + i
                                xc = slice(64 * ii, 64 * ii + 64)
                                zc = slice(128 * ii, 128 * ii + 128)
                                zc2 = slice(128 * i, 128 * i + 128)
                                e.matmul(zps[0:64, zc2], xs[0:64, xc],
                                         zw_all[0:64, zc], start=True,
                                         stop=False, tile_position=(0, 0),
                                         skip_group_check=True)
                                e.matmul(zps[0:64, zc2],
                                         iden_bf[0:64, 0:64],
                                         zw_all[0:64, zc], start=False,
                                         stop=True, tile_position=(0, 0),
                                         skip_group_check=True)
                                e.matmul(zps[64:128, zc2], xs[64:128, xc],
                                         zw_all[64:128, zc], start=True,
                                         stop=False, tile_position=(64, 64),
                                         skip_group_check=True)
                                last = e.matmul(zps[64:128, zc2],
                                                iden_bf[64:128, 64:128],
                                                zw_all[64:128, zc],
                                                start=False, stop=True,
                                                tile_position=(64, 64),
                                                skip_group_check=True)
                            return last
                        tm = sch.add("tensor", mmz,
                                     waits=[tokx[lvl + 1], tok_zcur], inc="t")
                        tok_zcur = sch.add("scalar",
                                           lambda e, w=w, zps=zps:
                                           e.copy(
                                               zw_all[:, 1024 * w:
                                                      1024 * w + 1024],
                                               zps),
                                           waits=[tm, tok_zcur], inc="s")
                tok_z = tok_zcur

                uv3o = bass.AP(wbuf[0], 3072 + sgbase,
                               [[8192, 128], [64, 16], [1, 64]])
                zv3 = bass.AP(st_d, 0, [[8192, 128], [128, 16], [1, 64]])
                tok_uv = sch.add("vector", lambda e, a=uv3o, b=zv3:
                                 e.tensor_copy(a, b), waits=[tok_z], inc="v")

                def mm_wf(e):
                    last = None
                    for i in range(16):
                        zd = zw_all[:, 128 * i + 64:128 * i + 128]
                        e.matmul(PSA[0:64, 64 * i:64 * i + 64], zd[0:64, :],
                                 iden_bf[0:64, 0:64], start=True, stop=True,
                                 tile_position=(0, 0), skip_group_check=True)
                        last = e.matmul(PSA[64:128, 64 * i:64 * i + 64],
                                        zd[64:128, :],
                                        iden_bf[64:128, 64:128], start=True,
                                        stop=True, tile_position=(64, 64),
                                        skip_group_check=True)
                    return last
                tok_wfm = sch.add("tensor", mm_wf, waits=[tok_z, tok_evx], inc="t")
                wfo = bass.AP(wbuf[0], 5120 + sgbase,
                              [[8192, 128], [64, 16], [1, 64]])
                tok_wt = sch.add("scalar", lambda e, a=wfo: e.copy(
                    a, PSA[:, 0:1024]), waits=[tok_wfm], inc="s")

                def mm_pl(e):
                    last = None
                    for i in range(16):
                        c = slice(64 * i, 64 * i + 64)
                        e.matmul(PSA[0:64, 1024 + 64 * i:1088 + 64 * i],
                                 kh_all[0:64, c],
                                 zw_all[0:64, 128 * i:128 * i + 64],
                                 start=True, stop=True, tile_position=(0, 0),
                                 skip_group_check=True)
                        last = e.matmul(PSA[64:128, 1024 + 64 * i:1088 + 64 * i],
                                        kh_all[64:128, c],
                                        zw_all[64:128, 128 * i:128 * i + 64],
                                        start=True, stop=True,
                                        tile_position=(64, 64),
                                        skip_group_check=True)
                    return last
                tok_plm = sch.add("tensor", mm_pl,
                                  waits=[tok_z] + tok_kh_l, inc="t")
                plo = bass.AP(wbuf[1], 3072 + sgbase,
                              [[8192, 128], [64, 16], [1, 64]])
                tok_pl = sch.add("vector", lambda e, a=plo: e.tensor_copy(
                    a, PSA[:, 1024:2048]), waits=[tok_plm], inc="v")

                def mm_h(e):
                    last = None
                    for i in range(16):
                        c = slice(64 * i, 64 * i + 64)
                        e.matmul(PSB[0:64, 64 * i:64 * i + 64],
                                 zw_all[0:64, 128 * i + 64:128 * i + 128],
                                 kh_all[0:64, c], start=True, stop=True,
                                 tile_position=(0, 0), skip_group_check=True)
                        last = e.matmul(PSB[64:128, 64 * i:64 * i + 64],
                                        zw_all[64:128, 128 * i + 64:128 * i + 128],
                                        kh_all[64:128, c], start=True,
                                        stop=True, tile_position=(64, 64),
                                        skip_group_check=True)
                    return last
                tok_hm = sch.add("tensor", mm_h,
                                 waits=[tok_z, tok_pl] + tok_kh_l, inc="t")
                tprev = tok_hm
                for i in range(16):
                    n = sgc + i
                    gecol = f_b[:, 768 + 32 * n + 24 + ci:
                                768 + 32 * n + 25 + ci]
                    tprev = sch.add("vector",
                                    lambda e, i=i, n=n, gecol=gecol:
                                    e.scalar_tensor_tensor(
                                        h_st[:, 64 * n:64 * n + 64],
                                        bdi_bf[:], gecol,
                                        PSB[:, 64 * i:64 * i + 64],
                                        ALU.mult, ALU.subtract),
                                    waits=[tprev], inc="v")
                tok_h = tprev
                t_ld = tok_evx

            stt = {}
            back = {"st": stt, "wo_pv": [], "seq": None,
                    "y_done": None, "ygate": None, "wo": None}
            ypair = st_d[:, 0:2048]

            def gen_seq(tok_h=tok_h, tok_wt=tok_wt, t_ld=t_ld, stt=stt):
                tv = sch.add("vector", lambda e: e.memset(smast[:], 0.0),
                             waits=[tok_h, tok_wt, t_ld], inc="v")
                tok_cast = sch.add("vector", lambda e: e.memset(sbf3[2][:], 0.0),
                                   waits=[tv], inc="v")
                tok_su_prev = None
                tok_ws = {}
                tok_u = {}
                tok_o = {}
                tok_yv = {}
                for n in range(NCH + 1):
                    if n >= 1:
                        yield
                    if n < NCH:
                        nc64 = slice(64 * n, 64 * n + 64)
                        sprev = sbf3[(n - 1) % 3]
                        snext = sbf3[n % 3]
                        def mm_su(e, nc64=nc64, sprev=sprev):
                            e.matmul(PSB[0:64, 512:576], h_st[0:64, nc64],
                                     sprev[0:64, :], start=True, stop=True,
                                     tile_position=(0, 0), skip_group_check=True)
                            return e.matmul(PSB[64:128, 512:576],
                                            h_st[64:128, nc64], sprev[64:128, :],
                                            start=True, stop=True,
                                            tile_position=(64, 64),
                                            skip_group_check=True)
                        w_su = [tok_cast]
                        if n >= 1:
                            w_su.append(tok_smadd)
                        tok_su = sch.add("tensor", mm_su, waits=w_su, inc="t")
                        tok_smadd = sch.add("vector", lambda e, n=n:
                                            e.tensor_tensor(
                                                smast[:], PSB[:, 512:576],
                                                wbuf[1][:, 3072 + 64 * n:
                                                        3072 + 64 * n + 64],
                                                ALU.add),
                                            waits=[tok_su], inc="v")
                        tok_cast_next = sch.add("vector",
                                                lambda e, snext=snext:
                                                e.tensor_copy(snext[:], smast[:]),
                                                waits=[tok_smadd], inc="v")
                        def mm_ws2(e, n=n, sprev=sprev):
                            wsl = wbuf[0][:, 5120 + 64 * n:5184 + 64 * n]
                            e.matmul(PSB[0:64, 1024:1088], wsl[0:64, :],
                                     sprev[0:64, :], start=True, stop=True,
                                     tile_position=(0, 0), skip_group_check=True)
                            return e.matmul(PSB[64:128, 1024:1088],
                                            wsl[64:128, :], sprev[64:128, :],
                                            start=True, stop=True,
                                            tile_position=(64, 64),
                                            skip_group_check=True)
                        w_ws = [tok_cast]
                        if n >= 1:
                            w_ws.append(tok_u[n - 1])
                        tok_ws[n] = sch.add("tensor", mm_ws2, waits=w_ws, inc="t")
                        tok_u[n] = sch.add("vector", lambda e, n=n, u=utm2[n % 2]:
                                           e.tensor_sub(
                                               u, wbuf[0][:, 3072 + 64 * n:
                                                          3072 + 64 * n + 64],
                                               PSB[:, 1024:1088]),
                                           waits=[tok_ws[n]], inc="v")
                        tok_cast = tok_cast_next
                    if n >= 1:
                        m = n - 1
                        mc64 = slice(64 * m, 64 * m + 64)
                        sm = sbf3[(m - 1) % 3]
                        def mm_o(e, mc64=mc64, sm=sm, u=utm2[(n - 1) % 2]):
                            e.matmul(PSB[0:64, 1536:1600], u[0:64, :],
                                     att_st[0:64, mc64], start=True, stop=False,
                                     tile_position=(0, 0), skip_group_check=True)
                            e.matmul(PSB[0:64, 1536:1600], sm[0:64, :],
                                     qg_st[0:64, mc64], start=False, stop=True,
                                     tile_position=(0, 0), skip_group_check=True)
                            e.matmul(PSB[64:128, 1536:1600], u[64:128, :],
                                     att_st[64:128, mc64], start=True, stop=False,
                                     tile_position=(64, 64),
                                     skip_group_check=True)
                            return e.matmul(PSB[64:128, 1536:1600], sm[64:128, :],
                                            qg_st[64:128, mc64], start=False,
                                            stop=True, tile_position=(64, 64),
                                            skip_group_check=True)
                        w_o = [tok_u[m]]
                        if m >= 1:
                            w_o.append(tok_yv[m - 1])
                        tok_o[m] = sch.add("tensor", mm_o, waits=w_o, inc="t")
                        tok_yv[m] = sch.add("scalar",
                                            lambda e, mc64=mc64: e.copy(
                                                ypair[:, mc64], PSB[:, 1536:1600]),
                                            waits=[tok_o[m]], inc="s")


                stt["o_last"] = tok_o[NCH - 1]
                stt["yv_last"] = tok_yv[NCH - 1]

            def do_ygate(extra=(), stt=stt):
                tok_sq = sch.add("scalar", lambda e: e.activation(
                    sq_scr, ypair, AF.Square),
                    waits=[stt["yv_last"], sch.now("v")] + list(extra), inc="s")
                def mm_seg_y(e):
                    last = None
                    for w in range(4):
                        cols = slice(w * 512, (w + 1) * 512)
                        last = e.matmul(PSB[0:2, cols], seg2_bf[:],
                                        sq_scr[:, cols], start=True, stop=True,
                                        skip_group_check=True)
                    return last
                tt = sch.add("tensor", mm_seg_y, waits=[tok_sq] + bar(), inc="t")
                back["segy_tok"] = tt
                ts = sch.add("scalar", lambda e: e.activation(
                    invn_v, PSB[0:2, 0:2048], AF.Ln, scale=1.0 / HD,
                    bias=cpool[0:2, 1:2]), waits=[tt], inc="s")
                ts = sch.add("scalar", lambda e: e.activation(
                    invn_v, invn_v, AF.Exp, scale=-0.5), waits=[ts], inc="s")
                def mm_rep_y(e):
                    last = None
                    for w in range(4):
                        cols = slice(w * 512, (w + 1) * 512)
                        last = e.matmul(PSB[:, cols], sel2_bf[:],
                                        invn_v[:, cols], start=True, stop=True,
                                        skip_group_check=True)
                    return last
                tt = sch.add("tensor", mm_rep_y, waits=[ts], inc="t")
                tv = sch.add("vector", lambda e: e.tensor_mul(
                    ypair, ypair, PSB[:, 0:2048]), waits=[tt], inc="v")
                tok_y = sch.add("vector", lambda e: e.tensor_mul(
                    ypair, ypair, z_st), waits=[tv], inc="v")


                back["y_done"] = tok_y
                back["tok_y"] = tok_y

            def do_wo():
                dst = wo_buf.rearrange("p (e c) -> p e c", e=1)
                s2 = ap["w_o"][128 * ci:128 * (ci + 1), :].rearrange(
                    "(e p) c -> p e c", p=128)
                tok_y = back["tok_y"]
                tok_wo = dma("gpsimd", dst, s2, waits=[tok_y])
                pv = [None, None]
                base0 = bar()
                for t8 in range(16):
                    dci, hv2 = t8 // 2, t8 % 2
                    ps = PSA[:, 1024 * (t8 % 2):1024 * (t8 % 2) + 1024]
                    def mm_wo(e, dci=dci, hv2=hv2, ps=ps):
                        last = None
                        wsl = wo_buf[:, dci * 128:dci * 128 + 128]
                        for w in range(2):
                            cols = slice(w * 512, (w + 1) * 512)
                            last = e.matmul(
                                ps[:, cols], wsl,
                                st_d[:, hv2 * 1024 + w * 512:
                                     hv2 * 1024 + w * 512 + 512],
                                start=True, stop=True,
                                skip_group_check=True)
                        return last
                    wts = [tok_wo, tok_y] + base0
                    if pv[t8 % 2] is not None:
                        wts.append(pv[t8 % 2])
                    tokm = sch.add("tensor", mm_wo, waits=wts, inc="t")
                    xc = x_sb[:, dci * T + hv2 * 1024:dci * T + hv2 * 1024 + 1024]
                    pv[t8 % 2] = sch.add("vector",
                                         lambda e, xc=xc, ps=ps: e.tensor_tensor(
                                             xc, xc, ps, ALU.add),
                                         waits=[tokm], inc="v")
                back["wo_pv"] = [pv[0], pv[1]]

            back["seq"] = gen_seq()
            back["ygate"] = do_ygate
            back["wo"] = do_wo
            return back


        sch.emit("p2_gmach")
        prev = None
        for ci in range(8):
            prev = emit_pair(ci, tok_gmach, prev)
        for _ in range(40):
            next(prev["seq"], None)
        prev["ygate"]([])
        prev["wo"]()
        sch.emit("p2_pairs")
        return None

    if 1 in parts:
        tok = emit_ffn(0.5, "ffn1")
        sch.emit("p1_ffn1")
    if 2 in parts:
        emit_attn()
    if 3 in parts:
        tok = emit_ffn(0.5, "ffn2")
        sch.emit("p3_ffn2")

    def store(eng):
        last = eng.dma_start(
            out.rearrange("(c p) t -> p c t", p=128),
            x_sb[:, 0:8 * T].rearrange("p (c t) -> p c t", c=8))
        return last.then_inc(sch.sems["st"], 16)
    sch.count["st"] += 16
    sch.ops["sync"].append((store, (), None, 0))
    sch.ops["sync"].append((
        lambda e: e.wait_ge(sch.sems["st"], sch.count["st"]) or None,
        (), None, 0))

    for k, dap in dbg_out.items():
        srct = {"x_sb": x_sb, "iv": iv_bf, "st_a": st_a, "st_b": st_b,
                "st_c": st_c, "st_d": st_d, "st_f": st_f,
                "miv": miv_bf, "fa": f_a, "fb": f_b, "gfm": gfm_sb,
                "wbuf0": wbuf[0], "wbuf1": wbuf[1]}[k]
        sh = dbg[k]
        def mkst(dap=dap, srct=srct, sh=sh):
            def fn(e):
                return e.dma_start(dap, srct[0:sh[0], 0:sh[1]]
                                   ).then_inc(sch.sems["st"], 16)
            return fn
        sch.count["st"] += 16
        sch.ops["gpsimd"].append((mkst(), (), None, 0))
    total_st = sch.count["st"]
    sch.ops["gpsimd"].append((
        lambda e: e.wait_ge(sch.sems["st"], total_st) or None,
        (), None, 0))
    sch.emit("p9_store")

    for t in reversed(ctx):
        t.__exit__(None, None, None)
    for s in list(sch.sems.values()):
        pass
    return nc


_CACHE = {}


def kernel(**inputs):
    maps = _host_layout(inputs)
    nc = build()
    res = run_bass_kernel_spmd(nc, maps, core_ids=list(range(B)))
    out = np.stack([np.ascontiguousarray(r["out"].T) for r in res.results], 0)
    return out.astype(np.float32)
